# version 08
# baseline (speedup 1.0000x reference)
import numpy as np
import ml_dtypes
from contextlib import ExitStack

import concourse.bass as bass
import concourse.mybir as mybir
from concourse.bass_utils import run_bass_kernel_spmd

F32 = mybir.dt.float32
BF16 = mybir.dt.bfloat16
AF = mybir.ActivationFunctionType
ALU = mybir.AluOpType
AX = mybir.AxisListType

D = 2048
S = 2048
EPS_LN = 1e-5
EPS_RMS = 1e-6
EPS_Q = 1e-8


def _make_constants():
    c = {}
    c["c_identb"] = np.eye(128, dtype=ml_dtypes.bfloat16)
    c["c_identf"] = np.eye(128, dtype=np.float32)
    c["c_ones_col_b"] = np.ones((128, 1), dtype=ml_dtypes.bfloat16)
    c["c_ones_col_f"] = np.ones((128, 1), dtype=np.float32)
    c["c_ones_row_f"] = np.ones((1, 128), dtype=np.float32)
    c["c_ones_row_b"] = np.ones((1, 128), dtype=ml_dtypes.bfloat16)
    c["c_scal"] = np.tile(np.array([[1e-5, 1e-6, 1e-8, 0.0]], dtype=np.float32),
                          (128, 1))
    c["c_ones128b"] = np.ones((128, 128), dtype=ml_dtypes.bfloat16)
    jj, ii = np.meshgrid(np.arange(128), np.arange(128), indexing="ij")
    c["c_T128b"] = (jj < ii).astype(ml_dtypes.bfloat16)
    c["c_miota"] = np.tile(np.arange(1024, dtype=np.float32)[None, :], (128, 1))
    c["c_mcols8"] = (np.arange(8, dtype=np.float32)[None, :] * 128
                     + np.arange(128, dtype=np.float32)[:, None])
    hd, half = 128, 64
    inv_freq = 1.0 / (10000.0 ** (np.arange(0, half, 2, dtype=np.float64) / half))
    fr = np.arange(S, dtype=np.float64)[:, None] * inv_freq[None, :]
    CE = np.ones((S, hd)); SE = np.zeros((S, hd))
    CE[:, 0:half:2] = np.cos(fr); CE[:, 1:half:2] = np.cos(fr)
    SE[:, 0:half:2] = -np.sin(fr); SE[:, 1:half:2] = np.sin(fr)
    sc = 1.0 / np.sqrt(hd)
    c["c_ceq"] = (CE * sc).astype(np.float32)
    c["c_seq"] = (SE * sc).astype(np.float32)
    c["c_cek"] = CE.astype(np.float32)
    c["c_sek"] = SE.astype(np.float32)
    return c


CONSTS = _make_constants()


class Ctr:

    def __init__(self, handle):
        self.h = handle
        self.n = 0

    def reserve(self, k=1):
        self.n += k
        return self.n


class Em:

    def __init__(self, P):
        self.P = P
        self.L = {k: [] for k in "gstva"}

    def g(self, fn): self.L["g"].append(fn)
    def s(self, fn): self.L["s"].append(fn)
    def t(self, fn): self.L["t"].append(fn)
    def v(self, fn): self.L["v"].append(fn)
    def a(self, fn): self.L["a"].append(fn)

    def _auto(self, lst, ctr, fn):
        n = ctr.reserve()

        def run(e, fn=fn, n=n, h=ctr.h):
            if n > 1:
                e.wait_ge(h, n - 1)
            fn(e).then_inc(h, 1)
        lst.append(run)
        return n

    def vo(self, fn):
        return self._auto(self.L["v"], self.P.sem("CV"), fn)

    def ao(self, fn):
        return self._auto(self.L["a"], self.P.sem("CS"), fn)

    def go(self, fn):
        return self._auto(self.L["g"], self.P.sem("CP"), fn)

    def to(self, fn):
        ctr = self.P.sem("CT")
        n = ctr.reserve()

        def run(e, fn=fn, n=n, h=ctr.h):
            fn(e).then_inc(h, 1)
        self.L["t"].append(run)
        return n


class Prog:
    def __init__(self, name):
        self.nc = bass.Bass("TRN2", target_bir_lowering=False, debug=False)
        self.name = name
        self.ctx = ExitStack()
        self.dram = {}
        self.sems = {}
        self.const_names = set()
        for s in ["CT", "CV", "CS", "CP", "ST"]:
            self.sems[s] = Ctr(self.ctx.enter_context(self.nc.semaphore(s)))

    def sem(self, k):
        if k not in self.sems:
            self.sems[k] = Ctr(self.ctx.enter_context(self.nc.semaphore(k)))
        return self.sems[k]

    def din(self, name, shape, dt):
        if name not in self.dram:
            self.dram[name] = self.nc.dram_tensor(name, list(shape), dt,
                                                  kind="ExternalInput").ap()
        return self.dram[name]

    def dout(self, name, shape, dt):
        self.dram[name] = self.nc.dram_tensor(name, list(shape), dt,
                                              kind="ExternalOutput").ap()
        return self.dram[name]

    def dint(self, name, shape, dt):
        if name not in self.dram:
            self.dram[name] = self.nc.dram_tensor(name, list(shape), dt,
                                                  kind="Internal").ap()
        return self.dram[name]

    def const(self, name):
        arr = CONSTS[name]
        dt = BF16 if arr.dtype == ml_dtypes.bfloat16 else F32
        self.const_names.add(name)
        return self.din(name, arr.shape, dt)

    def fence(self):
        return {k: c.n for k, c in self.sems.items()}

    def run_block(self, name, em):
        with self.nc.Block(name) as b:
            @b.gpsimd
            def _(e):
                for fn in em.L["g"]:
                    fn(e)

            @b.sync
            def _(e):
                for fn in em.L["s"]:
                    fn(e)

            @b.tensor
            def _(e):
                for fn in em.L["t"]:
                    fn(e)

            @b.vector
            def _(e):
                for fn in em.L["v"]:
                    fn(e)

            @b.scalar
            def _(e):
                for fn in em.L["a"]:
                    fn(e)

    def wait_fence_fns(self, f, skip=()):
        items = [(self.sems[k].h, n) for k, n in f.items() if n > 0 and k not in skip]

        def w(eng):
            for h, n in items:
                eng.wait_ge(h, n)
        return w


def W(h, n):
    return lambda e: e.wait_ge(h, n)


def load_const_tiles(P, em, names_tiles, sem_name="aCLC"):
    CL = P.sem(sem_name)
    for cname, tile in names_tiles:
        ap = P.const(cname)
        n = CL.reserve(16)
        em.s(lambda s, ap=ap, tile=tile, CL=CL: s.dma_start(out=tile[:], in_=ap)
             .then_inc(CL.h, 16))
    return CL.n




def emit_l0a(P, last=False):
    nc = P.nc
    x = P.din("x", [S, D], F32)
    wq = P.din("L0_attn_Wq", [64, D], F32)
    wk = P.din("L0_attn_Wk", [64, D], F32)
    wv = P.din("L0_attn_Wv", [D, D], F32)
    wo = P.din("L0_attn_Wo", [D, D], F32)
    X1 = (P.dout if last else P.dint)("X1_tok", [S, D], BF16)

    x0bf = P.dint("x0bf", [S, D], BF16)
    scwv = P.dint("scwv", [D, D], BF16)
    scwo = P.dint("scwo", [D, D], BF16)

    CT, CV, CS, ST = (P.sem(k) for k in ["CT", "CV", "CS", "ST"])
    CGX = P.sem("aCGX"); CGA = P.sem("aCGA"); CGB = P.sem("aCGB"); CGC = P.sem("aCGC")
    CLC = P.sem("aCLC"); CLX = P.sem("aCLX"); CLW = P.sem("aCLW")
    SL0 = P.sem("aSL0"); SL1 = P.sem("aSL1")
    f0 = P.fence()

    ctx = ExitStack()
    sb = lambda n, shp, dt=F32: ctx.enter_context(nc.sbuf_tensor("l0a_" + n, list(shp), dt))
    pst_ = lambda n, shp, dt=F32: ctx.enter_context(nc.psum_tensor("l0a_" + n, list(shp), dt))

    cidb = sb("cidb", [128, 128], BF16)
    cidf = sb("cidf", [128, 128], F32)
    conesr = sb("conesr", [1, 128], F32)
    conesrb = sb("conesrb", [1, 128], BF16)
    conescb = sb("conescb", [128, 1], BF16)
    conescf = sb("conescf", [128, 1], F32)
    cscal = sb("cscal", [128, 4], F32)

    ARX = sb("ARX", [128, 16, S], BF16)
    XB = ARX
    WOT = ARX
    WqT = sb("WqT", [128, 16, 64], BF16)
    WkT = sb("WkT", [128, 16, 64], BF16)
    qfT = sb("qfT", [64, S], BF16)
    kfT = sb("kfT", [64, S], BF16)
    ARW = sb("ARW", [128, 16, 1024], BF16)

    def wqk_tile(which, c):
        base = 0 if which == 0 else 2
        f0_ = c * 128
        return ARW[0:64, base + f0_ // 1024, (f0_ % 1024):(f0_ % 1024) + 128]
    wq_full = ARW[0:64, 0:2, :]
    wk_full = ARW[0:64, 2:4, :]
    WVT = ARW
    PT = lambda jc: ARW[:, jc, 0:512]
    AVT = lambda dvc: ARW[:, dvc, 512:1024]
    zit = lambda ic_l: ARW[:, 4 * ic_l:4 * (ic_l + 1), 0:512]

    Vb = sb("Vb", [128, 16, D], BF16)
    rowz = sb("rowz", [1, 512])
    rzb = sb("rzb", [128, 512])
    xsl = sb("xsl", [128, 2, 512])
    zsq = sb("zsq", [128, 2, 512], BF16)
    sumB = sb("sumB", [128, 16])
    sqB = sb("sqB", [128, 16])
    red2 = sb("red2", [128, 2])
    c1 = sb("c1", [128, 16])
    c0 = sb("c0", [128, 16])
    scr16 = sb("scr16", [128, 16])
    x1c = sb("x1c", [128, D], BF16)

    psB = pst_("psB", [128, 2, 1024], BF16)
    psA = pst_("psA", [128, 6, 512])

    em = Em(P)
    wf = P.wait_fence_fns(f0)
    for k in "gstva":
        em.L[k].append(wf)

    n_x0 = CGX.reserve(16)
    em.g(lambda g: g.dma_start(out=x0bf, in_=x).then_inc(CGX.h, 16))
    CGA.reserve(16)
    em.g(lambda g: g.dma_start(out=wq_full, in_=wq.rearrange("r (a f) -> r a f", a=2))
         .then_inc(CGA.h, 16))
    n_wqk = CGA.reserve(16)
    em.g(lambda g: g.dma_start(out=wk_full, in_=wk.rearrange("r (a f) -> r a f", a=2))
         .then_inc(CGA.h, 16))
    n_wv = CGB.reserve(16)
    em.g(lambda g: g.dma_start(out=scwv, in_=wv).then_inc(CGB.h, 16))
    n_wo = CGC.reserve(16)
    em.g(lambda g: g.dma_start(out=scwo, in_=wo).then_inc(CGC.h, 16))

    n_const = load_const_tiles(P, em, [
        ("c_identb", cidb), ("c_identf", cidf), ("c_ones_row_f", conesr),
        ("c_ones_row_b", conesrb), ("c_ones_col_b", conescb),
        ("c_ones_col_f", conescf), ("c_scal", cscal)])

    em.s(W(CGX.h, n_x0))
    for c in range(16):
        CLX.reserve(16)
        em.s(lambda s, c=c: s.dma_start(out=XB[:, c, :],
                                        in_=x0bf[0:S, 128 * c:128 * (c + 1)],
                                        transpose=True).then_inc(CLX.h, 16))
    n_xb = CLX.n

    em.t(W(CGA.h, n_wqk))
    em.t(W(CLC.h, n_const))
    evac_t = {}
    for i in range(32):
        c = i // 2
        which = i % 2
        dst = WqT if which == 0 else WkT
        if i >= 2:
            em.t(W(CV.h, evac_t[i - 2]))
        nt = em.to(lambda t, i=i, c=c, which=which: t.transpose(
            psB[:, i % 2, 0:64], wqk_tile(which, c), cidb[0:64, 0:64]))
        em.v(W(CT.h, nt))
        evac_t[i] = em.vo(lambda v, c=c, dst=dst, i=i: v.tensor_copy(
            out=dst[:, c, :], in_=psB[:, i % 2, 0:64]))
    tp_end_t = CT.n
    n_tp_evac = CV.n

    em.t(W(CLX.h, n_xb))
    em.t(W(CV.h, n_tp_evac))
    qk_evac = {}
    for i in range(8):
        mat = WqT if i < 4 else WkT
        dstb = qfT if i < 4 else kfT
        it = i % 4
        r = i % 2
        if i >= 2:
            em.t(W(CV.h, qk_evac[i - 2]))
        for dc in range(15):
            em.t(lambda t, r=r, dc=dc, it=it, mat=mat: t.matmul(
                psA[0:64, r, :], mat[:, dc, :], XB[:, dc, 512 * it:512 * (it + 1)],
                start=(dc == 0), stop=False))
        nt = em.to(lambda t, r=r, it=it, mat=mat: t.matmul(
            psA[0:64, r, :], mat[:, 15, :], XB[:, 15, 512 * it:512 * (it + 1)],
            start=False, stop=True))
        em.v(W(CT.h, nt))
        qk_evac[i] = em.vo(lambda v, r=r, it=it, dstb=dstb: v.tensor_copy(
            out=dstb[0:64, 512 * it:512 * (it + 1)], in_=psA[0:64, r, :]))
    n_qk_v = CV.n


    v_base_t = CT.n
    for h in range(2):
        em.s(W(CGB.h, n_wv))
        em.s(W(CT.h, tp_end_t))
        if h == 1:
            em.s(W(CT.h, v_base_t + 32))
        for c in range(16):
            CLW.reserve(16)
            em.s(lambda s, h=h, c=c: s.dma_start(
                out=WVT[:, c, :], in_=scwv[1024 * h:1024 * (h + 1), 128 * c:128 * (c + 1)],
                transpose=True).then_inc(CLW.h, 16))
        n_wvt_h = CLW.n
        em.t(W(CLW.h, n_wvt_h))
        v_evac = {}
        for jc in range(16):
            for dt_ in range(2):
                i = jc * 2 + dt_
                r = i % 2
                if i >= 2:
                    em.t(W(CV.h, v_evac[i - 2]))
                elif h == 1:
                    em.t(W(CV.h, CV.n))
                else:
                    em.t(W(CV.h, n_qk_v))
                for dc in range(15):
                    em.t(lambda t, r=r, dc=dc, jc=jc, dt_=dt_: t.matmul(
                        psA[:, r, :], XB[:, dc, 128 * jc:128 * (jc + 1)],
                        WVT[:, dc, 512 * dt_:512 * (dt_ + 1)],
                        start=(dc == 0), stop=False))
                nt = em.to(lambda t, r=r, jc=jc, dt_=dt_: t.matmul(
                    psA[:, r, :], XB[:, 15, 128 * jc:128 * (jc + 1)],
                    WVT[:, 15, 512 * dt_:512 * (dt_ + 1)], start=False, stop=True))
                em.v(W(CT.h, nt))
                v_evac[i] = em.vo(lambda v, r=r, jc=jc, h=h, dt_=dt_: v.tensor_copy(
                    out=Vb[:, jc, 1024 * h + 512 * dt_:1024 * h + 512 * (dt_ + 1)],
                    in_=psA[:, r, :]))

    em.s(W(CT.h, CT.n))
    em.s(W(CGC.h, n_wo))
    for c in range(16):
        CLW.reserve(16)
        em.s(lambda s, c=c: s.dma_start(out=WOT[:, c, :],
                                        in_=scwo[0:D, 128 * c:128 * (c + 1)],
                                        transpose=True).then_inc(CLW.h, 16))
    n_wot = CLW.n

    st_base = ST.n
    em.t(W(CLW.h, n_wot))
    for it in range(4):
        em.t(W(CV.h, CV.n))
        em.t(W(CS.h, CS.n))
        em.a(W(CV.h, CV.n))
        exp_n = {}
        for jc in range(16):
            r = jc % 2
            if jc >= 2:
                em.t(W(CS.h, exp_n[jc - 2]))
            nt = em.to(lambda t, r=r, jc=jc, it=it: t.matmul(
                psA[:, r, :], kfT[0:64, 128 * jc:128 * (jc + 1)],
                qfT[0:64, 512 * it:512 * (it + 1)], start=True, stop=True))
            em.a(W(CT.h, nt))
            exp_n[jc] = em.ao(lambda a, r=r, jc=jc: a.activation(
                out=PT(jc), in_=psA[:, r, :], func=AF.Exp, bias=0.0, scale=1.0))
        em.t(W(CS.h, CS.n))
        for jc in range(15):
            em.t(lambda t, jc=jc: t.matmul(psA[0:1, 2, :], conescb[:, :], PT(jc),
                                           start=(jc == 0), stop=False))
        ntz = em.to(lambda t: t.matmul(psA[0:1, 2, :], conescb[:, :], PT(15),
                                       start=False, stop=True))
        em.v(W(CT.h, ntz))
        nv = em.vo(lambda v: v.reciprocal(out=rowz[0:1, :], in_=psA[0:1, 2, :]))
        em.t(W(CV.h, nv))
        ntb = em.to(lambda t: t.matmul(psA[:, 3, :], conesr[0:1, :], rowz[0:1, :],
                                       start=True, stop=True))
        em.v(W(CT.h, ntb))
        nvb = em.vo(lambda v: v.tensor_copy(out=rzb[:, :], in_=psA[:, 3, :]))

        av_evac = {}
        em.t(W(CV.h, nvb))
        for dvc in range(16):
            r = dvc % 2
            if dvc >= 2:
                em.t(W(CV.h, av_evac[dvc - 2]))
            for jc in range(15):
                em.t(lambda t, r=r, dvc=dvc, jc=jc: t.matmul(
                    psA[:, r, :], Vb[:, jc, 128 * dvc:128 * (dvc + 1)], PT(jc),
                    start=(jc == 0), stop=False))
            nt = em.to(lambda t, r=r, dvc=dvc: t.matmul(
                psA[:, r, :], Vb[:, 15, 128 * dvc:128 * (dvc + 1)], PT(15),
                start=False, stop=True))
            em.v(W(CT.h, nt))
            av_evac[dvc] = em.vo(lambda v, r=r, dvc=dvc: v.tensor_tensor(
                out=AVT(dvc), in0=psA[:, r, :], in1=rzb[:, :], op=ALU.mult))

        em.t(W(CV.h, CV.n))
        o_tt = {}
        for k in range(16):
            ic_l = k // 4
            ft = k % 4
            ic = it * 4 + ic_l
            r = k % 2
            SLp = SL0 if k % 2 == 0 else SL1
            if k >= 2:
                em.s(W(CV.h, o_tt[k - 2]))
            elif it > 0:
                em.s(W(CV.h, CV.n))
            nl = SLp.reserve(16)
            em.s(lambda s, k=k, ic=ic, ft=ft, SLp=SLp: s.dma_start(
                out=xsl[:, k % 2, :],
                in_=x[128 * ic:128 * (ic + 1), 512 * ft:512 * (ft + 1)])
                .then_inc(SLp.h, 16))
            if k >= 2:
                em.t(W(CV.h, o_tt[k - 2]))
            for dvc in range(15):
                em.t(lambda t, r=r, ic_l=ic_l, ft=ft, dvc=dvc: t.matmul(
                    psA[:, 2 + r, :], AVT(dvc)[:, 128 * ic_l:128 * (ic_l + 1)],
                    WOT[:, dvc, 512 * ft:512 * (ft + 1)], start=(dvc == 0), stop=False))
            nto = em.to(lambda t, r=r, ic_l=ic_l, ft=ft: t.matmul(
                psA[:, 2 + r, :], AVT(15)[:, 128 * ic_l:128 * (ic_l + 1)],
                WOT[:, 15, 512 * ft:512 * (ft + 1)], start=False, stop=True))
            em.v(W(CT.h, nto))
            em.v(W(SLp.h, nl))
            o_tt[k] = em.vo(lambda v, r=r, k=k, ic_l=ic_l, ft=ft: v.tensor_tensor(
                out=zit(ic_l)[:, ft, :], in0=psA[:, 2 + r, :],
                in1=xsl[:, k % 2, :], op=ALU.add))
            em.a(W(CV.h, o_tt[k]))
            na = em.ao(lambda a, k=k, ic_l=ic_l, ft=ft: a.activation(
                out=zsq[:, k % 2, :], in_=zit(ic_l)[:, ft, :],
                func=AF.Square, bias=0.0, scale=1.0))
            em.vo(lambda v, ic_l=ic_l, ft=ft: v.tensor_reduce(
                out=red2[:, 0:1], in_=zit(ic_l)[:, ft, :], axis=AX.X, op=ALU.add))
            em.v(W(CS.h, na))
            em.vo(lambda v, k=k: v.tensor_reduce(
                out=red2[:, 1:2], in_=zsq[:, k % 2, :], axis=AX.X, op=ALU.add))
            if ft == 0:
                em.vo(lambda v, ic=ic: v.tensor_copy(out=sumB[:, ic:ic + 1],
                                                     in_=red2[:, 0:1]))
                em.vo(lambda v, ic=ic: v.tensor_copy(out=sqB[:, ic:ic + 1],
                                                     in_=red2[:, 1:2]))
            else:
                em.vo(lambda v, ic=ic: v.tensor_tensor(
                    out=sumB[:, ic:ic + 1], in0=sumB[:, ic:ic + 1], in1=red2[:, 0:1],
                    op=ALU.add))
                em.vo(lambda v, ic=ic: v.tensor_tensor(
                    out=sqB[:, ic:ic + 1], in0=sqB[:, ic:ic + 1], in1=red2[:, 1:2],
                    op=ALU.add))

        i0 = it * 4
        sl = slice(i0, i0 + 4)
        em.vo(lambda v, sl=sl: v.tensor_scalar(out=sumB[:, sl], in0=sumB[:, sl],
                                               scalar1=1.0 / D, scalar2=None,
                                               op0=ALU.mult))
        em.vo(lambda v, sl=sl: v.tensor_scalar(out=sqB[:, sl], in0=sqB[:, sl],
                                               scalar1=1.0 / D, scalar2=None,
                                               op0=ALU.mult))
        em.vo(lambda v, sl=sl: v.tensor_tensor(out=scr16[:, sl], in0=sumB[:, sl],
                                               in1=sumB[:, sl], op=ALU.mult))
        nvs = em.vo(lambda v, sl=sl: v.tensor_tensor(out=sqB[:, sl], in0=sqB[:, sl],
                                                     in1=scr16[:, sl], op=ALU.subtract))
        em.a(W(CV.h, nvs))
        na_std = em.ao(lambda a, sl=sl: a.activation(out=sqB[:, sl], in_=sqB[:, sl],
                                                     func=AF.Sqrt, bias=cscal[:, 0:1],
                                                     scale=1.0))
        em.v(W(CS.h, na_std))
        em.vo(lambda v, sl=sl: v.reciprocal(out=c1[:, sl], in_=sqB[:, sl]))
        em.vo(lambda v, sl=sl: v.scalar_tensor_tensor(out=c0[:, sl], in0=sumB[:, sl],
                                                      scalar=-1.0, in1=c1[:, sl],
                                                      op0=ALU.mult, op1=ALU.mult))
        for ic_l in range(4):
            ic = it * 4 + ic_l
            if ic >= 1:
                em.v(W(ST.h, st_base + 16 * ic))
            nv = em.vo(lambda v, ic=ic, ic_l=ic_l: v.tensor_scalar(
                out=x1c[:, :], in0=zit(ic_l)[:, :, :], scalar1=c1[:, ic:ic + 1],
                scalar2=c0[:, ic:ic + 1], op0=ALU.mult, op1=ALU.add))
            em.s(W(CV.h, nv))
            ST.reserve(16)
            em.s(lambda s, ic=ic: s.dma_start(out=X1[128 * ic:128 * (ic + 1), :],
                                              in_=x1c[:, :]).then_inc(ST.h, 16))
    em.s(W(ST.h, ST.n))
    ctx.close()
    return em


def emit_l0f(P, last=False):
    nc = P.nc
    X1 = P.dram.get("X1_tok")
    if X1 is None:
        X1 = P.din("X1_tok", [S, D], BF16)
    wu = P.din("L0_ffn_W_up", [8192, D], F32)
    wg = P.din("L0_ffn_W_gate", [8192, D], F32)
    wd = P.din("L0_ffn_W_down", [D, 8192], F32)
    X2T = (P.dout if last else P.dint)("X2T_fm", [D, S], BF16)

    sc_tu = P.dint("sc_tu", [8192, D], BF16)
    sc_tg = P.dint("sc_tg", [8192, D], BF16)
    sc_td = P.dint("sc_td", [D, 8192], BF16)
    Hd = P.dint("l0f_H", [8192, S], BF16)

    CT, CV, CS, CP, ST = (P.sem(k) for k in ["CT", "CV", "CS", "CP", "ST"])

    phctx = ExitStack()
    sbp = lambda n, shp, dt=F32: phctx.enter_context(nc.sbuf_tensor("l0f_" + n, list(shp), dt))
    XBF = sbp("XBF", [128, 16, S], BF16)
    tauph = sbp("tauph", [128, 4])
    cscal2 = sbp("cscal2", [128, 4])
    conesr = sbp("conesr", [1, 128])
    conescf = sbp("conescf", [128, 1])
    conescb2 = sbp("conescb2", [128, 1], BF16)
    psF = phctx.enter_context(nc.psum_tensor("l0f_psF", [128, 6, 512], F32))

    f0 = P.fence()
    ctx = ExitStack()
    sb = lambda n, shp, dt=F32: ctx.enter_context(nc.sbuf_tensor("l0fq_" + n, list(shp), dt))
    wt4 = sb("wt4", [128, 4, D])
    s1 = sb("s1", [128, D])
    s2 = sb("s2", [128, D])
    qt = sb("qt", [128, 2, D], BF16)
    acc = sb("acc", [128, 1])
    rtmp = sb("rtmp", [128, 1])
    svals = sb("svals", [1, 8])
    taun = sb("taun", [128, 3])

    em = Em(P)
    wf = P.wait_fence_fns(f0)
    for k in "gstva":
        em.L[k].append(wf)
    BCLC = P.sem("bCLC")
    n_const = load_const_tiles(P, em, [("c_ones_row_f", conesr), ("c_ones_col_f", conescf),
                                       ("c_scal", cscal2), ("c_ones_col_b", conescb2)],
                               "bCLC")
    SQ = [P.sem("bSQ%d" % i) for i in range(4)]
    SQW0 = P.sem("bSQW0"); SQW1 = P.sem("bSQW1")

    def tile_src(m, ti):
        w = (wu, wg, wd)[m]
        if m < 2:
            return w[128 * ti:128 * (ti + 1), :]
        r, cb = ti // 4, ti % 4
        return w[128 * r:128 * (r + 1), 2048 * cb:2048 * (cb + 1)]

    def tile_dst(m, ti):
        d_ = (sc_tu, sc_tg, sc_td)[m]
        if m < 2:
            return d_[128 * ti:128 * (ti + 1), :]
        r, cb = ti // 4, ti % 4
        return d_[128 * r:128 * (r + 1), 2048 * cb:2048 * (cb + 1)]

    em.v(W(BCLC.h, n_const))
    em.t(W(BCLC.h, n_const))
    em.a(W(BCLC.h, n_const))
    slot_cons = {}
    stq_cons = {}
    for m in range(3):
        for ti in range(64):
            sl = ti % 4
            if sl in slot_cons:
                em.s(W(*slot_cons[sl]))
            nl = SQ[sl].reserve(16)
            em.s(lambda s, m=m, ti=ti, sl=sl: s.dma_start(
                out=wt4[:, sl, :], in_=tile_src(m, ti)).then_inc(SQ[sl].h, 16))
            em.v(W(SQ[sl].h, nl))
            nred = em.vo(lambda v, sl=sl: v.tensor_reduce(
                out=rtmp[:, 0:1], in_=wt4[:, sl, :], axis=AX.X, op=ALU.add,
                apply_absolute_value=True))
            slot_cons[sl] = (CV.h, nred)
            if ti == 0:
                em.vo(lambda v: v.tensor_copy(out=acc[:, 0:1], in_=rtmp[:, 0:1]))
            else:
                em.vo(lambda v: v.tensor_tensor(out=acc[:, 0:1], in0=acc[:, 0:1],
                                                in1=rtmp[:, 0:1], op=ALU.add))
        em.t(W(CV.h, CV.n))
        nt = em.to(lambda t: t.matmul(psF[0:1, 4, 0:1], acc[:, 0:1], conescf[:, 0:1],
                                      start=True, stop=True))
        em.v(W(CT.h, nt))
        em.vo(lambda v, m=m: v.tensor_copy(out=svals[0:1, m:m + 1], in_=psF[0:1, 4, 0:1]))
        em.vo(lambda v, m=m: v.tensor_scalar(
            out=svals[0:1, 3 + m:4 + m], in0=svals[0:1, m:m + 1],
            scalar1=1.0 / (8192.0 * D), scalar2=EPS_Q, op0=ALU.mult, op1=ALU.add))
        nv_tau = em.vo(lambda v, m=m: v.tensor_scalar(
            out=svals[0:1, 3 + m:4 + m], in0=svals[0:1, 3 + m:4 + m],
            scalar1=0.5, scalar2=None, op0=ALU.mult))
        em.t(W(CV.h, nv_tau))
        ntb = em.to(lambda t, m=m: t.matmul(psF[:, 5, 0:1], conesr[0:1, :],
                                            svals[0:1, 3 + m:4 + m],
                                            start=True, stop=True))
        em.v(W(CT.h, ntb))
        em.vo(lambda v, m=m: v.tensor_copy(out=tauph[:, m:m + 1], in_=psF[:, 5, 0:1]))
        nv_taub = em.vo(lambda v, m=m: v.tensor_scalar(
            out=taun[:, m:m + 1], in0=tauph[:, m:m + 1], scalar1=-1.0, scalar2=None,
            op0=ALU.mult))
        em.a(W(CV.h, nv_taub))
        for ti in range(64):
            sl = ti % 4
            if sl in slot_cons:
                em.s(W(*slot_cons[sl]))
            nl = SQ[sl].reserve(16)
            em.s(lambda s, m=m, ti=ti, sl=sl: s.dma_start(
                out=wt4[:, sl, :], in_=tile_src(m, ti)).then_inc(SQ[sl].h, 16))
            em.a(W(SQ[sl].h, nl))
            if ti > 0 or m > 0:
                em.a(W(CV.h, CV.n))
            em.ao(lambda a, m=m, sl=sl: a.activation(
                out=s1[:, :], in_=wt4[:, sl, :], func=AF.Sign,
                bias=taun[:, m:m + 1], scale=1.0))
            na = em.ao(lambda a, m=m, sl=sl: a.activation(
                out=s2[:, :], in_=wt4[:, sl, :], func=AF.Sign,
                bias=tauph[:, m:m + 1], scale=1.0))
            slot_cons[sl] = (CS.h, na)
            qsl = ti % 2
            SQWp = SQW0 if qsl == 0 else SQW1
            if qsl in stq_cons:
                em.v(W(*stq_cons[qsl]))
            em.v(W(CS.h, na))
            nv = em.vo(lambda v, qsl=qsl: v.tensor_tensor(
                out=qt[:, qsl, :], in0=s1[:, :], in1=s2[:, :], op=ALU.add))
            em.g(W(CV.h, nv))
            nst = SQWp.reserve(16)
            em.g(lambda g, m=m, ti=ti, qsl=qsl, SQWp=SQWp: g.dma_start(
                out=tile_dst(m, ti), in_=qt[:, qsl, :]).then_inc(SQWp.h, 16))
            stq_cons[qsl] = (SQWp.h, nst)
    n_sqw0, n_sqw1 = SQW0.n, SQW1.n
    P.run_block("l0f_quant", em)
    ctx.close()

    f1 = P.fence()
    ctx = ExitStack()
    sb = lambda n, shp, dt=F32: ctx.enter_context(nc.sbuf_tensor("l0fu_" + n, list(shp), dt))
    TU = sb("TU", [128, 16, 2, 512], BF16)
    TG = sb("TG", [128, 16, 2, 512], BF16)
    hrow = sb("hrow", [128, 2, S], BF16)
    sil = sb("sil", [128, 2, 512])

    em = Em(P)
    wf = P.wait_fence_fns(f1)
    for k in "gstva":
        em.L[k].append(wf)
    BXF = P.sem("bBXF")
    for c in range(16):
        BXF.reserve(16)
        em.s(lambda s, c=c: s.dma_start(out=XBF[:, c, :],
                                        in_=X1[0:S, 128 * c:128 * (c + 1)],
                                        transpose=True).then_inc(BXF.h, 16))
    n_xbf = BXF.n
    em.t(W(BXF.h, n_xbf))

    SU = [P.sem("bSU%d" % i) for i in range(2)]
    HW0 = P.sem("bHW0"); HW1 = P.sem("bHW1")
    half_cons = {}
    hrow_cons = {}
    sil_cons = {}
    upg_cons = {0: None, 1: None, 2: None, 3: None}
    for su in range(8):
        for half in range(2):
            if (su, half) != (0, 0):
                pass
            if half in half_cons:
                em.s(W(CT.h, half_cons[half]))
            for dc in range(16):
                SU[half].reserve(16)
                em.s(lambda s, su=su, half=half, dc=dc: s.dma_start(
                    out=TU[:, dc, half, :],
                    in_=sc_tu[1024 * su + 512 * half:1024 * su + 512 * (half + 1),
                              128 * dc:128 * (dc + 1)],
                    transpose=True).then_inc(SU[half].h, 16))
                SU[half].reserve(16)
                em.s(lambda s, su=su, half=half, dc=dc: s.dma_start(
                    out=TG[:, dc, half, :],
                    in_=sc_tg[1024 * su + 512 * half:1024 * su + 512 * (half + 1),
                              128 * dc:128 * (dc + 1)],
                    transpose=True).then_inc(SU[half].h, 16))
            n_half = SU[half].n
            em.t(W(SU[half].h, n_half))
            for ei in range(4):
                ech = su * 8 + half * 4 + ei
                eoff = half * 512 + 128 * ei
                for st in range(4):
                    k = (ech * 4 + st)
                    ru = k % 2
                    rg = 2 + k % 2
                    if upg_cons[ru] is not None:
                        em.t(W(*upg_cons[ru]))
                    for dc in range(15):
                        em.t(lambda t, ru=ru, dc=dc, half=half, ei=ei, st=st: t.matmul(
                            psF[:, ru, :], TU[:, dc, half, 128 * ei:128 * (ei + 1)],
                            XBF[:, dc, 512 * st:512 * (st + 1)],
                            start=(dc == 0), stop=False))
                    ntu = em.to(lambda t, ru=ru, half=half, ei=ei, st=st: t.matmul(
                        psF[:, ru, :], TU[:, 15, half, 128 * ei:128 * (ei + 1)],
                        XBF[:, 15, 512 * st:512 * (st + 1)], start=False, stop=True))
                    if upg_cons[rg] is not None:
                        em.t(W(*upg_cons[rg]))
                    for dc in range(15):
                        em.t(lambda t, rg=rg, dc=dc, half=half, ei=ei, st=st: t.matmul(
                            psF[:, rg, :], TG[:, dc, half, 128 * ei:128 * (ei + 1)],
                            XBF[:, dc, 512 * st:512 * (st + 1)],
                            start=(dc == 0), stop=False))
                    ntg = em.to(lambda t, rg=rg, half=half, ei=ei, st=st: t.matmul(
                        psF[:, rg, :], TG[:, 15, half, 128 * ei:128 * (ei + 1)],
                        XBF[:, 15, 512 * st:512 * (st + 1)], start=False, stop=True))
                    ssl = k % 2
                    if ssl in sil_cons:
                        em.a(W(CV.h, sil_cons[ssl]))
                    em.a(W(CT.h, ntu))
                    na = em.ao(lambda a, ru=ru, ssl=ssl: a.activation(
                        out=sil[:, ssl, :], in_=psF[:, ru, :], func=AF.Silu,
                        bias=0.0, scale=tauph[:, 0:1]))
                    upg_cons[ru] = (CS.h, na)
                    em.v(W(CS.h, na))
                    em.v(W(CT.h, ntg))
                    if st == 0 and (ech % 2) in hrow_cons:
                        em.v(W(*hrow_cons[ech % 2]))
                    nv = em.vo(lambda v, rg=rg, ssl=ssl, ech=ech, st=st:
                               v.scalar_tensor_tensor(
                        out=hrow[:, ech % 2, 512 * st:512 * (st + 1)],
                        in0=psF[:, rg, :], scalar=tauph[:, 1:2], in1=sil[:, ssl, :],
                        op0=ALU.mult, op1=ALU.mult))
                    upg_cons[rg] = (CV.h, nv)
                    sil_cons[ssl] = nv
                HWp = HW0 if ech % 2 == 0 else HW1
                em.g(W(CV.h, CV.n))
                nst = HWp.reserve(16)
                em.g(lambda g, ech=ech, HWp=HWp: g.dma_start(
                    out=Hd[128 * ech:128 * (ech + 1), :], in_=hrow[:, ech % 2, :])
                    .then_inc(HWp.h, 16))
                hrow_cons[ech % 2] = (HWp.h, nst)
            half_cons[half] = CT.n
    n_hw0, n_hw1 = HW0.n, HW1.n
    P.run_block("l0f_upgate", em)
    ctx.close()

    f2 = P.fence()
    ctx = ExitStack()
    sb = lambda n, shp, dt=F32: ctx.enter_context(nc.sbuf_tensor("l0fd_" + n, list(shp), dt))
    TD = sb("TD", [128, 64, 512], BF16)
    hsl = sb("hsl", [128, 2, 8, 512], BF16)

    em = Em(P)
    wf = P.wait_fence_fns(f2)
    for k in "gstva":
        em.L[k].append(wf)
    TDC = [P.sem(n) for n in ["aCLC", "aCLX", "aCLW", "aSL0",
                              "aSL1", "bCLC", "bSU0", "bSU1"]]
    HS0 = P.sem("bHS0"); HS1 = P.sem("bHS1")
    hsl_cons = {}
    st3_cnt = {}
    for dgrp in range(4):
        tdc_n = {}
        for g in range(8):
            if dgrp > 0:
                em.a(W(CT.h, st3_cnt[g]))
            for ei in range(8):
                ec = 8 * g + ei
                tdc_n[g] = TDC[g].reserve(16)
                em.a(lambda a, dgrp=dgrp, ec=ec, g=g: a.dma_start(
                    out=TD[:, ec, :],
                    in_=sc_td[512 * dgrp:512 * (dgrp + 1), 128 * ec:128 * (ec + 1)],
                    transpose=True).then_inc(TDC[g].h, 16))
        for st in range(4):
            ev_guard = CV.n
            em.t(W(CV.h, ev_guard))
            for ecg in range(8):
                hslot = ecg % 2
                HSp = HS0 if hslot == 0 else HS1
                if hslot in hsl_cons:
                    em.s(W(CT.h, hsl_cons[hslot]))
                nl = HSp.reserve(16)
                em.s(lambda s, st=st, ecg=ecg, hslot=hslot, HSp=HSp: s.dma_start(
                    out=hsl[:, hslot, :, :],
                    in_=Hd.rearrange("(g p) s -> p g s", p=128)
                    [:, 8 * ecg:8 * (ecg + 1), 512 * st:512 * (st + 1)])
                    .then_inc(HSp.h, 16))
                em.t(W(HSp.h, nl))
                if st == 0:
                    em.t(W(TDC[ecg].h, tdc_n[ecg]))
                for ei in range(8):
                    ec = ecg * 8 + ei
                    for dc4 in range(4):
                        last = (ec == 63)
                        if (dc4 == 3 and ei == 7) or last:
                            nlast = em.to(lambda t, dc4=dc4, ec=ec, hslot=hslot, ei=ei,
                                          last=last: t.matmul(
                                psF[:, dc4, :], TD[:, ec, 128 * dc4:128 * (dc4 + 1)],
                                hsl[:, hslot, ei, :], start=(ec == 0), stop=last))
                            if dc4 == 3:
                                hsl_cons[hslot] = nlast
                        else:
                            em.t(lambda t, dc4=dc4, ec=ec, hslot=hslot, ei=ei,
                                 last=last: t.matmul(
                                psF[:, dc4, :], TD[:, ec, 128 * dc4:128 * (dc4 + 1)],
                                hsl[:, hslot, ei, :], start=(ec == 0), stop=last))
                if st == 3:
                    st3_cnt[ecg] = CT.n
            nt_done = CT.n
            em.v(W(CT.h, nt_done))
            for dc4 in range(4):
                dc = dgrp * 4 + dc4
                em.vo(lambda v, dc4=dc4, dc=dc, st=st: v.scalar_tensor_tensor(
                    out=XBF[:, dc, 512 * st:512 * (st + 1)], in0=psF[:, dc4, :],
                    scalar=tauph[:, 2:3], in1=XBF[:, dc, 512 * st:512 * (st + 1)],
                    op0=ALU.mult, op1=ALU.add))
    P.run_block("l0f_down", em)
    ctx.close()

    f3 = P.fence()
    ctx = ExitStack()
    sb = lambda n, shp, dt=F32: ctx.enter_context(nc.sbuf_tensor("l0fl_" + n, list(shp), dt))
    zsq = sb("zsq", [128, 2, 512], BF16)
    srow = sb("srow", [1, S])
    sqrow = sb("sqrow", [1, S])
    c1b = sb("c1b", [128, S])
    c0b = sb("c0b", [128, S])
    tmpf = sb("tmpf", [128, S])
    x2c = sb("x2c", [128, D], BF16)

    em = Em(P)
    wf = P.wait_fence_fns(f3)
    for k in "gstva":
        em.L[k].append(wf)
    zsq_cons = {}
    for st in range(4):
        if st > 0:
            em.t(W(CV.h, CV.n))
        for dc in range(16):
            ssl = dc % 2
            if ssl in zsq_cons:
                em.a(W(CT.h, zsq_cons[ssl]))
            na = em.ao(lambda a, dc=dc, st=st, ssl=ssl: a.activation(
                out=zsq[:, ssl, :], in_=XBF[:, dc, 512 * st:512 * (st + 1)],
                func=AF.Square, bias=0.0, scale=1.0))
            em.t(W(CS.h, na))
            em.t(lambda t, dc=dc, st=st: t.matmul(
                psF[0:1, 4, :], conescb2[:, 0:1], XBF[:, dc, 512 * st:512 * (st + 1)],
                start=(dc == 0), stop=(dc == 15)))
            ntq = em.to(lambda t, dc=dc, ssl=ssl: t.matmul(
                psF[0:1, 5, :], conescb2[:, 0:1], zsq[:, ssl, :],
                start=(dc == 0), stop=(dc == 15)))
            zsq_cons[ssl] = ntq
        em.v(W(CT.h, CT.n))
        em.vo(lambda v, st=st: v.tensor_copy(out=srow[0:1, 512 * st:512 * (st + 1)],
                                             in_=psF[0:1, 4, :]))
        em.vo(lambda v, st=st: v.tensor_copy(out=sqrow[0:1, 512 * st:512 * (st + 1)],
                                             in_=psF[0:1, 5, :]))
    em.vo(lambda v: v.tensor_scalar(out=srow[0:1, :], in0=srow[0:1, :],
                                    scalar1=1.0 / D, scalar2=None, op0=ALU.mult))
    em.vo(lambda v: v.tensor_scalar(out=sqrow[0:1, :], in0=sqrow[0:1, :],
                                    scalar1=1.0 / D, scalar2=None, op0=ALU.mult))
    em.vo(lambda v: v.tensor_copy(out=tmpf[0:1, 0:S], in_=srow[0:1, :]))
    em.vo(lambda v: v.tensor_tensor(out=tmpf[0:1, 0:S], in0=tmpf[0:1, 0:S],
                                    in1=srow[0:1, :], op=ALU.mult))
    nvv = em.vo(lambda v: v.tensor_tensor(out=sqrow[0:1, :], in0=sqrow[0:1, :],
                                          in1=tmpf[0:1, 0:S], op=ALU.subtract))
    em.a(W(CV.h, nvv))
    na_s = em.ao(lambda a: a.activation(out=sqrow[0:1, :], in_=sqrow[0:1, :],
                                        func=AF.Sqrt, bias=cscal2[0:1, 0:1], scale=1.0))
    em.v(W(CS.h, na_s))
    em.vo(lambda v: v.reciprocal(out=sqrow[0:1, :], in_=sqrow[0:1, :]))
    nvr = em.vo(lambda v: v.scalar_tensor_tensor(out=srow[0:1, :], in0=srow[0:1, :],
                                                 scalar=-1.0, in1=sqrow[0:1, :],
                                                 op0=ALU.mult, op1=ALU.mult))
    em.t(W(CV.h, nvr))
    for st in range(4):
        nt1 = em.to(lambda t, st=st: t.matmul(psF[:, 4, :], conesr[0:1, :],
                                              sqrow[0:1, 512 * st:512 * (st + 1)],
                                              start=True, stop=True))
        nt2 = em.to(lambda t, st=st: t.matmul(psF[:, 5, :], conesr[0:1, :],
                                              srow[0:1, 512 * st:512 * (st + 1)],
                                              start=True, stop=True))
        em.v(W(CT.h, nt2))
        em.vo(lambda v, st=st: v.tensor_copy(out=c1b[:, 512 * st:512 * (st + 1)],
                                             in_=psF[:, 4, :]))
        nvc = em.vo(lambda v, st=st: v.tensor_copy(out=c0b[:, 512 * st:512 * (st + 1)],
                                                   in_=psF[:, 5, :]))
        em.t(W(CV.h, nvc))
    st_base = ST.n
    for dc in range(16):
        em.vo(lambda v, dc=dc: v.tensor_tensor(out=tmpf[:, :], in0=XBF[:, dc, :],
                                               in1=c1b[:, :], op=ALU.mult))
        if dc >= 1:
            em.v(W(ST.h, st_base + 16 * dc))
        nv = em.vo(lambda v, dc=dc: v.tensor_tensor(out=x2c[:, :], in0=tmpf[:, :],
                                                    in1=c0b[:, :], op=ALU.add))
        em.s(W(CV.h, nv))
        ST.reserve(16)
        em.s(lambda s, dc=dc: s.dma_start(out=X2T[128 * dc:128 * (dc + 1), :],
                                          in_=x2c[:, :]).then_inc(ST.h, 16))
    em.s(W(ST.h, ST.n))
    P.run_block("l0f_ln2", em)
    ctx.close()
    phctx.close()
    return None


def emit_l1a(P, last=False):
    nc = P.nc
    X2T = P.dram.get("X2T_fm")
    if X2T is None:
        X2T = P.din("X2T_fm", [D, S], BF16)
    wq = P.din("L1_attn_Wq", [D, D], F32)
    wk = P.din("L1_attn_Wk", [D, 512], F32)
    wv = P.din("L1_attn_Wv", [D, 512], F32)
    wo = P.din("L1_attn_Wo", [D, D], F32)
    X3 = (P.dout if last else P.dint)("X3_tok", [S, D], BF16)
    QR = P.dint("l1a_QR", [S, D], BF16)
    KR = P.dint("l1a_KR", [S, 512], BF16)

    CT, CV, CS, CP, ST = (P.sem(k) for k in ["CT", "CV", "CS", "CP", "ST"])

    phctx = ExitStack()
    sbp = lambda n, shp, dt=F32: phctx.enter_context(nc.sbuf_tensor("l1a_" + n, list(shp), dt))
    psF = phctx.enter_context(nc.psum_tensor("l1a_psF", [128, 8, 512], F32))
    conesr = sbp("conesr", [1, 128])
    conesrb = sbp("conesrb", [1, 128], BF16)
    conescb = sbp("conescb", [128, 1], BF16)
    cscal = sbp("cscal", [128, 4])

    abctx = ExitStack()
    VB = abctx.enter_context(nc.sbuf_tensor("l1a_VB", [128, 16, 512], BF16))

    f0 = P.fence()
    ctxA = ExitStack()
    sbA = lambda n, shp, dt=F32: ctxA.enter_context(nc.sbuf_tensor("l1aA_" + n, list(shp), dt))
    XB = sbA("XB", [128, 16, S], BF16)
    CEq = sbA("CEq", [128, 16, 128])
    SEq = sbA("SEq", [128, 16, 128])
    CEk = sbA("CEk", [128, 16, 128])
    SEk = sbA("SEk", [128, 16, 128])
    qt1 = sbA("qt1", [128, S])
    qsw = sbA("qsw", [128, S])
    qrc = sbA("qrc", [128, 2, S], BF16)

    em = Em(P)
    wf = P.wait_fence_fns(f0)
    for k in "gstva":
        em.L[k].append(wf)
    CLC = P.sem("cCLC")
    n_const = load_const_tiles(P, em, [("c_ones_row_f", conesr), ("c_ones_row_b", conesrb),
                                       ("c_ones_col_b", conescb), ("c_scal", cscal)], "cCLC")
    CXB = P.sem("cCXB")
    for c in range(16):
        CXB.reserve(16)
        em.s(lambda s, c=c: s.dma_start(out=XB[:, c, :],
                                        in_=X2T[128 * c:128 * (c + 1), :])
             .then_inc(CXB.h, 16))
    n_xb = CXB.n
    CTB = P.sem("cCTB")
    for nm, tile, cn in [("ceq", CEq, "c_ceq"), ("seq", SEq, "c_seq"),
                         ("cek", CEk, "c_cek"), ("sek", SEk, "c_sek")]:
        ap = P.const(cn)
        CTB.reserve(16)
        em.s(lambda s, ap=ap, tile=tile: s.dma_start(
            out=tile[:], in_=ap.rearrange("(c p) f -> p c f", p=128)).then_inc(CTB.h, 16))
    n_tb = CTB.n

    ctxA2 = ExitStack()
    WqB = ctxA2.enter_context(nc.sbuf_tensor("l1aA_WqB", [128, 16, S], BF16))
    CWQ = P.sem("cCWQ")
    n_wq = CWQ.reserve(16)
    em.g(lambda g: g.dma_start(out=WqB[:], in_=wq.rearrange("(c p) f -> p c f", p=128))
         .then_inc(CWQ.h, 16))
    em.t(W(CWQ.h, n_wq))
    em.t(W(CXB.h, n_xb))
    em.t(W(CLC.h, n_const))
    QW0 = P.sem("cQW0"); QW1 = P.sem("cQW1")
    qst_cons = {}
    rope_done = {}
    for ic in range(16):
        par = ic % 2
        bb = par * 4
        if par in rope_done:
            em.t(W(CV.h, rope_done[par]))
        for ft in range(4):
            for dc in range(15):
                em.t(lambda t, ic=ic, ft=ft, dc=dc, bb=bb: t.matmul(
                    psF[:, bb + ft, :], XB[:, dc, 128 * ic:128 * (ic + 1)],
                    WqB[:, dc, 512 * ft:512 * (ft + 1)], start=(dc == 0), stop=False))
            nt = em.to(lambda t, ic=ic, ft=ft, bb=bb: t.matmul(
                psF[:, bb + ft, :], XB[:, 15, 128 * ic:128 * (ic + 1)],
                WqB[:, 15, 512 * ft:512 * (ft + 1)], start=False, stop=True))
        em.v(W(CT.h, nt))
        em.v(W(CTB.h, n_tb))
        pview = psF[:, bb:bb + 4, :].rearrange("p a b -> p (a b)").rearrange(
            "p (h f) -> p h f", h=16)
        pv4 = psF[:, bb:bb + 4, :].rearrange("p a b -> p (a b)").rearrange(
            "p (h k w) -> p h k w", k=64, w=2)
        sw4 = qsw[:].rearrange("p (h k w) -> p h k w", k=64, w=2)
        em.vo(lambda v, pv4=pv4, ic=ic: v.tensor_copy(out=sw4[:, :, :, 0],
                                                      in_=pv4[:, :, :, 1]))
        em.vo(lambda v, pv4=pv4, ic=ic: v.tensor_copy(out=sw4[:, :, :, 1],
                                                      in_=pv4[:, :, :, 0]))
        nv_qt1 = em.vo(lambda v, pview=pview, ic=ic: v.tensor_tensor(
            out=qt1[:].rearrange("p (h f) -> p h f", h=16), in0=pview,
            in1=CEq[:, ic:ic + 1, :].to_broadcast([128, 16, 128]), op=ALU.mult))
        rope_done[par] = nv_qt1
        em.vo(lambda v, ic=ic: v.tensor_tensor(
            out=qsw[:].rearrange("p (h f) -> p h f", h=16),
            in0=qsw[:].rearrange("p (h f) -> p h f", h=16),
            in1=SEq[:, ic:ic + 1, :].to_broadcast([128, 16, 128]), op=ALU.mult))
        qsl = ic % 2
        if qsl in qst_cons:
            em.v(W(*qst_cons[qsl]))
        nv = em.vo(lambda v, qsl=qsl: v.tensor_tensor(
            out=qrc[:, qsl, :], in0=qt1[:], in1=qsw[:], op=ALU.add))
        em.s(W(CV.h, nv))
        QWp = QW0 if qsl == 0 else QW1
        nst = QWp.reserve(16)
        em.s(lambda s, ic=ic, qsl=qsl, QWp=QWp: s.dma_start(
            out=QR[128 * ic:128 * (ic + 1), :], in_=qrc[:, qsl, :]).then_inc(QWp.h, 16))
        qst_cons[qsl] = (QWp.h, nst)
    P.run_block("l1a_q", em)
    ctxA2.close()

    f1 = P.fence()
    ctxA3 = ExitStack()
    WkB = ctxA3.enter_context(nc.sbuf_tensor("l1aA_WkB", [128, 16, 512], BF16))
    WvB = ctxA3.enter_context(nc.sbuf_tensor("l1aA_WvB", [128, 16, 512], BF16))
    em = Em(P)
    wf = P.wait_fence_fns(f1)
    for k in "gstva":
        em.L[k].append(wf)
    CWK = P.sem("cCWK")
    CWV = P.sem("cCWV")
    n_wk = CWK.reserve(16)
    em.g(lambda g: g.dma_start(out=WkB[:], in_=wk.rearrange("(c p) f -> p c f", p=128))
         .then_inc(CWK.h, 16))
    n_wv = CWV.reserve(16)
    em.g(lambda g: g.dma_start(out=WvB[:], in_=wv.rearrange("(c p) f -> p c f", p=128))
         .then_inc(CWV.h, 16))
    em.t(W(CWK.h, n_wk))
    em.t(W(CWV.h, n_wv))
    kst_cons = {}
    kv_done = {}
    for ic in range(16):
        par = ic % 2
        bk = 4 + par * 2
        bv = 5 + par * 2
        if par in kv_done:
            em.t(W(CV.h, kv_done[par]))
        for dc in range(15):
            em.t(lambda t, ic=ic, dc=dc, bk=bk: t.matmul(
                psF[:, bk, :], XB[:, dc, 128 * ic:128 * (ic + 1)], WkB[:, dc, :],
                start=(dc == 0), stop=False))
        ntk = em.to(lambda t, ic=ic, bk=bk: t.matmul(
            psF[:, bk, :], XB[:, 15, 128 * ic:128 * (ic + 1)], WkB[:, 15, :],
            start=False, stop=True))
        for dc in range(15):
            em.t(lambda t, ic=ic, dc=dc, bv=bv: t.matmul(
                psF[:, bv, :], XB[:, dc, 128 * ic:128 * (ic + 1)], WvB[:, dc, :],
                start=(dc == 0), stop=False))
        ntv = em.to(lambda t, ic=ic, bv=bv: t.matmul(
            psF[:, bv, :], XB[:, 15, 128 * ic:128 * (ic + 1)], WvB[:, 15, :],
            start=False, stop=True))
        em.v(W(CT.h, ntk))
        pk = psF[:, bk, :].rearrange("p (h f) -> p h f", h=4)
        pk4 = psF[:, bk, :].rearrange("p (h k w) -> p h k w", k=64, w=2)
        skw = qsw[:, 0:512].rearrange("p (h k w) -> p h k w", k=64, w=2)
        em.vo(lambda v, pk4=pk4: v.tensor_copy(out=skw[:, :, :, 0], in_=pk4[:, :, :, 1]))
        em.vo(lambda v, pk4=pk4: v.tensor_copy(out=skw[:, :, :, 1], in_=pk4[:, :, :, 0]))
        nv_kt1 = em.vo(lambda v, pk=pk, ic=ic: v.tensor_tensor(
            out=qt1[:, 0:512].rearrange("p (h f) -> p h f", h=4), in0=pk,
            in1=CEk[:, ic:ic + 1, :].to_broadcast([128, 4, 128]), op=ALU.mult))
        em.vo(lambda v, ic=ic: v.tensor_tensor(
            out=qsw[:, 0:512].rearrange("p (h f) -> p h f", h=4),
            in0=qsw[:, 0:512].rearrange("p (h f) -> p h f", h=4),
            in1=SEk[:, ic:ic + 1, :].to_broadcast([128, 4, 128]), op=ALU.mult))
        ksl = ic % 2
        if ksl in kst_cons:
            em.v(W(*kst_cons[ksl]))
        nvk = em.vo(lambda v, ksl=ksl: v.tensor_tensor(
            out=qrc[:, ksl, 0:512], in0=qt1[:, 0:512], in1=qsw[:, 0:512], op=ALU.add))
        em.s(W(CV.h, nvk))
        KWp = QW0 if ksl == 0 else QW1
        nstk = KWp.reserve(16)
        em.s(lambda s, ic=ic, ksl=ksl, KWp=KWp: s.dma_start(
            out=KR[128 * ic:128 * (ic + 1), :], in_=qrc[:, ksl, 0:512])
            .then_inc(KWp.h, 16))
        kst_cons[ksl] = (KWp.h, nstk)
        em.v(W(CT.h, ntv))
        kv_done[par] = em.vo(lambda v, ic=ic, bv=bv: v.tensor_copy(
            out=VB[:, ic, :], in_=psF[:, bv, :]))
    P.run_block("l1a_kv", em)
    ctxA3.close()
    ctxA.close()

    f2 = P.fence()
    ctxB1 = ExitStack()
    sbB = lambda n, shp, dt=F32: ctxB1.enter_context(nc.sbuf_tensor("l1aB_" + n, list(shp), dt))
    KT = sbB("KT", [128, 4, S], BF16)
    ATT = sbB("ATT", [128, 16, S], BF16)
    ctxB2 = ExitStack()
    sbB2 = lambda n, shp, dt=F32: ctxB2.enter_context(nc.sbuf_tensor("l1aB2_" + n, list(shp), dt))
    qh = sbB2("qh", [128, 2, S], BF16)
    PTb = sbB2("PTb", [128, 16, 512], BF16)
    rowz = sbB2("rowz", [1, 512])
    rzb = sbB2("rzb", [128, 512])

    em = Em(P)
    wf = P.wait_fence_fns(f2)
    for k in "gstva":
        em.L[k].append(wf)
    CKT = P.sem("cCKT")
    for kh in range(4):
        CKT.reserve(16)
        em.s(lambda s, kh=kh: s.dma_start(out=KT[:, kh, :],
                                          in_=KR[0:S, 128 * kh:128 * (kh + 1)],
                                          transpose=True).then_inc(CKT.h, 16))
    n_kt = CKT.n
    em.t(W(CKT.h, n_kt))

    CQH0 = P.sem("cCQH0"); CQH1 = P.sem("cCQH1")
    qh_cons = {}
    evac_n = {}
    recip_prev = None
    for h in range(16):
        kh = h // 4
        hsl = h % 2
        QHp = CQH0 if hsl == 0 else CQH1
        if hsl in qh_cons:
            em.s(W(CT.h, qh_cons[hsl]))
        nqh = QHp.reserve(16)
        em.s(lambda s, h=h, hsl=hsl, QHp=QHp: s.dma_start(
            out=qh[:, hsl, :], in_=QR[0:S, 128 * h:128 * (h + 1)], transpose=True)
            .then_inc(QHp.h, 16))
        em.t(W(QHp.h, nqh))
        for it in range(4):
            exp_n = {}
            for jc in range(16):
                r = jc % 2
                if jc >= 2:
                    em.t(W(CS.h, exp_n[jc - 2]))
                nt = em.to(lambda t, r=r, jc=jc, it=it, hsl=hsl, kh=kh: t.matmul(
                    psF[:, r, :], KT[:, kh, 128 * jc:128 * (jc + 1)],
                    qh[:, hsl, 512 * it:512 * (it + 1)], start=True, stop=True))
                em.a(W(CT.h, nt))
                exp_n[jc] = em.ao(lambda a, r=r, jc=jc: a.activation(
                    out=PTb[:, jc, :], in_=psF[:, r, :], func=AF.Exp,
                    bias=0.0, scale=1.0))
            if recip_prev is not None:
                em.t(W(CV.h, recip_prev))
            for jc in range(15):
                em.t(W(CS.h, exp_n[jc]))
                em.t(lambda t, jc=jc: t.matmul(psF[0:1, 4, :], conescb[:, 0:1],
                                               PTb[:, jc, :], start=(jc == 0),
                                               stop=False))
            em.t(W(CS.h, exp_n[15]))
            ntz = em.to(lambda t: t.matmul(psF[0:1, 4, :], conescb[:, 0:1],
                                           PTb[:, 15, :], start=False, stop=True))
            em.v(W(CT.h, ntz))
            nv_recip = em.vo(lambda v: v.reciprocal(out=rowz[0:1, :],
                                                    in_=psF[0:1, 4, :]))
            recip_prev = nv_recip
            par = it % 2
            if par in evac_n:
                em.t(W(CV.h, evac_n[par]))
            for jc in range(15):
                em.t(lambda t, jc=jc, kh=kh, par=par: t.matmul(
                    psF[:, 2 + par, :], VB[:, jc, 128 * kh:128 * (kh + 1)],
                    PTb[:, jc, :], start=(jc == 0), stop=False))
            nta = em.to(lambda t, kh=kh, par=par: t.matmul(
                psF[:, 2 + par, :], VB[:, 15, 128 * kh:128 * (kh + 1)],
                PTb[:, 15, :], start=False, stop=True))
            em.t(W(CV.h, nv_recip))
            ntb = em.to(lambda t: t.matmul(psF[:, 5, :], conesr[0:1, :],
                                           rowz[0:1, :], start=True, stop=True))
            em.v(W(CT.h, ntb))
            em.vo(lambda v: v.tensor_copy(out=rzb[:, :], in_=psF[:, 5, :]))
            em.v(W(CT.h, nta))
            evac_n[par] = em.vo(lambda v, h=h, it=it, par=par: v.tensor_tensor(
                out=ATT[:, h, 512 * it:512 * (it + 1)], in0=psF[:, 2 + par, :],
                in1=rzb[:, :], op=ALU.mult))
        qh_cons[hsl] = CT.n
    P.run_block("l1a_attn", em)
    ctxB2.close()

    f3 = P.fence()
    ctxC = ExitStack()
    sbC = lambda n, shp, dt=F32: ctxC.enter_context(nc.sbuf_tensor("l1aC_" + n, list(shp), dt))
    WoB = sbC("WoB", [128, 16, S], BF16)
    xsl = sbC("xsl", [128, 2, 512], BF16)
    zsq = sbC("zsq", [128, 2, 512], BF16)
    sumB = sbC("sumB", [128, 16])
    sqB = sbC("sqB", [128, 16])
    red2 = sbC("red2", [128, 2])
    c1 = sbC("c1", [128, 16])
    c0 = sbC("c0", [128, 16])
    scr16 = sbC("scr16", [128, 16])
    x3c = sbC("x3c", [128, D], BF16)
    ARZ = sbC("ARZ", [128, 4, D], BF16)

    em = Em(P)
    wf = P.wait_fence_fns(f3)
    for k in "gstva":
        em.L[k].append(wf)
    CWO = P.sem("cCWO")
    n_wo = CWO.reserve(16)
    em.g(lambda g: g.dma_start(out=WoB[:], in_=wo.rearrange("(c p) f -> p c f", p=128))
         .then_inc(CWO.h, 16))
    em.t(W(CWO.h, n_wo))
    SL0 = P.sem("cSL0"); SL1 = P.sem("cSL1")
    st_base = ST.n
    o_tt = {}
    for icg in range(4):
        for k in range(16):
            ic_l = k // 4
            ft = k % 4
            ic = icg * 4 + ic_l
            r = k % 2
            SLp = SL0 if k % 2 == 0 else SL1
            if k >= 2:
                em.s(W(CV.h, o_tt[k - 2]))
            elif icg > 0:
                em.s(W(CV.h, CV.n))
            nl = SLp.reserve(16)
            em.s(lambda s, ic=ic, ft=ft, SLp=SLp, k=k: s.dma_start(
                out=xsl[:, k % 2, :],
                in_=X2T[512 * ft:512 * (ft + 1), 128 * ic:128 * (ic + 1)],
                transpose=True).then_inc(SLp.h, 16))
            if k >= 2:
                em.t(W(CV.h, o_tt[k - 2]))
            elif icg > 0:
                em.t(W(CV.h, CV.n))
            for hh in range(15):
                em.t(lambda t, r=r, ic=ic, ft=ft, hh=hh: t.matmul(
                    psF[:, 2 + r, :], ATT[:, hh, 128 * ic:128 * (ic + 1)],
                    WoB[:, hh, 512 * ft:512 * (ft + 1)], start=(hh == 0), stop=False))
            nto = em.to(lambda t, r=r, ic=ic, ft=ft: t.matmul(
                psF[:, 2 + r, :], ATT[:, 15, 128 * ic:128 * (ic + 1)],
                WoB[:, 15, 512 * ft:512 * (ft + 1)], start=False, stop=True))
            em.v(W(CT.h, nto))
            em.v(W(SLp.h, nl))
            o_tt[k] = em.vo(lambda v, r=r, ic_l=ic_l, ft=ft, k=k: v.tensor_tensor(
                out=ARZ[:, ic_l, 512 * ft:512 * (ft + 1)], in0=psF[:, 2 + r, :],
                in1=xsl[:, k % 2, :], op=ALU.add))
            em.a(W(CV.h, o_tt[k]))
            na = em.ao(lambda a, k=k, ic_l=ic_l, ft=ft: a.activation(
                out=zsq[:, k % 2, :], in_=ARZ[:, ic_l, 512 * ft:512 * (ft + 1)],
                func=AF.Square, bias=0.0, scale=1.0))
            em.vo(lambda v, ic_l=ic_l, ft=ft: v.tensor_reduce(
                out=red2[:, 0:1], in_=ARZ[:, ic_l, 512 * ft:512 * (ft + 1)],
                axis=AX.X, op=ALU.add))
            em.v(W(CS.h, na))
            em.vo(lambda v, k=k: v.tensor_reduce(
                out=red2[:, 1:2], in_=zsq[:, k % 2, :], axis=AX.X, op=ALU.add))
            if ft == 0:
                em.vo(lambda v, ic=ic: v.tensor_copy(out=sumB[:, ic:ic + 1],
                                                     in_=red2[:, 0:1]))
                em.vo(lambda v, ic=ic: v.tensor_copy(out=sqB[:, ic:ic + 1],
                                                     in_=red2[:, 1:2]))
            else:
                em.vo(lambda v, ic=ic: v.tensor_tensor(
                    out=sumB[:, ic:ic + 1], in0=sumB[:, ic:ic + 1], in1=red2[:, 0:1],
                    op=ALU.add))
                em.vo(lambda v, ic=ic: v.tensor_tensor(
                    out=sqB[:, ic:ic + 1], in0=sqB[:, ic:ic + 1], in1=red2[:, 1:2],
                    op=ALU.add))
        sl = slice(icg * 4, icg * 4 + 4)
        em.vo(lambda v, sl=sl: v.tensor_scalar(out=sumB[:, sl], in0=sumB[:, sl],
                                               scalar1=1.0 / D, scalar2=None,
                                               op0=ALU.mult))
        em.vo(lambda v, sl=sl: v.tensor_scalar(out=sqB[:, sl], in0=sqB[:, sl],
                                               scalar1=1.0 / D, scalar2=None,
                                               op0=ALU.mult))
        em.vo(lambda v, sl=sl: v.tensor_tensor(out=scr16[:, sl], in0=sumB[:, sl],
                                               in1=sumB[:, sl], op=ALU.mult))
        nvs = em.vo(lambda v, sl=sl: v.tensor_tensor(out=sqB[:, sl], in0=sqB[:, sl],
                                                     in1=scr16[:, sl],
                                                     op=ALU.subtract))
        em.a(W(CV.h, nvs))
        na_std = em.ao(lambda a, sl=sl: a.activation(out=sqB[:, sl], in_=sqB[:, sl],
                                                     func=AF.Sqrt, bias=cscal[:, 0:1],
                                                     scale=1.0))
        em.v(W(CS.h, na_std))
        em.vo(lambda v, sl=sl: v.reciprocal(out=c1[:, sl], in_=sqB[:, sl]))
        em.vo(lambda v, sl=sl: v.scalar_tensor_tensor(out=c0[:, sl], in0=sumB[:, sl],
                                                      scalar=-1.0, in1=c1[:, sl],
                                                      op0=ALU.mult, op1=ALU.mult))
        for ic_l in range(4):
            ic = icg * 4 + ic_l
            if ic >= 1:
                em.v(W(ST.h, st_base + 16 * ic))
            nv = em.vo(lambda v, ic=ic, ic_l=ic_l: v.tensor_scalar(
                out=x3c[:, :], in0=ARZ[:, ic_l, :], scalar1=c1[:, ic:ic + 1],
                scalar2=c0[:, ic:ic + 1], op0=ALU.mult, op1=ALU.add))
            em.s(W(CV.h, nv))
            ST.reserve(16)
            em.s(lambda s, ic=ic: s.dma_start(out=X3[128 * ic:128 * (ic + 1), :],
                                              in_=x3c[:, :]).then_inc(ST.h, 16))
    em.s(W(ST.h, ST.n))
    P.run_block("l1a_wo", em)
    ctxC.close()
    ctxB1.close()
    abctx.close()
    phctx.close()
    return None


L1F_PARTS = [6]


def emit_l1f(P, last=True):
    nc = P.nc
    X3 = P.dram.get("X3_tok")
    if X3 is None:
        X3 = P.din("X3_tok", [S, D], BF16)
    mask = P.din("L1_ffn_mask", [1024, 1024], F32)
    freqs = P.din("L1_ffn_freqs_cis", [1024, 64, 2], F32)
    wrouter = P.din("L1_ffn_w_router", [D], F32)
    g1 = P.din("L1_ffn_g1", [D], F32)
    g2 = P.din("L1_ffn_g2", [D], F32)
    wqI = P.din("L1_ffn_wq", [D, D], F32)
    wkI = P.din("L1_ffn_wk", [D, D], F32)
    wvI = P.din("L1_ffn_wv", [D, D], F32)
    woI = P.din("L1_ffn_wo", [D, D], F32)
    w1I = P.din("L1_ffn_w1", [D, 8192], F32)
    w3I = P.din("L1_ffn_w3", [D, 8192], F32)
    w2I = P.din("L1_ffn_w2", [8192, D], F32)
    OUT = P.dout("OUT", [S, D], F32)

    M = 1024
    scmask = P.dint("l1f_maskb", [1024, 1024], BF16)
    XG_d = P.dint("l1f_XG", [M, D], BF16)
    H1_d = P.dint("l1f_H1", [M, D], BF16)
    QI_d = P.dint("l1f_QI", [M, D], BF16)
    KI_d = P.dint("l1f_KI", [M, D], BF16)
    Z2_d = P.dint("l1f_Z2", [M, D], BF16)
    H2_d = P.dint("l1f_H2", [M, D], BF16)
    HZ_d = P.dint("l1f_HZ", [8192, M], BF16)
    Z3T_d = P.dint("l1f_Z3T", [D, M], BF16)
    ATT_d = P.dint("l1f_ATT", [D, M], BF16)

    def ATTd_rows(h):
        return ATT_d[128 * h:128 * (h + 1), :]
    row_d = P.dint("l1f_rows", [4, 16, 128], F32)

    CT, CV, CS, CP, ST = (P.sem(k) for k in ["CT", "CV", "CS", "CP", "ST"])

    phctx = ExitStack()
    sbp = lambda n, shp, dt=F32: phctx.enter_context(nc.sbuf_tensor("l1f_" + n, list(shp), dt))
    psF = phctx.enter_context(nc.psum_tensor("l1f_psF", [128, 8, 512], F32))
    conesr = sbp("conesr", [1, 128])
    conesrb = sbp("conesrb", [1, 128], BF16)
    conescb = sbp("conescb", [128, 1], BF16)
    conescf = sbp("conescf", [128, 1])
    cidb = sbp("cidb", [128, 128], BF16)
    cidf = sbp("cidf", [128, 128], F32)
    cscal = sbp("cscal", [128, 4])
    cmcols = sbp("cmcols", [128, 8])
    T128 = sbp("T128", [128, 128], BF16)
    ones128 = sbp("ones128", [128, 128], BF16)
    miota = sbp("miota", [128, 1024])
    selmat = sbp("selmat", [128, 16])
    cummat = sbp("cummat", [128, 16])
    Lmat = sbp("Lmat", [128, 16])
    Lmatb = sbp("Lmatb", [128, 16], BF16)
    rsmat = sbp("rsmat", [128, 8])
    rw = sbp("rw", [128, 8])
    t16 = sbp("t16", [16, 128])
    ssc = sbp("ssc", [128, 16])

    f0 = P.fence()
    ctx1 = ExitStack()
    sb1 = lambda n, shp, dt=F32: ctx1.enter_context(nc.sbuf_tensor("l1f1_" + n, list(shp), dt))
    XTOK = sb1("XTOK", [128, 16, D], BF16)
    bcb = sb1("bcb", [128, D])
    scr = sb1("scr", [128, D])
    wrow = sb1("wrow", [1, D])
    lrow = sb1("lrow", [1, D])
    PT = sb1("PT", [128, 16, 1024], BF16)
    xgv = sb1("xgv", [128, 2, 512], BF16)

    em = Em(P)
    wf = P.wait_fence_fns(f0)
    for k in "gstva":
        em.L[k].append(wf)
    DCLC = P.sem("dCLC")
    n_const = load_const_tiles(P, em, [
        ("c_ones_row_f", conesr), ("c_ones_row_b", conesrb), ("c_ones_col_b", conescb),
        ("c_ones_col_f", conescf), ("c_identb", cidb), ("c_identf", cidf),
        ("c_scal", cscal), ("c_mcols8", cmcols), ("c_T128b", T128),
        ("c_ones128b", ones128), ("c_miota", miota)], "dCLC")
    DXT = P.sem("dDXT")
    for c in range(16):
        DXT.reserve(16)
        em.s(lambda s, c=c: s.dma_start(out=XTOK[:, c, :],
                                        in_=X3[128 * c:128 * (c + 1), :])
             .then_inc(DXT.h, 16))
    n_xt = DXT.n
    DWR = P.sem("dDWR")
    n_wr = DWR.reserve(16)
    em.s(lambda s: s.dma_start(out=wrow[:], in_=wrouter.rearrange("(a d) -> a d", a=1))
         .then_inc(DWR.h, 16))
    DMSK = P.sem("dDMSK")
    n_mk = DMSK.reserve(16)
    em.g(lambda g: g.dma_start(out=scmask, in_=mask).then_inc(DMSK.h, 16))

    em.t(W(DWR.h, n_wr))
    em.t(W(DCLC.h, n_const))
    for ft in range(4):
        if ft > 0:
            em.t(W(CV.h, CV.n))
        nt = em.to(lambda t, ft=ft: t.matmul(psF[:, 4, :], conesr[0:1, :],
                                             wrow[0:1, 512 * ft:512 * (ft + 1)],
                                             start=True, stop=True))
        em.v(W(CT.h, nt))
        em.vo(lambda v, ft=ft: v.tensor_copy(out=bcb[:, 512 * ft:512 * (ft + 1)],
                                             in_=psF[:, 4, :]))
    em.v(W(DXT.h, n_xt))
    for ic in range(16):
        em.vo(lambda v, ic=ic: v.tensor_tensor(out=scr[:, :], in0=XTOK[:, ic, :],
                                               in1=bcb[:, :], op=ALU.mult))
        em.vo(lambda v, ic=ic: v.tensor_reduce(out=Lmat[:, ic:ic + 1], in_=scr[:, :],
                                               axis=AX.X, op=ALU.add))
    nvL = em.vo(lambda v: v.tensor_copy(out=Lmatb[:, :], in_=Lmat[:, :]))
    em.t(W(CV.h, nvL))
    ntt = em.to(lambda t: t.matmul(psF[0:16, 4, 0:128], Lmat[:, :], cidf[:, :],
                                   start=True, stop=True))
    em.v(W(CT.h, ntt))
    nvt = em.vo(lambda v: v.tensor_copy(out=t16[:, :], in_=psF[0:16, 4, 0:128]))
    em.s(W(CV.h, nvt))
    DRW = P.sem("dDRW")
    n_r1 = DRW.reserve(16)
    em.s(lambda s: s.dma_start(out=row_d[0], in_=t16[:, :]).then_inc(DRW.h, 16))
    em.s(W(DRW.h, n_r1))
    n_r2 = DRW.reserve(16)
    em.s(lambda s: s.dma_start(out=lrow[:], in_=row_d.rearrange("a b c -> a (b c)")[0:1, :])
         .then_inc(DRW.h, 16))
    em.t(W(DRW.h, n_r2))
    for ft in range(4):
        em.t(W(CV.h, CV.n))
        nt = em.to(lambda t, ft=ft: t.matmul(psF[:, 4, :], conesr[0:1, :],
                                             lrow[0:1, 512 * ft:512 * (ft + 1)],
                                             start=True, stop=True))
        em.v(W(CT.h, nt))
        em.vo(lambda v, ft=ft: v.tensor_copy(out=bcb[:, 512 * ft:512 * (ft + 1)],
                                             in_=psF[:, 4, :]))
    for ic in range(16):
        em.vo(lambda v, ic=ic: v.tensor_scalar(out=scr[:, :], in0=bcb[:, :],
                                               scalar1=Lmat[:, ic:ic + 1], scalar2=None,
                                               op0=ALU.is_gt))
        em.vo(lambda v, ic=ic: v.tensor_reduce(out=ssc[:, ic:ic + 1], in_=scr[:, :],
                                               axis=AX.X, op=ALU.add))
    nv_sel = em.vo(lambda v: v.tensor_scalar(out=selmat[:, :], in0=ssc[:, :],
                                             scalar1=float(M), scalar2=None,
                                             op0=ALU.is_lt))
    nv_selb = em.vo(lambda v: v.tensor_copy(out=ssc[:, :], in_=selmat[:, :]))
    nv_selb2 = em.vo(lambda v: v.tensor_copy(out=PT[:, 0, 0:16], in_=selmat[:, :]))
    em.t(W(CV.h, nv_selb2))
    for I in range(16):
        for J in range(I):
            em.t(lambda t, I=I, J=J: t.matmul(psF[:, 5, 0:1], ones128[:, :],
                                              PT[:, 0, J:J + 1], start=(J == 0),
                                              stop=False))
        ntc = em.to(lambda t, I=I: t.matmul(psF[:, 5, 0:1], T128[:, :],
                                            PT[:, 0, I:I + 1], start=(I == 0 and True)
                                            if I == 0 else False, stop=True))
        em.v(W(CT.h, ntc))
        nvc = em.vo(lambda v, I=I: v.tensor_copy(out=cummat[:, I:I + 1],
                                                 in_=psF[:, 5, 0:1]))
        em.t(W(CV.h, nvc))
    for ic in range(16):
        em.vo(lambda v, ic=ic: v.tensor_scalar(
            out=PT[:, ic, :], in0=miota[:, :], scalar1=cummat[:, ic:ic + 1],
            scalar2=selmat[:, ic:ic + 1], op0=ALU.is_equal, op1=ALU.mult))
    nv_pt = CV.n
    em.t(W(CV.h, nv_pt))
    xg_st = {}
    GW0 = P.sem("dGW0"); GW1 = P.sem("dGW1")
    for mc in range(8):
        for dt_ in range(4):
            kk = mc * 4 + dt_
            r = kk % 2
            if kk >= 2:
                em.t(W(CV.h, xg_st[kk - 2][2]))
            for ic in range(15):
                em.t(lambda t, mc=mc, dt_=dt_, ic=ic, r=r: t.matmul(
                    psF[:, r, :], PT[:, ic, 128 * mc:128 * (mc + 1)],
                    XTOK[:, ic, 512 * dt_:512 * (dt_ + 1)], start=(ic == 0), stop=False))
            nt = em.to(lambda t, mc=mc, dt_=dt_, r=r: t.matmul(
                psF[:, r, :], PT[:, 15, 128 * mc:128 * (mc + 1)],
                XTOK[:, 15, 512 * dt_:512 * (dt_ + 1)], start=False, stop=True))
            em.v(W(CT.h, nt))
            if kk >= 2:
                em.v(W(*xg_st[kk - 2][0:2]))
            nv = em.vo(lambda v, r=r, kk=kk: v.tensor_copy(out=xgv[:, kk % 2, :],
                                                           in_=psF[:, r, :]))
            em.s(W(CV.h, nv))
            GWp = GW0 if kk % 2 == 0 else GW1
            nst = GWp.reserve(16)
            em.s(lambda s, mc=mc, dt_=dt_, kk=kk, GWp=GWp: s.dma_start(
                out=XG_d[128 * mc:128 * (mc + 1), 512 * dt_:512 * (dt_ + 1)],
                in_=xgv[:, kk % 2, :]).then_inc(GWp.h, 16))
            xg_st[kk] = (GWp.h, nst, nv)
    for mc in range(8):
        for ic in range(15):
            em.t(lambda t, mc=mc, ic=ic: t.matmul(
                psF[:, 4, 0:1], PT[:, ic, 128 * mc:128 * (mc + 1)], Lmatb[:, ic:ic + 1],
                start=(ic == 0), stop=False))
        ntr = em.to(lambda t, mc=mc: t.matmul(
            psF[:, 4, 0:1], PT[:, 15, 128 * mc:128 * (mc + 1)], Lmatb[:, 15:16],
            start=False, stop=True))
        em.v(W(CT.h, ntr))
        nvr = em.vo(lambda v, mc=mc: v.tensor_copy(out=rsmat[:, mc:mc + 1],
                                                   in_=psF[:, 4, 0:1]))
        em.t(W(CV.h, nvr))
    em.vo(lambda v: v.tensor_reduce(out=ssc[:, 0:1], in_=rsmat[:, :], axis=AX.X,
                                    op=ALU.max))
    nvg = em.vo(lambda v: v.tensor_copy(out=ssc[:, 1:2], in_=ssc[:, 0:1]))
    em.t(W(CV.h, nvg))
    ntg = em.to(lambda t: t.matmul(psF[0:1, 4, 0:128], ssc[:, 1:2], cidf[:, :],
                                   start=True, stop=True))
    em.v(W(CT.h, ntg))
    em.vo(lambda v: v.tensor_reduce(out=ssc[0:1, 2:3], in_=psF[0:1, 4, 0:128],
                                    axis=AX.X, op=ALU.max))
    nvn = em.vo(lambda v: v.tensor_scalar(out=ssc[0:1, 3:4], in0=ssc[0:1, 2:3],
                                          scalar1=-1.0, scalar2=None, op0=ALU.mult))
    em.t(W(CV.h, nvn))
    ntgb = em.to(lambda t: t.matmul(psF[:, 4, 0:1], conesr[0:1, :], ssc[0:1, 3:4],
                                    start=True, stop=True))
    em.v(W(CT.h, ntgb))
    nvb = em.vo(lambda v: v.tensor_copy(out=ssc[:, 4:5], in_=psF[:, 4, 0:1]))
    em.a(W(CV.h, nvb))
    nae = em.ao(lambda a: a.activation(out=rsmat[:, :], in_=rsmat[:, :], func=AF.Exp,
                                       bias=ssc[:, 4:5], scale=1.0,
                                       accum_out=ssc[:, 5:6]))
    em.t(W(CS.h, nae))
    ntz = em.to(lambda t: t.matmul(psF[0:1, 4, 0:1], ssc[:, 5:6], conescf[:, 0:1],
                                   start=True, stop=True))
    em.v(W(CT.h, ntz))
    em.vo(lambda v: v.reciprocal(out=ssc[0:1, 6:7], in_=psF[0:1, 4, 0:1]))
    nvz = em.vo(lambda v: v.tensor_copy(out=ssc[0:1, 7:8], in_=ssc[0:1, 6:7]))
    em.t(W(CV.h, nvz))
    ntzb = em.to(lambda t: t.matmul(psF[:, 4, 0:1], conesr[0:1, :], ssc[0:1, 7:8],
                                    start=True, stop=True))
    em.v(W(CT.h, ntzb))
    em.vo(lambda v: v.tensor_copy(out=ssc[:, 8:9] if False else rw[:, 0:1],
                                  in_=psF[:, 4, 0:1]))
    nv_rz = em.vo(lambda v: v.tensor_scalar(out=rw[:, :], in0=rsmat[:, :],
                                            scalar1=rw[:, 0:1], scalar2=None,
                                            op0=ALU.mult))
    em.vo(lambda v: v.tensor_scalar(out=ssc[:, :], in0=cummat[:, :], scalar1=1.0,
                                    scalar2=None, op0=ALU.add))
    em.vo(lambda v: v.tensor_tensor(out=ssc[:, :], in0=ssc[:, :], in1=selmat[:, :],
                                    op=ALU.mult))
    nv_cp = em.vo(lambda v: v.tensor_scalar(out=ssc[:, :], in0=ssc[:, :], scalar1=-1.0,
                                            scalar2=None, op0=ALU.add))
    em.t(W(CV.h, nv_cp))
    ntt2 = em.to(lambda t: t.matmul(psF[0:16, 4, 0:128], ssc[:, :], cidf[:, :],
                                    start=True, stop=True))
    em.v(W(CT.h, ntt2))
    nvt2 = em.vo(lambda v: v.tensor_copy(out=t16[:, :], in_=psF[0:16, 4, 0:128]))
    em.s(W(CV.h, nvt2))
    n_r3 = DRW.reserve(16)
    em.s(lambda s: s.dma_start(out=row_d[1], in_=t16[:, :]).then_inc(DRW.h, 16))
    P.run_block("l1f_sel", em)
    ctx1.close()

    if L1F_PARTS[0] < 2:
        return None
    ctxVI = ExitStack()
    VI = ctxVI.enter_context(nc.sbuf_tensor("l1f_VI", [128, 8, D], BF16))

    f1 = P.fence()
    ctx2 = ExitStack()
    sb2 = lambda n, shp, dt=F32: ctx2.enter_context(nc.sbuf_tensor("l1f2_" + n, list(shp), dt))
    WB = sb2("WB", [128, 16, D], BF16)
    H1T = sb2("H1T", [128, 16, M], BF16)
    CI = sb2("CI", [128, 8, 128])
    SI = sb2("SI", [128, 8, 128])
    CIq = sb2("CIq", [128, 8, 128])
    SIq = sb2("SIq", [128, 8, 128])
    FR = sb2("FR", [128, 8, 128])
    g1b = sb2("g1b", [128, D], BF16)
    xgs = sb2("xgs", [128, 2, D], BF16)
    sqs = sb2("sqs", [128, D], BF16)
    inv1 = sb2("inv1", [128, 2])
    h1c = sb2("h1c", [128, 2, D], BF16)
    qt1 = sb2("qt1", [128, D])
    qsw = sb2("qsw", [128, D], BF16)
    qrc = sb2("qrc", [128, 2, D], BF16)
    grow = sb2("grow", [1, D], BF16)

    em = Em(P)
    wf = P.wait_fence_fns(f1)
    for k in "gstva":
        em.L[k].append(wf)
    DG1 = P.sem("dDG1")
    n_g1 = DG1.reserve(16)
    em.g(lambda g_: g_.dma_start(out=grow[:], in_=g1.rearrange("(a d) -> a d", a=1))
         .then_inc(DG1.h, 16))
    DFR = P.sem("dDFR")
    n_fr = DFR.reserve(16)
    em.s(lambda s: s.dma_start(out=FR[:], in_=freqs.rearrange("(c p) k w -> p c (k w)",
                                                             p=128)).then_inc(DFR.h, 16))
    em.v(W(DFR.h, n_fr))
    FRv = FR[:].rearrange("p c (k w) -> p c k w", w=2)
    em.vo(lambda v: v.tensor_copy(out=CI[:, :, 0:64], in_=FRv[:, :, :, 0]))
    em.vo(lambda v: v.tensor_copy(out=CI[:, :, 64:128], in_=FRv[:, :, :, 0]))
    em.vo(lambda v: v.tensor_scalar(out=SI[:, :, 0:64], in0=FRv[:, :, :, 1],
                                    scalar1=-1.0, scalar2=None, op0=ALU.mult))
    em.vo(lambda v: v.tensor_copy(out=SI[:, :, 64:128], in_=FRv[:, :, :, 1]))
    rsc = 1.0 / float(np.sqrt(128.0))
    em.vo(lambda v: v.tensor_scalar(out=CIq[:], in0=CI[:], scalar1=rsc, scalar2=None,
                                    op0=ALU.mult))
    em.vo(lambda v: v.tensor_scalar(out=SIq[:], in0=SI[:], scalar1=rsc, scalar2=None,
                                    op0=ALU.mult))
    em.t(W(DG1.h, n_g1))
    for ft in range(4):
        em.t(W(CV.h, CV.n))
        nt = em.to(lambda t, ft=ft: t.matmul(psF[:, 4, :], conesrb[0:1, :],
                                             grow[0:1, 512 * ft:512 * (ft + 1)],
                                             start=True, stop=True))
        em.v(W(CT.h, nt))
        em.vo(lambda v, ft=ft: v.tensor_copy(out=g1b[:, 512 * ft:512 * (ft + 1)],
                                             in_=psF[:, 4, :]))
    DXG = [P.sem("dDXG%d" % i) for i in range(2)]
    DH1 = [P.sem("dDH1%d" % i) for i in range(2)]
    xg_cons = {}
    h1_cons = {}
    for mc in range(8):
        sl = mc % 2
        if sl in xg_cons:
            em.s(W(CV.h, xg_cons[sl]))
        nl = DXG[sl].reserve(16)
        em.s(lambda s, mc=mc, sl=sl: s.dma_start(
            out=xgs[:, sl, :], in_=XG_d[128 * mc:128 * (mc + 1), :])
            .then_inc(DXG[sl].h, 16))
        em.a(W(DXG[sl].h, nl))
        em.a(W(CV.h, CV.n))
        nsq = em.ao(lambda a, sl=sl: a.activation(out=sqs[:, :], in_=xgs[:, sl, :],
                                                  func=AF.Square, bias=0.0, scale=1.0))
        em.v(W(DXG[sl].h, nl))
        em.v(W(CS.h, nsq))
        em.vo(lambda v, sl=sl: v.tensor_reduce(out=inv1[:, sl:sl + 1], in_=sqs[:, :],
                                               axis=AX.X, op=ALU.add))
        nvv = em.vo(lambda v, sl=sl: v.tensor_scalar(out=inv1[:, sl:sl + 1],
                                                     in0=inv1[:, sl:sl + 1],
                                                     scalar1=1.0 / D, scalar2=None,
                                                     op0=ALU.mult))
        em.a(W(CV.h, nvv))
        na = em.ao(lambda a, sl=sl: a.activation(out=inv1[:, sl:sl + 1],
                                                 in_=inv1[:, sl:sl + 1], func=AF.Sqrt,
                                                 bias=cscal[:, 1:2], scale=1.0))
        em.v(W(CS.h, na))
        em.vo(lambda v, sl=sl: v.reciprocal(out=inv1[:, sl:sl + 1],
                                            in_=inv1[:, sl:sl + 1]))
        if sl in h1_cons:
            em.v(W(*h1_cons[sl]))
        nvh = em.vo(lambda v, sl=sl: v.scalar_tensor_tensor(
            out=h1c[:, sl, :], in0=xgs[:, sl, :], scalar=inv1[:, sl:sl + 1],
            in1=g1b[:, :], op0=ALU.mult, op1=ALU.mult))
        xg_cons[sl] = nvh
        em.s(W(CV.h, nvh))
        nst = DH1[sl].reserve(16)
        em.s(lambda s, mc=mc, sl=sl: s.dma_start(
            out=H1_d[128 * mc:128 * (mc + 1), :], in_=h1c[:, sl, :])
            .then_inc(DH1[sl].h, 16))
        h1_cons[sl] = (DH1[sl].h, nst)
    DH1T = P.sem("dDH1T")
    em.s(W(DH1[0].h, DH1[0].n))
    em.s(W(DH1[1].h, DH1[1].n))
    for dc in range(16):
        DH1T.reserve(16)
        em.s(lambda s, dc=dc: s.dma_start(out=H1T[:, dc, :],
                                          in_=H1_d[0:M, 128 * dc:128 * (dc + 1)],
                                          transpose=True).then_inc(DH1T.h, 16))
    n_h1t = DH1T.n

    DWB = P.sem("dDWB")
    QIW0 = P.sem("dQIW0"); QIW1 = P.sem("dQIW1")
    qst = {}
    qkv_done = {}
    em.t(W(DH1T.h, n_h1t))
    for wi, (wsrc, kind) in enumerate([(wqI, "q"), (wkI, "k"), (wvI, "v")]):
        if wi > 0:
            em.g(W(CT.h, CT.n))
        n_w = DWB.reserve(16)
        em.g(lambda g, wsrc=wsrc: g.dma_start(
            out=WB[:], in_=wsrc.rearrange("(c p) f -> p c f", p=128)).then_inc(DWB.h, 16))
        em.t(W(DWB.h, n_w))
        for mc in range(8):
            par = (wi * 8 + mc) % 2
            bb = par * 4
            if par in qkv_done:
                em.t(W(CV.h, qkv_done[par]))
            for ft in range(4):
                for dc in range(15):
                    em.t(lambda t, mc=mc, ft=ft, dc=dc, bb=bb: t.matmul(
                        psF[:, bb + ft, :], H1T[:, dc, 128 * mc:128 * (mc + 1)],
                        WB[:, dc, 512 * ft:512 * (ft + 1)], start=(dc == 0), stop=False))
                nt = em.to(lambda t, mc=mc, ft=ft, bb=bb: t.matmul(
                    psF[:, bb + ft, :], H1T[:, 15, 128 * mc:128 * (mc + 1)],
                    WB[:, 15, 512 * ft:512 * (ft + 1)], start=False, stop=True))
            em.v(W(CT.h, nt))
            if kind == "v":
                qkv_done[par] = em.vo(lambda v, mc=mc, bb=bb: v.tensor_copy(
                    out=VI[:, mc, :],
                    in_=psF[:, bb:bb + 4, :].rearrange("p a b -> p (a b)")))
            else:
                CE_, SE_ = (CIq, SIq) if kind == "q" else (CI, SI)
                pview = psF[:, bb:bb + 4, :].rearrange("p a b -> p (a b)").rearrange(
                    "p (h f) -> p h f", h=16)
                pv2 = psF[:, bb:bb + 4, :].rearrange("p a b -> p (a b)").rearrange(
                    "p (h a f) -> p h a f", a=2, f=64)
                sw2 = qsw[:].rearrange("p (h a f) -> p h a f", a=2, f=64)
                em.vo(lambda v, pv2=pv2: v.tensor_copy(out=sw2[:, :, 0, :],
                                                       in_=pv2[:, :, 1, :]))
                em.vo(lambda v, pv2=pv2: v.tensor_copy(out=sw2[:, :, 1, :],
                                                       in_=pv2[:, :, 0, :]))
                qkv_done[par] = em.vo(lambda v, mc=mc, CE_=CE_, pview=pview:
                    v.tensor_tensor(
                        out=qt1[:].rearrange("p (h f) -> p h f", h=16), in0=pview,
                        in1=CE_[:, mc:mc + 1, :].to_broadcast([128, 16, 128]),
                        op=ALU.mult))
                em.vo(lambda v, mc=mc, SE_=SE_: v.tensor_tensor(
                    out=qsw[:].rearrange("p (h f) -> p h f", h=16),
                    in0=qsw[:].rearrange("p (h f) -> p h f", h=16),
                    in1=SE_[:, mc:mc + 1, :].to_broadcast([128, 16, 128]), op=ALU.mult))
                qsl = mc % 2
                if qsl in qst:
                    em.v(W(*qst[qsl]))
                nv = em.vo(lambda v, qsl=qsl: v.tensor_tensor(
                    out=qrc[:, qsl, :], in0=qt1[:], in1=qsw[:], op=ALU.add))
                em.s(W(CV.h, nv))
                QWp = QIW0 if qsl == 0 else QIW1
                nst = QWp.reserve(16)
                dst = QI_d if kind == "q" else KI_d
                em.s(lambda s, mc=mc, qsl=qsl, QWp=QWp, dst=dst: s.dma_start(
                    out=dst[128 * mc:128 * (mc + 1), :], in_=qrc[:, qsl, :])
                    .then_inc(QWp.h, 16))
                qst[qsl] = (QWp.h, nst)
    P.run_block("l1f_qkv", em)
    ctx2.close()

    if L1F_PARTS[0] < 3:
        return None
    f2 = P.fence()
    ctx3 = ExitStack()
    sb3 = lambda n, shp, dt=F32: ctx3.enter_context(nc.sbuf_tensor("l1f3_" + n, list(shp), dt))
    MT = sb3("MT", [128, 8, M], BF16)
    MB = sb3("MB", [128, 2, M], BF16)
    PTI = sb3("PTI", [128, 8, 512], BF16)
    qh = sb3("qh", [128, 2, M], BF16)
    kh = sb3("kh", [128, 2, M], BF16)
    mm8 = sb3("mm8", [128, 8])
    t16b = sb3("t16b", [16, 128])
    mrow = sb3("mrow", [1, M])
    mrowb = sb3("mrowb", [128, M], BF16)
    rowz = sb3("rowz", [1, 512])
    rzb = sb3("rzb", [128, 512])
    attc = sb3("attc", [128, 2, M], BF16)

    em = Em(P)
    wf = P.wait_fence_fns(f2)
    for k in "gstva":
        em.L[k].append(wf)
    DMT = P.sem("dDMT")
    DMB = [P.sem("dDMB%d" % i) for i in range(2)]
    for jc in range(8):
        DMT.reserve(16)
        em.s(lambda s, jc=jc: s.dma_start(out=MT[:, jc, :],
                                          in_=scmask[0:M, 128 * jc:128 * (jc + 1)],
                                          transpose=True).then_inc(DMT.h, 16))
    n_mt = DMT.n
    mb_cons = {}
    for ic in range(8):
        sl = ic % 2
        if sl in mb_cons:
            em.s(W(CV.h, mb_cons[sl]))
        nl = DMB[sl].reserve(16)
        em.s(lambda s, ic=ic, sl=sl: s.dma_start(
            out=MB[:, sl, :], in_=scmask[128 * ic:128 * (ic + 1), :])
            .then_inc(DMB[sl].h, 16))
        em.v(W(DMB[sl].h, nl))
        mb_cons[sl] = em.vo(lambda v, ic=ic, sl=sl: v.tensor_reduce(
            out=mm8[:, ic:ic + 1], in_=MB[:, sl, :], axis=AX.X, op=ALU.max))
    nvm = CV.n
    em.t(W(CV.h, nvm))
    ntm = em.to(lambda t: t.matmul(psF[0:8, 4, 0:128], mm8[:, :], cidf[:, :],
                                   start=True, stop=True))
    em.v(W(CT.h, ntm))
    nvt = em.vo(lambda v: v.tensor_copy(out=t16b[0:8, :], in_=psF[0:8, 4, 0:128]))
    em.s(W(CV.h, nvt))
    DRW2 = P.sem("dDRW2")
    n_m1 = DRW2.reserve(16)
    em.s(lambda s: s.dma_start(out=row_d[2, 0:8, :], in_=t16b[0:8, :]).then_inc(DRW2.h, 16))
    em.s(W(DRW2.h, n_m1))
    n_m2 = DRW2.reserve(16)
    em.s(lambda s: s.dma_start(out=mrow[:],
                               in_=row_d.rearrange("a b c -> a (b c)")[2:3, 0:M])
         .then_inc(DRW2.h, 16))
    em.v(W(DRW2.h, n_m2))
    em.t(W(DRW2.h, n_m2))
    for half in range(2):
        ntb2 = em.to(lambda t, half=half: t.matmul(
            psF[:, 4 + half, :], conesr[0:1, :], mrow[0:1, 512 * half:512 * (half + 1)],
            start=True, stop=True))
        em.v(W(CT.h, ntb2))
        em.vo(lambda v, half=half: v.tensor_copy(
            out=mrowb[:, 512 * half:512 * (half + 1)], in_=psF[:, 4 + half, :]))
    em.v(W(DMT.h, n_mt))
    for jc in range(8):
        em.vo(lambda v, jc=jc: v.tensor_tensor(
            out=MT[:, jc, :], in0=MT[:, jc, :], in1=mrowb[:, :], op=ALU.subtract))
    nv_mtp = CV.n

    DQH = [P.sem("dDQH%d" % i) for i in range(2)]
    DKH = [P.sem("dDKH%d" % i) for i in range(2)]
    AW = [P.sem("dAW%d" % i) for i in range(2)]
    qh_cons = {}
    att_cons = {}
    evac_n = {}
    recip_prev = nv_mtp
    em.t(W(DMT.h, n_mt))
    em.t(W(CV.h, nv_mtp))
    for h in range(16):
        hsl = h % 2
        if hsl in qh_cons:
            em.s(W(CT.h, qh_cons[hsl]))
        nqh = DQH[hsl].reserve(16)
        em.s(lambda s, h=h, hsl=hsl: s.dma_start(
            out=qh[:, hsl, :], in_=QI_d[0:M, 128 * h:128 * (h + 1)], transpose=True)
            .then_inc(DQH[hsl].h, 16))
        nkh = DKH[hsl].reserve(16)
        em.s(lambda s, h=h, hsl=hsl: s.dma_start(
            out=kh[:, hsl, :], in_=KI_d[0:M, 128 * h:128 * (h + 1)], transpose=True)
            .then_inc(DKH[hsl].h, 16))
        em.t(W(DQH[hsl].h, nqh))
        em.t(W(DKH[hsl].h, nkh))
        for it in range(2):
            exp_n = {}
            for jc in range(8):
                r = jc % 2
                if jc >= 2:
                    em.t(W(CS.h, exp_n[jc - 2]))
                em.t(lambda t, r=r, jc=jc, it=it, hsl=hsl: t.matmul(
                    psF[:, r, :], kh[:, hsl, 128 * jc:128 * (jc + 1)],
                    qh[:, hsl, 512 * it:512 * (it + 1)], start=True, stop=False))
                nt = em.to(lambda t, r=r, jc=jc, it=it: t.matmul(
                    psF[:, r, :], cidb[:, :], MT[:, jc, 512 * it:512 * (it + 1)],
                    start=False, stop=True))
                em.a(W(CT.h, nt))
                exp_n[jc] = em.ao(lambda a, r=r, jc=jc: a.activation(
                    out=PTI[:, jc, :], in_=psF[:, r, :], func=AF.Exp,
                    bias=0.0, scale=1.0))
            em.t(W(CV.h, recip_prev))
            for jc in range(7):
                em.t(W(CS.h, exp_n[jc]))
                em.t(lambda t, jc=jc: t.matmul(psF[0:1, 4, :], conescb[:, 0:1],
                                               PTI[:, jc, :], start=(jc == 0),
                                               stop=False))
            em.t(W(CS.h, exp_n[7]))
            ntz = em.to(lambda t: t.matmul(psF[0:1, 4, :], conescb[:, 0:1],
                                           PTI[:, 7, :], start=False, stop=True))
            em.v(W(CT.h, ntz))
            nv_recip = em.vo(lambda v: v.reciprocal(out=rowz[0:1, :],
                                                    in_=psF[0:1, 4, :]))
            recip_prev = nv_recip
            if it in evac_n:
                em.t(W(CV.h, evac_n[it]))
            for jc in range(7):
                em.t(lambda t, jc=jc, it=it, h=h: t.matmul(
                    psF[:, 2 + it, :], VI[:, jc, 128 * h:128 * (h + 1)],
                    PTI[:, jc, :], start=(jc == 0), stop=False))
            nta = em.to(lambda t, it=it, h=h: t.matmul(
                psF[:, 2 + it, :], VI[:, 7, 128 * h:128 * (h + 1)],
                PTI[:, 7, :], start=False, stop=True))
            em.t(W(CV.h, nv_recip))
            ntb = em.to(lambda t: t.matmul(psF[:, 5, :], conesr[0:1, :],
                                           rowz[0:1, :], start=True, stop=True))
            em.v(W(CT.h, ntb))
            em.vo(lambda v: v.tensor_copy(out=rzb[:, :], in_=psF[:, 5, :]))
            em.v(W(CT.h, nta))
            if it == 0 and hsl in att_cons:
                em.v(W(*att_cons[hsl]))
            evac_n[it] = em.vo(lambda v, it=it, hsl=hsl: v.tensor_tensor(
                out=attc[:, hsl, 512 * it:512 * (it + 1)], in0=psF[:, 2 + it, :],
                in1=rzb[:, :], op=ALU.mult))
        nv_att = CV.n
        em.s(W(CV.h, nv_att))
        AWp = AW[hsl]
        nsta = AWp.reserve(16)
        em.s(lambda s, h=h, hsl=hsl, AWp=AWp: s.dma_start(
            out=ATTd_rows(h), in_=attc[:, hsl, :])
            .then_inc(AWp.h, 16))
        att_cons[hsl] = (AWp.h, nsta)
        qh_cons[hsl] = CT.n
    P.run_block("l1f_attn", em)
    ctx3.close()
    ctxVI.close()

    if L1F_PARTS[0] < 4:
        return None
    f3 = P.fence()
    ctx4 = ExitStack()
    sb4 = lambda n, shp, dt=F32: ctx4.enter_context(nc.sbuf_tensor("l1f4_" + n, list(shp), dt))
    WB2 = sb4("WB2", [128, 16, D], BF16)
    ATs = sb4("ATs", [128, 2, M], BF16)
    xgs2 = sb4("xgs2", [128, 2, D], BF16)
    z2c = sb4("z2c", [128, 2, D], BF16)
    g2b = sb4("g2b", [128, D], BF16)
    grow2 = sb4("grow2", [1, D], BF16)
    sqs2 = sb4("sqs2", [128, D], BF16)
    inv2 = sb4("inv2", [128, 2])
    h2c = sb4("h2c", [128, 2, D], BF16)

    em = Em(P)
    wf = P.wait_fence_fns(f3)
    for k in "gstva":
        em.L[k].append(wf)
    DWB2 = P.sem("dDWB2")
    n_w = DWB2.reserve(16)
    em.g(lambda g_: g_.dma_start(out=WB2[:], in_=woI.rearrange("(c p) f -> p c f", p=128))
         .then_inc(DWB2.h, 16))
    DG2 = P.sem("dDG2")
    n_g2 = DG2.reserve(16)
    em.g(lambda g_: g_.dma_start(out=grow2[:], in_=g2.rearrange("(a d) -> a d", a=1))
         .then_inc(DG2.h, 16))
    em.t(W(DG2.h, n_g2))
    for ft in range(4):
        em.t(W(CV.h, CV.n))
        nt = em.to(lambda t, ft=ft: t.matmul(psF[:, 4, :], conesrb[0:1, :],
                                             grow2[0:1, 512 * ft:512 * (ft + 1)],
                                             start=True, stop=True))
        em.v(W(CT.h, nt))
        em.vo(lambda v, ft=ft: v.tensor_copy(out=g2b[:, 512 * ft:512 * (ft + 1)],
                                             in_=psF[:, 4, :]))
    ATF = sb4("ATF", [128, 16, M], BF16)
    DAT = P.sem("dDAT")
    for h in range(16):
        DAT.reserve(16)
        em.s(lambda s, h=h: s.dma_start(out=ATF[:, h, :],
                                        in_=ATT_d[128 * h:128 * (h + 1), :])
             .then_inc(DAT.h, 16))
    n_at = DAT.n
    em.t(W(DAT.h, n_at))
    em.t(W(DWB2.h, n_w))
    DXG2 = [P.sem("dDXG2%d" % i) for i in range(2)]
    DZ2 = [P.sem("dDZ2%d" % i) for i in range(2)]
    DH2 = [P.sem("dDH2%d" % i) for i in range(2)]
    xg2_cons = {}
    z2_cons = {}
    h2_cons = {}
    for mc in range(8):
        sl = mc % 2
        if sl in xg2_cons:
            em.s(W(CV.h, xg2_cons[sl]))
        nl = DXG2[sl].reserve(16)
        em.s(lambda s, mc=mc, sl=sl: s.dma_start(
            out=xgs2[:, sl, :], in_=XG_d[128 * mc:128 * (mc + 1), :])
            .then_inc(DXG2[sl].h, 16))
        if mc > 0:
            em.t(W(CV.h, CV.n))
        for ft in range(4):
            for hh in range(15):
                em.t(lambda t, mc=mc, ft=ft, hh=hh: t.matmul(
                    psF[:, ft, :], ATF[:, hh, 128 * mc:128 * (mc + 1)],
                    WB2[:, hh, 512 * ft:512 * (ft + 1)], start=(hh == 0), stop=False))
            nt = em.to(lambda t, mc=mc, ft=ft: t.matmul(
                psF[:, ft, :], ATF[:, 15, 128 * mc:128 * (mc + 1)],
                WB2[:, 15, 512 * ft:512 * (ft + 1)], start=False, stop=True))
        em.v(W(CT.h, nt))
        em.v(W(DXG2[sl].h, nl))
        if sl in z2_cons:
            em.v(W(*z2_cons[sl]))
        nvz = em.vo(lambda v, sl=sl: v.tensor_tensor(
            out=z2c[:, sl, :], in0=psF[:, 0:4, :].rearrange("p a b -> p (a b)"),
            in1=xgs2[:, sl, :], op=ALU.add))
        xg2_cons[sl] = nvz
        em.s(W(CV.h, nvz))
        nstz = DZ2[sl].reserve(16)
        em.s(lambda s, mc=mc, sl=sl: s.dma_start(
            out=Z2_d[128 * mc:128 * (mc + 1), :], in_=z2c[:, sl, :])
            .then_inc(DZ2[sl].h, 16))
        z2_cons[sl] = (DZ2[sl].h, nstz)
        em.a(W(CV.h, nvz))
        nsq = em.ao(lambda a, sl=sl: a.activation(out=sqs2[:, :], in_=z2c[:, sl, :],
                                                  func=AF.Square, bias=0.0, scale=1.0))
        em.v(W(CS.h, nsq))
        em.vo(lambda v, sl=sl: v.tensor_reduce(out=inv2[:, sl:sl + 1], in_=sqs2[:, :],
                                               axis=AX.X, op=ALU.add))
        nvv = em.vo(lambda v, sl=sl: v.tensor_scalar(out=inv2[:, sl:sl + 1],
                                                     in0=inv2[:, sl:sl + 1],
                                                     scalar1=1.0 / D, scalar2=None,
                                                     op0=ALU.mult))
        em.a(W(CV.h, nvv))
        na = em.ao(lambda a, sl=sl: a.activation(out=inv2[:, sl:sl + 1],
                                                 in_=inv2[:, sl:sl + 1], func=AF.Sqrt,
                                                 bias=cscal[:, 1:2], scale=1.0))
        em.v(W(CS.h, na))
        em.vo(lambda v, sl=sl: v.reciprocal(out=inv2[:, sl:sl + 1],
                                            in_=inv2[:, sl:sl + 1]))
        if sl in h2_cons:
            em.v(W(*h2_cons[sl]))
        nvh = em.vo(lambda v, sl=sl: v.scalar_tensor_tensor(
            out=h2c[:, sl, :], in0=z2c[:, sl, :], scalar=inv2[:, sl:sl + 1],
            in1=g2b[:, :], op0=ALU.mult, op1=ALU.mult))
        em.s(W(CV.h, nvh))
        nsth = DH2[sl].reserve(16)
        em.s(lambda s, mc=mc, sl=sl: s.dma_start(
            out=H2_d[128 * mc:128 * (mc + 1), :], in_=h2c[:, sl, :])
            .then_inc(DH2[sl].h, 16))
        h2_cons[sl] = (DH2[sl].h, nsth)
    P.run_block("l1f_wo", em)
    ctx4.close()

    if L1F_PARTS[0] < 5:
        return None
    f4 = P.fence()
    ctx5 = ExitStack()
    sb5 = lambda n, shp, dt=F32: ctx5.enter_context(nc.sbuf_tensor("l1f5_" + n, list(shp), dt))
    H2T = sb5("H2T", [128, 16, M], BF16)
    W1S = sb5("W1S", [128, 2, 16, 1024], BF16)
    W3S = sb5("W3S", [128, 2, 16, 1024], BF16)
    silb = sb5("silb", [128, 2, 512])
    hzc = sb5("hzc", [128, 2, M], BF16)

    em = Em(P)
    wf = P.wait_fence_fns(f4)
    for k in "gstva":
        em.L[k].append(wf)
    DH2T = P.sem("dDH2T")
    for dc in range(16):
        DH2T.reserve(16)
        em.s(lambda s, dc=dc: s.dma_start(out=H2T[:, dc, :],
                                          in_=H2_d[0:M, 128 * dc:128 * (dc + 1)],
                                          transpose=True).then_inc(DH2T.h, 16))
    n_h2t = DH2T.n
    em.t(W(DH2T.h, n_h2t))
    DW13 = [P.sem("dDW13%d" % i) for i in range(2)]
    DHZ0 = P.sem("dDHZ0"); DHZ1 = P.sem("dDHZ1")
    sil_cons = {}
    hz_cons = {}
    w13_n = {}
    w13_cons = {}

    def load_w13(su):
        sl = su % 2
        for ws, wsrc in ((W1S, w1I), (W3S, w3I)):
            w13_n[sl] = DW13[sl].reserve(16)
            em.g(lambda g_, su=su, sl=sl, ws=ws, wsrc=wsrc: g_.dma_start(
                out=ws[:, sl, :, :], in_=wsrc.rearrange("(c p) e -> p c e", p=128)
                [:, :, 1024 * su:1024 * (su + 1)]).then_inc(DW13[sl].h, 16))

    load_w13(0)
    for su in range(8):
        sl = su % 2
        if su + 1 < 8:
            nsl = (su + 1) % 2
            if nsl in w13_cons:
                em.g(W(CT.h, w13_cons[nsl]))
            load_w13(su + 1)
        em.t(W(DW13[sl].h, w13_n[sl]))
        for ei in range(8):
            ec = su * 8 + ei
            for mt in range(2):
                kk = ei * 2 + mt
                ru = kk % 2
                rg = 2 + kk % 2
                if (su, ei, mt) != (0, 0, 0):
                    em.t(W(CS.h, CS.n))
                for dc in range(15):
                    em.t(lambda t, ru=ru, dc=dc, ei=ei, mt=mt, sl=sl: t.matmul(
                        psF[:, ru, :], W1S[:, sl, dc, 128 * ei:128 * (ei + 1)],
                        H2T[:, dc, 512 * mt:512 * (mt + 1)], start=(dc == 0), stop=False))
                ntu = em.to(lambda t, ru=ru, ei=ei, mt=mt, sl=sl: t.matmul(
                    psF[:, ru, :], W1S[:, sl, 15, 128 * ei:128 * (ei + 1)],
                    H2T[:, 15, 512 * mt:512 * (mt + 1)], start=False, stop=True))
                if (su, ei, mt) != (0, 0, 0):
                    em.t(W(CV.h, CV.n))
                for dc in range(15):
                    em.t(lambda t, rg=rg, dc=dc, ei=ei, mt=mt, sl=sl: t.matmul(
                        psF[:, rg, :], W3S[:, sl, dc, 128 * ei:128 * (ei + 1)],
                        H2T[:, dc, 512 * mt:512 * (mt + 1)], start=(dc == 0), stop=False))
                ntg = em.to(lambda t, rg=rg, ei=ei, mt=mt, sl=sl: t.matmul(
                    psF[:, rg, :], W3S[:, sl, 15, 128 * ei:128 * (ei + 1)],
                    H2T[:, 15, 512 * mt:512 * (mt + 1)], start=False, stop=True))
                ssl = kk % 2
                if ssl in sil_cons:
                    em.a(W(CV.h, sil_cons[ssl]))
                em.a(W(CT.h, ntu))
                na = em.ao(lambda a, ru=ru, ssl=ssl: a.activation(
                    out=silb[:, ssl, :], in_=psF[:, ru, :], func=AF.Silu,
                    bias=0.0, scale=1.0))
                em.v(W(CS.h, na))
                em.v(W(CT.h, ntg))
                if mt == 0 and (ec % 2) in hz_cons:
                    em.v(W(*hz_cons[ec % 2]))
                nv = em.vo(lambda v, rg=rg, ssl=ssl, ec=ec, mt=mt: v.tensor_tensor(
                    out=hzc[:, ec % 2, 512 * mt:512 * (mt + 1)],
                    in0=psF[:, rg, :], in1=silb[:, ssl, :], op=ALU.mult))
                sil_cons[ssl] = nv
            DHZp = DHZ0 if ec % 2 == 0 else DHZ1
            em.s(W(CV.h, CV.n))
            nst = DHZp.reserve(16)
            em.s(lambda s, ec=ec, DHZp=DHZp: s.dma_start(
                out=HZ_d[128 * ec:128 * (ec + 1), :], in_=hzc[:, ec % 2, :])
                .then_inc(DHZp.h, 16))
            hz_cons[ec % 2] = (DHZp.h, nst)
        w13_cons[sl] = CT.n
    P.run_block("l1f_ffn1", em)
    ctx5.close()

    f5 = P.fence()
    ctx6 = ExitStack()
    sb6 = lambda n, shp, dt=F32: ctx6.enter_context(nc.sbuf_tensor("l1f6_" + n, list(shp), dt))
    TD2 = sb6("TD2", [128, 2, 64, 512], BF16)
    hzsl = sb6("hzsl", [128, 2, 8, 512], BF16)
    z2ts = sb6("z2ts", [128, 2, 512], BF16)
    z3c = sb6("z3c", [128, 2, 512], BF16)

    em = Em(P)
    wf = P.wait_fence_fns(f5)
    for k in "gstva":
        em.L[k].append(wf)
    DTD = [P.sem("dDTD%d" % i) for i in range(2)]
    DHS = [P.sem("dDHS%d" % i) for i in range(2)]
    DZT = [P.sem("dDZT%d" % i) for i in range(2)]
    DZ3 = [P.sem("dDZ3%d" % i) for i in range(2)]
    hs_cons = {}
    zt_cons = {}
    z3_cons = {}
    td_n = {}
    td_cons = {}

    def load_td2(dgrp):
        sl = dgrp % 2
        td_n[sl] = DTD[sl].reserve(16)
        em.g(lambda g_, dgrp=dgrp, sl=sl: g_.dma_start(
            out=TD2[:, sl, :, :], in_=w2I.rearrange("(c p) d -> p c d", p=128)
            [:, :, 512 * dgrp:512 * (dgrp + 1)]).then_inc(DTD[sl].h, 16))

    load_td2(0)
    for dgrp in range(4):
        sl = dgrp % 2
        if dgrp + 1 < 4:
            nsl = (dgrp + 1) % 2
            if nsl in td_cons:
                em.g(W(CT.h, td_cons[nsl]))
            load_td2(dgrp + 1)
        em.t(W(DTD[sl].h, td_n[sl]))
        for mt in range(2):
            em.t(W(CV.h, CV.n))
            for ecg in range(8):
                hslot = ecg % 2
                if hslot in hs_cons:
                    em.s(W(CT.h, hs_cons[hslot]))
                nl = DHS[hslot].reserve(16)
                em.s(lambda s, ecg=ecg, mt=mt, hslot=hslot: s.dma_start(
                    out=hzsl[:, hslot, :, :],
                    in_=HZ_d.rearrange("(g p) s -> p g s", p=128)
                    [:, 8 * ecg:8 * (ecg + 1), 512 * mt:512 * (mt + 1)])
                    .then_inc(DHS[hslot].h, 16))
                em.t(W(DHS[hslot].h, nl))
                for ei in range(8):
                    ec = ecg * 8 + ei
                    for dc4 in range(4):
                        last = (ec == 63)
                        if (dc4 == 3 and ei == 7) or last:
                            nlast = em.to(lambda t, dc4=dc4, ec=ec, hslot=hslot, ei=ei,
                                          last=last, sl=sl: t.matmul(
                                psF[:, dc4, :], TD2[:, sl, ec, 128 * dc4:128 * (dc4 + 1)],
                                hzsl[:, hslot, ei, :], start=(ec == 0), stop=last))
                            if dc4 == 3:
                                hs_cons[hslot] = nlast
                        else:
                            em.t(lambda t, dc4=dc4, ec=ec, hslot=hslot, ei=ei,
                                 last=last, sl=sl: t.matmul(
                                psF[:, dc4, :], TD2[:, sl, ec, 128 * dc4:128 * (dc4 + 1)],
                                hzsl[:, hslot, ei, :], start=(ec == 0), stop=last))
            nt_done = CT.n
            for dc4 in range(4):
                dc = dgrp * 4 + dc4
                zsl = dc4 % 2
                if zsl in zt_cons:
                    em.s(W(CV.h, zt_cons[zsl]))
                nlz = DZT[zsl].reserve(16)
                em.s(lambda s, dc=dc, mt=mt, zsl=zsl: s.dma_start(
                    out=z2ts[:, zsl, :],
                    in_=Z2_d[512 * mt:512 * (mt + 1), 128 * dc:128 * (dc + 1)],
                    transpose=True).then_inc(DZT[zsl].h, 16))
                em.v(W(CT.h, nt_done))
                em.v(W(DZT[zsl].h, nlz))
                if zsl in z3_cons:
                    em.v(W(*z3_cons[zsl]))
                nvz = em.vo(lambda v, dc4=dc4, zsl=zsl: v.tensor_tensor(
                    out=z3c[:, zsl, :], in0=psF[:, dc4, :], in1=z2ts[:, zsl, :],
                    op=ALU.add))
                zt_cons[zsl] = nvz
                em.s(W(CV.h, nvz))
                nstz = DZ3[zsl].reserve(16)
                em.s(lambda s, dc=dc, mt=mt, zsl=zsl: s.dma_start(
                    out=Z3T_d[128 * dc:128 * (dc + 1), 512 * mt:512 * (mt + 1)],
                    in_=z3c[:, zsl, :]).then_inc(DZ3[zsl].h, 16))
                z3_cons[zsl] = (DZ3[zsl].h, nstz)
        td_cons[sl] = CT.n
    P.run_block("l1f_ffn2", em)
    ctx6.close()

    if L1F_PARTS[0] < 6:
        return None
    f6 = P.fence()
    ctx7 = ExitStack()
    sb7 = lambda n, shp, dt=F32: ctx7.enter_context(nc.sbuf_tensor("l1f7_" + n, list(shp), dt))
    Pbuf = sb7("Pbuf", [128, 8, S], BF16)
    Z3S = sb7("Z3S", [128, 8, D], BF16)
    XT2 = sb7("XT2", [128, 16, D], BF16)
    bcb2 = sb7("bcb2", [128, S])
    lrow2 = sb7("lrow2", [1, S])
    zit = sb7("zit", [128, 4, D], BF16)
    zsq4 = sb7("zsq4", [128, 2, 512], BF16)
    sumB4 = sb7("sumB4", [128, 16])
    sqB4 = sb7("sqB4", [128, 16])
    red24 = sb7("red24", [128, 2])
    c14 = sb7("c14", [128, 16])
    c04 = sb7("c04", [128, 16])
    scr164 = sb7("scr164", [128, 16])
    outc = sb7("outc", [128, D])

    em = Em(P)
    wf = P.wait_fence_fns(f6)
    for k in "gstva":
        em.L[k].append(wf)
    DRW3 = P.sem("dDRW3")
    n_r4 = DRW3.reserve(16)
    em.s(lambda s: s.dma_start(out=lrow2[:],
                               in_=row_d.rearrange("a b c -> a (b c)")[1:2, :])
         .then_inc(DRW3.h, 16))
    em.t(W(DRW3.h, n_r4))
    for ft in range(4):
        em.t(W(CV.h, CV.n))
        nt = em.to(lambda t, ft=ft: t.matmul(psF[:, 4, :], conesr[0:1, :],
                                             lrow2[0:1, 512 * ft:512 * (ft + 1)],
                                             start=True, stop=True))
        em.v(W(CT.h, nt))
        em.vo(lambda v, ft=ft: v.tensor_copy(out=bcb2[:, 512 * ft:512 * (ft + 1)],
                                             in_=psF[:, 4, :]))
    for mc in range(8):
        em.vo(lambda v, mc=mc: v.tensor_scalar(
            out=Pbuf[:, mc, :], in0=bcb2[:, :], scalar1=cmcols[:, mc:mc + 1],
            scalar2=rw[:, mc:mc + 1], op0=ALU.is_equal, op1=ALU.mult))
    nv_p = CV.n
    if L1F_PARTS[0] == 60:
        P.run_block("l1f_scatter", em)
        ctx7.close()
        return None
    DZ3S = P.sem("dDZ3S")
    for mc in range(8):
        DZ3S.reserve(16)
        em.s(lambda s, mc=mc: s.dma_start(out=Z3S[:, mc, :],
                                          in_=Z3T_d[0:D, 128 * mc:128 * (mc + 1)],
                                          transpose=True).then_inc(DZ3S.h, 16))
    n_z3s = DZ3S.n
    DXT2 = P.sem("dDXT2")
    for c in range(16):
        DXT2.reserve(16)
        em.s(lambda s, c=c: s.dma_start(out=XT2[:, c, :],
                                        in_=X3[128 * c:128 * (c + 1), :])
             .then_inc(DXT2.h, 16))
    n_xt2 = DXT2.n
    em.t(W(DZ3S.h, n_z3s))
    em.t(W(CV.h, nv_p))
    em.v(W(DXT2.h, n_xt2))
    if L1F_PARTS[0] == 61:
        P.run_block("l1f_scatter", em)
        ctx7.close()
        return None
    st_base = ST.n
    o_tt = {}
    for icg in range(([1] if L1F_PARTS[0] == 62 else [4])[0]):
        for k in range(16):
            ic_l = k // 4
            ft = k % 4
            ic = icg * 4 + ic_l
            r = k % 2
            if k >= 2:
                em.t(W(CV.h, o_tt[k - 2]))
            elif icg > 0:
                em.t(W(CV.h, CV.n))
            for mc in range(7):
                em.t(lambda t, r=r, ic=ic, ft=ft, mc=mc: t.matmul(
                    psF[:, 2 + r, :], Pbuf[:, mc, 128 * ic:128 * (ic + 1)],
                    Z3S[:, mc, 512 * ft:512 * (ft + 1)], start=(mc == 0), stop=False))
            nto = em.to(lambda t, r=r, ic=ic, ft=ft: t.matmul(
                psF[:, 2 + r, :], Pbuf[:, 7, 128 * ic:128 * (ic + 1)],
                Z3S[:, 7, 512 * ft:512 * (ft + 1)], start=False, stop=True))
            em.v(W(CT.h, nto))
            o_tt[k] = em.vo(lambda v, r=r, ic=ic, ic_l=ic_l, ft=ft: v.tensor_tensor(
                out=zit[:, ic_l, 512 * ft:512 * (ft + 1)], in0=psF[:, 2 + r, :],
                in1=XT2[:, ic, 512 * ft:512 * (ft + 1)], op=ALU.add))
            em.a(W(CV.h, o_tt[k]))
            na = em.ao(lambda a, k=k, ic_l=ic_l, ft=ft: a.activation(
                out=zsq4[:, k % 2, :], in_=zit[:, ic_l, 512 * ft:512 * (ft + 1)],
                func=AF.Square, bias=0.0, scale=1.0))
            em.vo(lambda v, ic_l=ic_l, ft=ft: v.tensor_reduce(
                out=red24[:, 0:1], in_=zit[:, ic_l, 512 * ft:512 * (ft + 1)],
                axis=AX.X, op=ALU.add))
            em.v(W(CS.h, na))
            em.vo(lambda v, k=k: v.tensor_reduce(
                out=red24[:, 1:2], in_=zsq4[:, k % 2, :], axis=AX.X, op=ALU.add))
            if ft == 0:
                em.vo(lambda v, ic=ic: v.tensor_copy(out=sumB4[:, ic:ic + 1],
                                                     in_=red24[:, 0:1]))
                em.vo(lambda v, ic=ic: v.tensor_copy(out=sqB4[:, ic:ic + 1],
                                                     in_=red24[:, 1:2]))
            else:
                em.vo(lambda v, ic=ic: v.tensor_tensor(
                    out=sumB4[:, ic:ic + 1], in0=sumB4[:, ic:ic + 1],
                    in1=red24[:, 0:1], op=ALU.add))
                em.vo(lambda v, ic=ic: v.tensor_tensor(
                    out=sqB4[:, ic:ic + 1], in0=sqB4[:, ic:ic + 1],
                    in1=red24[:, 1:2], op=ALU.add))
        sl = slice(icg * 4, icg * 4 + 4)
        em.vo(lambda v, sl=sl: v.tensor_scalar(out=sumB4[:, sl], in0=sumB4[:, sl],
                                               scalar1=1.0 / D, scalar2=None,
                                               op0=ALU.mult))
        em.vo(lambda v, sl=sl: v.tensor_scalar(out=sqB4[:, sl], in0=sqB4[:, sl],
                                               scalar1=1.0 / D, scalar2=None,
                                               op0=ALU.mult))
        em.vo(lambda v, sl=sl: v.tensor_tensor(out=scr164[:, sl], in0=sumB4[:, sl],
                                               in1=sumB4[:, sl], op=ALU.mult))
        nvs = em.vo(lambda v, sl=sl: v.tensor_tensor(out=sqB4[:, sl], in0=sqB4[:, sl],
                                                     in1=scr164[:, sl],
                                                     op=ALU.subtract))
        em.a(W(CV.h, nvs))
        na_std = em.ao(lambda a, sl=sl: a.activation(out=sqB4[:, sl], in_=sqB4[:, sl],
                                                     func=AF.Sqrt, bias=cscal[:, 0:1],
                                                     scale=1.0))
        em.v(W(CS.h, na_std))
        em.vo(lambda v, sl=sl: v.reciprocal(out=c14[:, sl], in_=sqB4[:, sl]))
        em.vo(lambda v, sl=sl: v.scalar_tensor_tensor(out=c04[:, sl], in0=sumB4[:, sl],
                                                      scalar=-1.0, in1=c14[:, sl],
                                                      op0=ALU.mult, op1=ALU.mult))
        for ic_l in range(4):
            ic = icg * 4 + ic_l
            if ic >= 1:
                em.v(W(ST.h, st_base + 16 * ic))
            nv = em.vo(lambda v, ic=ic, ic_l=ic_l: v.tensor_scalar(
                out=outc[:, :], in0=zit[:, ic_l, :], scalar1=c14[:, ic:ic + 1],
                scalar2=c04[:, ic:ic + 1], op0=ALU.mult, op1=ALU.add))
            em.s(W(CV.h, nv))
            ST.reserve(16)
            em.s(lambda s, ic=ic: s.dma_start(out=OUT[128 * ic:128 * (ic + 1), :],
                                              in_=outc[:, :]).then_inc(ST.h, 16))
    em.s(W(ST.h, ST.n))
    P.run_block("l1f_scatter", em)
    ctx7.close()
    return None



def _build_program(phase_list, last_flags):
    P = Prog("+".join(n for n, _ in phase_list))
    for (name, emitfn), last in zip(phase_list, last_flags):
        em = emitfn(P, last=last)
        if em is not None:
            P.run_block(name, em)
    fin = P.fence()
    em = Em(P)
    em.s(P.wait_fence_fns(fin))
    P.run_block("drain", em)
    return P


def _collect_inputs(P, inputs, b, carry_b):
    m = {}
    for alloc in P.nc.m.functions[0].allocations:
        if not isinstance(alloc, mybir.MemoryLocationSet) or alloc.kind != "ExternalInput":
            continue
        name = alloc.memorylocations[0].name
        if name == "partition_id":
            continue
        if name in CONSTS:
            m[name] = CONSTS[name]
        elif name == "x":
            m[name] = np.ascontiguousarray(
                np.asarray(inputs["L0_attn_hidden_states"], dtype=np.float32)[b])
        elif name in carry_b:
            m[name] = carry_b[name]
        elif name in inputs:
            m[name] = np.ascontiguousarray(np.asarray(inputs[name], dtype=np.float32))
        else:
            raise KeyError(f"no source for input {name}")
    return m


def _run_programs(programs, inputs):
    carry = [dict() for _ in range(8)]
    for P in programs:
        in_maps = [_collect_inputs(P, inputs, b, carry[b]) for b in range(8)]
        res = run_bass_kernel_spmd(P.nc, in_maps, core_ids=list(range(8)))
        for b in range(8):
            carry[b].update(res.results[b])
    return carry


FUSED = True


def kernel(**inputs):
    if FUSED:
        progs = [_build_program(
            [("l0a", emit_l0a), ("l0f", emit_l0f), ("l1a", emit_l1a),
             ("l1f", emit_l1f)], [False, False, False, True])]
    else:
        progs = [
            _build_program([("l0a", emit_l0a)], [True]),
            _build_program([("l0f", emit_l0f)], [True]),
            _build_program([("l1a", emit_l1a)], [True]),
            _build_program([("l1f", emit_l1f)], [True]),
        ]
    carry = _run_programs(progs, inputs)
    return np.stack([carry[b]["OUT"] for b in range(8)], axis=0)


def dev_run_l0a(inputs):
    P = _build_program([("l0a", emit_l0a)], [True])
    carry = _run_programs([P], inputs)
    return [c["X1_tok"] for c in carry]


# version 10 (final)
# speedup vs baseline: 1.1756x; 1.1756x over previous
import numpy as np
import ml_dtypes
from contextlib import ExitStack

import concourse.bass as bass
import concourse.mybir as mybir
from concourse.bass_utils import run_bass_kernel_spmd

F32 = mybir.dt.float32
BF16 = mybir.dt.bfloat16
AF = mybir.ActivationFunctionType
ALU = mybir.AluOpType
AX = mybir.AxisListType

D = 2048
S = 2048
EPS_LN = 1e-5
EPS_RMS = 1e-6
EPS_Q = 1e-8


def _make_constants():
    c = {}
    c["c_identb"] = np.eye(128, dtype=ml_dtypes.bfloat16)
    c["c_identf"] = np.eye(128, dtype=np.float32)
    c["c_ones_col_b"] = np.ones((128, 1), dtype=ml_dtypes.bfloat16)
    c["c_ones_col_f"] = np.ones((128, 1), dtype=np.float32)
    c["c_ones_row_f"] = np.ones((1, 128), dtype=np.float32)
    c["c_ones_row_b"] = np.ones((1, 128), dtype=ml_dtypes.bfloat16)
    c["c_scal"] = np.tile(np.array([[1e-5, 1e-6, 1e-8, 0.0]], dtype=np.float32),
                          (128, 1))
    c["c_ones128b"] = np.ones((128, 128), dtype=ml_dtypes.bfloat16)
    jj, ii = np.meshgrid(np.arange(128), np.arange(128), indexing="ij")
    c["c_T128b"] = (jj < ii).astype(ml_dtypes.bfloat16)
    c["c_miota"] = np.tile(np.arange(1024, dtype=np.float32)[None, :], (128, 1))
    c["c_mcols8"] = (np.arange(8, dtype=np.float32)[None, :] * 128
                     + np.arange(128, dtype=np.float32)[:, None])
    hd, half = 128, 64
    inv_freq = 1.0 / (10000.0 ** (np.arange(0, half, 2, dtype=np.float64) / half))
    fr = np.arange(S, dtype=np.float64)[:, None] * inv_freq[None, :]
    CE = np.ones((S, hd)); SE = np.zeros((S, hd))
    CE[:, 0:half:2] = np.cos(fr); CE[:, 1:half:2] = np.cos(fr)
    SE[:, 0:half:2] = -np.sin(fr); SE[:, 1:half:2] = np.sin(fr)
    sc = 1.0 / np.sqrt(hd)
    c["c_ceq"] = (CE * sc).astype(np.float32)
    c["c_seq"] = (SE * sc).astype(np.float32)
    c["c_cek"] = CE.astype(np.float32)
    c["c_sek"] = SE.astype(np.float32)
    return c


CONSTS = _make_constants()


class Ctr:

    def __init__(self, handle):
        self.h = handle
        self.n = 0

    def reserve(self, k=1):
        self.n += k
        return self.n


class Em:

    def __init__(self, P):
        self.P = P
        self.L = {k: [] for k in "gstva"}

    def g(self, fn): self.L["g"].append(fn)
    def s(self, fn): self.L["s"].append(fn)
    def t(self, fn): self.L["t"].append(fn)
    def v(self, fn): self.L["v"].append(fn)
    def a(self, fn): self.L["a"].append(fn)

    def _auto(self, lst, ctr, fn):
        n = ctr.reserve()

        def run(e, fn=fn, n=n, h=ctr.h):
            if n > 1:
                e.wait_ge(h, n - 1)
            fn(e).then_inc(h, 1)
        lst.append(run)
        return n

    def vo(self, fn):
        return self._auto(self.L["v"], self.P.sem("CV"), fn)

    def ao(self, fn):
        return self._auto(self.L["a"], self.P.sem("CS"), fn)

    def go(self, fn):
        return self._auto(self.L["g"], self.P.sem("CP"), fn)

    def to(self, fn):
        ctr = self.P.sem("CT")
        n = ctr.reserve()

        def run(e, fn=fn, n=n, h=ctr.h):
            fn(e).then_inc(h, 1)
        self.L["t"].append(run)
        return n


class Prog:
    def __init__(self, name):
        self.nc = bass.Bass("TRN2", target_bir_lowering=False, debug=False)
        self.name = name
        self.ctx = ExitStack()
        self.dram = {}
        self.sems = {}
        self.const_names = set()
        for s in ["CT", "CV", "CS", "CP", "ST"]:
            self.sems[s] = Ctr(self.ctx.enter_context(self.nc.semaphore(s)))

    def sem(self, k):
        if k not in self.sems:
            self.sems[k] = Ctr(self.ctx.enter_context(self.nc.semaphore(k)))
        return self.sems[k]

    def din(self, name, shape, dt):
        if name not in self.dram:
            self.dram[name] = self.nc.dram_tensor(name, list(shape), dt,
                                                  kind="ExternalInput").ap()
        return self.dram[name]

    def dout(self, name, shape, dt):
        self.dram[name] = self.nc.dram_tensor(name, list(shape), dt,
                                              kind="ExternalOutput").ap()
        return self.dram[name]

    def dint(self, name, shape, dt):
        if name not in self.dram:
            self.dram[name] = self.nc.dram_tensor(name, list(shape), dt,
                                                  kind="Internal").ap()
        return self.dram[name]

    def const(self, name):
        arr = CONSTS[name]
        dt = BF16 if arr.dtype == ml_dtypes.bfloat16 else F32
        self.const_names.add(name)
        return self.din(name, arr.shape, dt)

    def fence(self):
        return {k: c.n for k, c in self.sems.items()}

    def run_block(self, name, em):
        with self.nc.Block(name) as b:
            @b.gpsimd
            def _(e):
                for fn in em.L["g"]:
                    fn(e)

            @b.sync
            def _(e):
                for fn in em.L["s"]:
                    fn(e)

            @b.tensor
            def _(e):
                for fn in em.L["t"]:
                    fn(e)

            @b.vector
            def _(e):
                for fn in em.L["v"]:
                    fn(e)

            @b.scalar
            def _(e):
                for fn in em.L["a"]:
                    fn(e)

    def wait_fence_fns(self, f, skip=()):
        items = [(self.sems[k].h, n) for k, n in f.items() if n > 0 and k not in skip]

        def w(eng):
            for h, n in items:
                eng.wait_ge(h, n)
        return w


def W(h, n):
    return lambda e: e.wait_ge(h, n)


def load_const_tiles(P, em, names_tiles, sem_name="aCLC"):
    CL = P.sem(sem_name)
    for cname, tile in names_tiles:
        ap = P.const(cname)
        n = CL.reserve(16)
        em.s(lambda s, ap=ap, tile=tile, CL=CL: s.dma_start(out=tile[:], in_=ap)
             .then_inc(CL.h, 16))
    return CL.n




def emit_l0a(P, last=False):
    nc = P.nc
    x = P.din("x", [S, D], F32)
    wq = P.din("L0_attn_Wq", [64, D], F32)
    wk = P.din("L0_attn_Wk", [64, D], F32)
    wv = P.din("L0_attn_Wv", [D, D], F32)
    wo = P.din("L0_attn_Wo", [D, D], F32)
    X1 = (P.dout if last else P.dint)("X1_tok", [S, D], BF16)

    x0bf = P.dint("x0bf", [S, D], BF16)
    scwv = P.dint("scwv", [D, D], BF16)
    scwo = P.dint("scwo", [D, D], BF16)

    CT, CV, CS, CP, ST = (P.sem(k) for k in ["CT", "CV", "CS", "CP", "ST"])
    CGX = P.sem("aCGX"); CGA = P.sem("aCGA"); CGB = P.sem("aCGB"); CGC = P.sem("aCGC")
    CLC = P.sem("aCLC"); CLX = P.sem("aCLX"); CLW = P.sem("aCLW")
    SL0 = P.sem("aSL0"); SL1 = P.sem("aSL1")
    f0 = P.fence()

    ctx = ExitStack()
    sb = lambda n, shp, dt=F32: ctx.enter_context(nc.sbuf_tensor("l0a_" + n, list(shp), dt))
    pst_ = lambda n, shp, dt=F32: ctx.enter_context(nc.psum_tensor("l0a_" + n, list(shp), dt))

    cidb = sb("cidb", [128, 128], BF16)
    cidf = sb("cidf", [128, 128], F32)
    conesr = sb("conesr", [1, 128], F32)
    conesrb = sb("conesrb", [1, 128], BF16)
    conescb = sb("conescb", [128, 1], BF16)
    conescf = sb("conescf", [128, 1], F32)
    cscal = sb("cscal", [128, 4], F32)

    ARX = sb("ARX", [128, 16, S], BF16)
    XB = ARX
    WOT = ARX
    WqT = sb("WqT", [128, 16, 64], BF16)
    WkT = sb("WkT", [128, 16, 64], BF16)
    qfT = sb("qfT", [64, S], BF16)
    kfT = sb("kfT", [64, S], BF16)
    ARW = sb("ARW", [128, 16, 1024], BF16)

    def wqk_tile(which, c):
        base = 0 if which == 0 else 2
        f0_ = c * 128
        return ARW[0:64, base + f0_ // 1024, (f0_ % 1024):(f0_ % 1024) + 128]
    wq_full = ARW[0:64, 0:2, :]
    wk_full = ARW[0:64, 2:4, :]
    WVT = ARW
    PT = lambda jc: ARW[:, jc, 0:512]
    AVT = lambda dvc: ARW[:, dvc, 512:1024]
    zit = lambda ic_l: ARW[:, 4 * ic_l:4 * (ic_l + 1), 0:512]

    Vb = sb("Vb", [128, 16, D], BF16)
    rowz = sb("rowz", [1, 512])
    rzb = sb("rzb", [128, 512])
    xsl = sb("xsl", [128, 2, 512])
    zsq = sb("zsq", [128, 2, 512], BF16)
    sumB = sb("sumB", [128, 16])
    sqB = sb("sqB", [128, 16])
    red2 = sb("red2", [128, 2])
    c1 = sb("c1", [128, 16])
    c0 = sb("c0", [128, 16])
    scr16 = sb("scr16", [128, 16])
    x1c = sb("x1c", [128, D], BF16)

    psB = pst_("psB", [128, 2, 1024], BF16)
    psA = pst_("psA", [128, 6, 512])

    em = Em(P)
    wf = P.wait_fence_fns(f0)
    for k in "gstva":
        em.L[k].append(wf)

    n_x0 = CGX.reserve(16)
    em.g(lambda g: g.dma_start(out=x0bf, in_=x).then_inc(CGX.h, 16))
    CGA.reserve(16)
    em.g(lambda g: g.dma_start(out=wq_full, in_=wq.rearrange("r (a f) -> r a f", a=2))
         .then_inc(CGA.h, 16))
    n_wqk = CGA.reserve(16)
    em.g(lambda g: g.dma_start(out=wk_full, in_=wk.rearrange("r (a f) -> r a f", a=2))
         .then_inc(CGA.h, 16))
    n_wv = CGB.reserve(16)
    em.g(lambda g: g.dma_start(out=scwv, in_=wv).then_inc(CGB.h, 16))
    n_wo = CGC.reserve(16)
    em.g(lambda g: g.dma_start(out=scwo, in_=wo).then_inc(CGC.h, 16))

    n_const = load_const_tiles(P, em, [
        ("c_identb", cidb), ("c_identf", cidf), ("c_ones_row_f", conesr),
        ("c_ones_row_b", conesrb), ("c_ones_col_b", conescb),
        ("c_ones_col_f", conescf), ("c_scal", cscal)])

    em.s(W(CGX.h, n_x0))
    for c in range(16):
        CLX.reserve(16)
        em.s(lambda s, c=c: s.dma_start(out=XB[:, c, :],
                                        in_=x0bf[0:S, 128 * c:128 * (c + 1)],
                                        transpose=True).then_inc(CLX.h, 16))
    n_xb = CLX.n

    em.t(W(CGA.h, n_wqk))
    em.t(W(CLC.h, n_const))
    evac_t = {}
    for i in range(32):
        c = i // 2
        which = i % 2
        dst = WqT if which == 0 else WkT
        if i >= 2:
            em.t(W(CV.h, evac_t[i - 2]))
        nt = em.to(lambda t, i=i, c=c, which=which: t.transpose(
            psB[:, i % 2, 0:64], wqk_tile(which, c), cidb[0:64, 0:64]))
        em.v(W(CT.h, nt))
        evac_t[i] = em.vo(lambda v, c=c, dst=dst, i=i: v.tensor_copy(
            out=dst[:, c, :], in_=psB[:, i % 2, 0:64]))
    tp_end_t = CT.n
    n_tp_evac = CV.n

    em.t(W(CLX.h, n_xb))
    em.t(W(CV.h, n_tp_evac))
    qk_evac = {}
    for i in range(8):
        mat = WqT if i < 4 else WkT
        dstb = qfT if i < 4 else kfT
        it = i % 4
        r = i % 2
        if i >= 2:
            em.t(W(CV.h, qk_evac[i - 2]))
        for dc in range(15):
            em.t(lambda t, r=r, dc=dc, it=it, mat=mat: t.matmul(
                psA[0:64, r, :], mat[:, dc, :], XB[:, dc, 512 * it:512 * (it + 1)],
                start=(dc == 0), stop=False))
        nt = em.to(lambda t, r=r, it=it, mat=mat: t.matmul(
            psA[0:64, r, :], mat[:, 15, :], XB[:, 15, 512 * it:512 * (it + 1)],
            start=False, stop=True))
        em.v(W(CT.h, nt))
        qk_evac[i] = em.vo(lambda v, r=r, it=it, dstb=dstb: v.tensor_copy(
            out=dstb[0:64, 512 * it:512 * (it + 1)], in_=psA[0:64, r, :]))
    n_qk_v = CV.n


    v_base_t = CT.n
    for h in range(2):
        em.s(W(CGB.h, n_wv))
        em.s(W(CT.h, tp_end_t))
        if h == 1:
            em.s(W(CT.h, v_base_t + 32))
        for c in range(16):
            CLW.reserve(16)
            em.s(lambda s, h=h, c=c: s.dma_start(
                out=WVT[:, c, :], in_=scwv[1024 * h:1024 * (h + 1), 128 * c:128 * (c + 1)],
                transpose=True).then_inc(CLW.h, 16))
        n_wvt_h = CLW.n
        em.t(W(CLW.h, n_wvt_h))
        v_evac = {}
        for jc in range(16):
            for dt_ in range(2):
                i = jc * 2 + dt_
                r = i % 2
                if i >= 2:
                    em.t(W(CV.h, v_evac[i - 2]))
                elif h == 1:
                    em.t(W(CV.h, CV.n))
                else:
                    em.t(W(CV.h, n_qk_v))
                for dc in range(15):
                    em.t(lambda t, r=r, dc=dc, jc=jc, dt_=dt_: t.matmul(
                        psA[:, r, :], XB[:, dc, 128 * jc:128 * (jc + 1)],
                        WVT[:, dc, 512 * dt_:512 * (dt_ + 1)],
                        start=(dc == 0), stop=False))
                nt = em.to(lambda t, r=r, jc=jc, dt_=dt_: t.matmul(
                    psA[:, r, :], XB[:, 15, 128 * jc:128 * (jc + 1)],
                    WVT[:, 15, 512 * dt_:512 * (dt_ + 1)], start=False, stop=True))
                em.v(W(CT.h, nt))
                v_evac[i] = em.vo(lambda v, r=r, jc=jc, h=h, dt_=dt_: v.tensor_copy(
                    out=Vb[:, jc, 1024 * h + 512 * dt_:1024 * h + 512 * (dt_ + 1)],
                    in_=psA[:, r, :]))

    em.s(W(CT.h, CT.n))
    em.s(W(CGC.h, n_wo))
    for c in range(16):
        CLW.reserve(16)
        em.s(lambda s, c=c: s.dma_start(out=WOT[:, c, :],
                                        in_=scwo[0:D, 128 * c:128 * (c + 1)],
                                        transpose=True).then_inc(CLW.h, 16))
    n_wot = CLW.n

    st_base = ST.n
    em.t(W(CLW.h, n_wot))
    for it in range(4):
        em.t(W(CV.h, CV.n))
        em.t(W(CS.h, CS.n))
        em.a(W(CV.h, CV.n))
        exp_n = {}
        for jc in range(16):
            r = jc % 2
            if jc >= 2:
                em.t(W(CS.h, exp_n[jc - 2]))
            nt = em.to(lambda t, r=r, jc=jc, it=it: t.matmul(
                psA[:, r, :], kfT[0:64, 128 * jc:128 * (jc + 1)],
                qfT[0:64, 512 * it:512 * (it + 1)], start=True, stop=True))
            em.a(W(CT.h, nt))
            exp_n[jc] = em.ao(lambda a, r=r, jc=jc: a.activation(
                out=PT(jc), in_=psA[:, r, :], func=AF.Exp, bias=0.0, scale=1.0))
        em.t(W(CS.h, CS.n))
        for jc in range(15):
            em.t(lambda t, jc=jc: t.matmul(psA[0:1, 2, :], conescb[:, :], PT(jc),
                                           start=(jc == 0), stop=False))
        ntz = em.to(lambda t: t.matmul(psA[0:1, 2, :], conescb[:, :], PT(15),
                                       start=False, stop=True))
        em.v(W(CT.h, ntz))
        nv = em.vo(lambda v: v.reciprocal(out=rowz[0:1, :], in_=psA[0:1, 2, :]))
        em.t(W(CV.h, nv))
        ntb = em.to(lambda t: t.matmul(psA[:, 3, :], conesr[0:1, :], rowz[0:1, :],
                                       start=True, stop=True))
        em.v(W(CT.h, ntb))
        nvb = em.vo(lambda v: v.tensor_copy(out=rzb[:, :], in_=psA[:, 3, :]))

        av_evac = {}
        em.t(W(CV.h, nvb))
        for dvc in range(16):
            r = dvc % 2
            if dvc >= 2:
                em.t(W(CV.h, av_evac[dvc - 2]))
            for jc in range(15):
                em.t(lambda t, r=r, dvc=dvc, jc=jc: t.matmul(
                    psA[:, r, :], Vb[:, jc, 128 * dvc:128 * (dvc + 1)], PT(jc),
                    start=(jc == 0), stop=False))
            nt = em.to(lambda t, r=r, dvc=dvc: t.matmul(
                psA[:, r, :], Vb[:, 15, 128 * dvc:128 * (dvc + 1)], PT(15),
                start=False, stop=True))
            em.v(W(CT.h, nt))
            av_evac[dvc] = em.vo(lambda v, r=r, dvc=dvc: v.tensor_tensor(
                out=AVT(dvc), in0=psA[:, r, :], in1=rzb[:, :], op=ALU.mult))

        em.t(W(CV.h, CV.n))
        o_tt = {}
        for k in range(16):
            ic_l = k // 4
            ft = k % 4
            ic = it * 4 + ic_l
            r = k % 2
            SLp = SL0 if k % 2 == 0 else SL1
            if k >= 2:
                em.s(W(CV.h, o_tt[k - 2]))
            elif it > 0:
                em.s(W(CV.h, CV.n))
            nl = SLp.reserve(16)
            em.s(lambda s, k=k, ic=ic, ft=ft, SLp=SLp: s.dma_start(
                out=xsl[:, k % 2, :],
                in_=x[128 * ic:128 * (ic + 1), 512 * ft:512 * (ft + 1)])
                .then_inc(SLp.h, 16))
            if k >= 2:
                em.t(W(CV.h, o_tt[k - 2]))
            for dvc in range(15):
                em.t(lambda t, r=r, ic_l=ic_l, ft=ft, dvc=dvc: t.matmul(
                    psA[:, 2 + r, :], AVT(dvc)[:, 128 * ic_l:128 * (ic_l + 1)],
                    WOT[:, dvc, 512 * ft:512 * (ft + 1)], start=(dvc == 0), stop=False))
            nto = em.to(lambda t, r=r, ic_l=ic_l, ft=ft: t.matmul(
                psA[:, 2 + r, :], AVT(15)[:, 128 * ic_l:128 * (ic_l + 1)],
                WOT[:, 15, 512 * ft:512 * (ft + 1)], start=False, stop=True))
            em.v(W(CT.h, nto))
            em.v(W(SLp.h, nl))
            o_tt[k] = em.vo(lambda v, r=r, k=k, ic_l=ic_l, ft=ft: v.tensor_tensor(
                out=zit(ic_l)[:, ft, :], in0=psA[:, 2 + r, :],
                in1=xsl[:, k % 2, :], op=ALU.add))
            em.a(W(CV.h, o_tt[k]))
            na = em.ao(lambda a, k=k, ic_l=ic_l, ft=ft: a.activation(
                out=zsq[:, k % 2, :], in_=zit(ic_l)[:, ft, :],
                func=AF.Square, bias=0.0, scale=1.0))
            em.vo(lambda v, ic_l=ic_l, ft=ft: v.tensor_reduce(
                out=red2[:, 0:1], in_=zit(ic_l)[:, ft, :], axis=AX.X, op=ALU.add))
            em.v(W(CS.h, na))
            em.vo(lambda v, k=k: v.tensor_reduce(
                out=red2[:, 1:2], in_=zsq[:, k % 2, :], axis=AX.X, op=ALU.add))
            if ft == 0:
                em.vo(lambda v, ic=ic: v.tensor_copy(out=sumB[:, ic:ic + 1],
                                                     in_=red2[:, 0:1]))
                em.vo(lambda v, ic=ic: v.tensor_copy(out=sqB[:, ic:ic + 1],
                                                     in_=red2[:, 1:2]))
            else:
                em.vo(lambda v, ic=ic: v.tensor_tensor(
                    out=sumB[:, ic:ic + 1], in0=sumB[:, ic:ic + 1], in1=red2[:, 0:1],
                    op=ALU.add))
                em.vo(lambda v, ic=ic: v.tensor_tensor(
                    out=sqB[:, ic:ic + 1], in0=sqB[:, ic:ic + 1], in1=red2[:, 1:2],
                    op=ALU.add))

        i0 = it * 4
        sl = slice(i0, i0 + 4)
        em.vo(lambda v, sl=sl: v.tensor_scalar(out=sumB[:, sl], in0=sumB[:, sl],
                                               scalar1=1.0 / D, scalar2=None,
                                               op0=ALU.mult))
        em.vo(lambda v, sl=sl: v.tensor_scalar(out=sqB[:, sl], in0=sqB[:, sl],
                                               scalar1=1.0 / D, scalar2=None,
                                               op0=ALU.mult))
        em.vo(lambda v, sl=sl: v.tensor_tensor(out=scr16[:, sl], in0=sumB[:, sl],
                                               in1=sumB[:, sl], op=ALU.mult))
        nvs = em.vo(lambda v, sl=sl: v.tensor_tensor(out=sqB[:, sl], in0=sqB[:, sl],
                                                     in1=scr16[:, sl], op=ALU.subtract))
        em.a(W(CV.h, nvs))
        na_std = em.ao(lambda a, sl=sl: a.activation(out=sqB[:, sl], in_=sqB[:, sl],
                                                     func=AF.Sqrt, bias=cscal[:, 0:1],
                                                     scale=1.0))
        em.v(W(CS.h, na_std))
        em.vo(lambda v, sl=sl: v.reciprocal(out=c1[:, sl], in_=sqB[:, sl]))
        em.vo(lambda v, sl=sl: v.scalar_tensor_tensor(out=c0[:, sl], in0=sumB[:, sl],
                                                      scalar=-1.0, in1=c1[:, sl],
                                                      op0=ALU.mult, op1=ALU.mult))
        for ic_l in range(4):
            ic = it * 4 + ic_l
            if ic >= 1:
                em.v(W(ST.h, st_base + 16 * ic))
            nv = em.vo(lambda v, ic=ic, ic_l=ic_l: v.tensor_scalar(
                out=x1c[:, :], in0=zit(ic_l)[:, :, :], scalar1=c1[:, ic:ic + 1],
                scalar2=c0[:, ic:ic + 1], op0=ALU.mult, op1=ALU.add))
            em.s(W(CV.h, nv))
            ST.reserve(16)
            em.s(lambda s, ic=ic: s.dma_start(out=X1[128 * ic:128 * (ic + 1), :],
                                              in_=x1c[:, :]).then_inc(ST.h, 16))

    em.s(W(ST.h, ST.n))
    ctx.close()
    return em


def emit_l0f(P, last=False):
    nc = P.nc
    X1 = P.dram.get("X1_tok")
    if X1 is None:
        X1 = P.din("X1_tok", [S, D], BF16)
    wu = P.din("L0_ffn_W_up", [8192, D], F32)
    wg = P.din("L0_ffn_W_gate", [8192, D], F32)
    wd = P.din("L0_ffn_W_down", [D, 8192], F32)
    X2T = (P.dout if last else P.dint)("X2T_fm", [D, S], BF16)

    sc_tu = P.dint("sc_tu", [8192, D], BF16)
    sc_tg = P.dint("sc_tg", [8192, D], BF16)
    sc_td = P.dint("sc_td", [D, 8192], BF16)
    Hd = P.dint("l0f_H", [8192, S], BF16)

    CT, CV, CS, CP, ST = (P.sem(k) for k in ["CT", "CV", "CS", "CP", "ST"])

    phctx = ExitStack()
    sbp = lambda n, shp, dt=F32: phctx.enter_context(nc.sbuf_tensor("l0f_" + n, list(shp), dt))
    XBF = sbp("XBF", [128, 16, S], BF16)
    tauph = sbp("tauph", [128, 4])
    cscal2 = sbp("cscal2", [128, 4])
    conesr = sbp("conesr", [1, 128])
    conescf = sbp("conescf", [128, 1])
    conescb2 = sbp("conescb2", [128, 1], BF16)
    psF = phctx.enter_context(nc.psum_tensor("l0f_psF", [128, 6, 512], F32))

    f0 = P.fence()
    ctx = ExitStack()
    sb = lambda n, shp, dt=F32: ctx.enter_context(nc.sbuf_tensor("l0fq_" + n, list(shp), dt))
    wt4 = sb("wt4", [128, 4, D])
    s1 = sb("s1", [128, 2, D])
    s2 = sb("s2", [128, D])
    qt = sb("qt", [128, 2, D], BF16)
    acc = sb("acc", [128, 1])
    rtmp = sb("rtmp", [128, 1])
    svals = sb("svals", [1, 8])
    taun = sb("taun", [128, 3])

    ext = getattr(P, "aux", {}).get("q_pass1")
    em = Em(P)
    wf = P.wait_fence_fns(f0, skip=("CP", "aQ1A", "aQ1B") if ext else ())
    for k in "gstva":
        em.L[k].append(wf)
    BCLC = P.sem("bCLC")
    n_const = load_const_tiles(P, em, [("c_ones_row_f", conesr), ("c_ones_col_f", conescf),
                                       ("c_scal", cscal2), ("c_ones_col_b", conescb2)],
                               "bCLC")
    SQ = [P.sem("bSQ%d" % i) for i in range(4)]
    SQW0 = P.sem("bSQW0"); SQW1 = P.sem("bSQW1")

    def tile_src(m, ti):
        w = (wu, wg, wd)[m]
        if m < 2:
            return w[128 * ti:128 * (ti + 1), :]
        r, cb = ti // 4, ti % 4
        return w[128 * r:128 * (r + 1), 2048 * cb:2048 * (cb + 1)]

    def tile_dst(m, ti):
        d_ = (sc_tu, sc_tg, sc_td)[m]
        if m < 2:
            return d_[128 * ti:128 * (ti + 1), :]
        r, cb = ti // 4, ti % 4
        return d_[128 * r:128 * (r + 1), 2048 * cb:2048 * (cb + 1)]

    em.v(W(BCLC.h, n_const))
    em.t(W(BCLC.h, n_const))
    em.a(W(BCLC.h, n_const))
    slot_cons = {}
    stq_cons = {}
    s1_cons = {}
    for m in range(3):
        if ext is None:
            for ti in range(64):
                sl = ti % 4
                if sl in slot_cons:
                    em.s(W(*[hn for hn in slot_cons[sl]][0]))
                    if len(slot_cons[sl]) > 1:
                        em.s(W(*slot_cons[sl][1]))
                nl = SQ[sl].reserve(16)
                em.s(lambda s, m=m, ti=ti, sl=sl: s.dma_start(
                    out=wt4[:, sl, :], in_=tile_src(m, ti)).then_inc(SQ[sl].h, 16))
                em.v(W(SQ[sl].h, nl))
                nred = em.vo(lambda v, sl=sl: v.tensor_reduce(
                    out=rtmp[:, 0:1], in_=wt4[:, sl, :], axis=AX.X, op=ALU.add,
                    apply_absolute_value=True))
                slot_cons[sl] = ((CV.h, nred),)
                if ti == 0:
                    em.vo(lambda v: v.tensor_copy(out=acc[:, 0:1], in_=rtmp[:, 0:1]))
                else:
                    em.vo(lambda v: v.tensor_tensor(out=acc[:, 0:1], in0=acc[:, 0:1],
                                                    in1=rtmp[:, 0:1], op=ALU.add))
            em.t(W(CV.h, CV.n))
            nt = em.to(lambda t: t.matmul(
                psF[0:1, 4, 0:1], acc[:, 0:1], conescf[:, 0:1], start=True, stop=True))
            em.v(W(CT.h, nt))
            em.vo(lambda v, m=m: v.tensor_copy(out=svals[0:1, m:m + 1],
                                               in_=psF[0:1, 4, 0:1]))
        else:
            accq, p1cnt = ext
            em.v(W(CP.h, p1cnt[m]))
            em.vo(lambda v, m=m: v.tensor_copy(out=svals[0:1, m:m + 1],
                                               in_=accq[0:1, m:m + 1]))
        em.vo(lambda v, m=m: v.tensor_scalar(
            out=svals[0:1, 3 + m:4 + m], in0=svals[0:1, m:m + 1],
            scalar1=1.0 / (8192.0 * D), scalar2=EPS_Q, op0=ALU.mult, op1=ALU.add))
        nv_tau = em.vo(lambda v, m=m: v.tensor_scalar(
            out=svals[0:1, 3 + m:4 + m], in0=svals[0:1, 3 + m:4 + m],
            scalar1=0.5, scalar2=None, op0=ALU.mult))
        em.t(W(CV.h, nv_tau))
        ntb = em.to(lambda t, m=m: t.matmul(psF[:, 5, 0:1], conesr[0:1, :],
                                            svals[0:1, 3 + m:4 + m],
                                            start=True, stop=True))
        em.v(W(CT.h, ntb))
        em.vo(lambda v, m=m: v.tensor_copy(out=tauph[:, m:m + 1], in_=psF[:, 5, 0:1]))
        nv_taub = em.vo(lambda v, m=m: v.tensor_scalar(
            out=taun[:, m:m + 1], in0=tauph[:, m:m + 1], scalar1=-1.0, scalar2=None,
            op0=ALU.mult))
        em.a(W(CV.h, nv_taub))
        for ti in range(64):
            sl = ti % 4
            if sl in slot_cons:
                for hn in slot_cons[sl]:
                    em.s(W(*hn))
            nl = SQ[sl].reserve(16)
            em.s(lambda s, m=m, ti=ti, sl=sl: s.dma_start(
                out=wt4[:, sl, :], in_=tile_src(m, ti)).then_inc(SQ[sl].h, 16))
            a_sl = ti % 2
            em.a(W(SQ[sl].h, nl))
            if a_sl in s1_cons:
                em.a(W(CV.h, s1_cons[a_sl]))
            na = em.ao(lambda a, m=m, sl=sl, a_sl=a_sl: a.activation(
                out=s1[:, a_sl, :], in_=wt4[:, sl, :], func=AF.Sign,
                bias=taun[:, m:m + 1], scale=1.0))
            em.v(W(SQ[sl].h, nl))
            em.vo(lambda v, m=m, sl=sl: v.tensor_scalar(
                out=s2[:, :], in0=wt4[:, sl, :], scalar1=taun[:, m:m + 1],
                scalar2=2.0, op0=ALU.is_gt, op1=ALU.mult))
            qsl = ti % 2
            SQWp = SQW0 if qsl == 0 else SQW1
            if qsl in stq_cons:
                em.v(W(*stq_cons[qsl]))
            em.v(W(CS.h, na))
            nv = em.vo(lambda v, qsl=qsl, a_sl=a_sl: v.scalar_tensor_tensor(
                out=qt[:, qsl, :], in0=s1[:, a_sl, :], scalar=-1.0, in1=s2[:, :],
                op0=ALU.add, op1=ALU.add))
            s1_cons[a_sl] = nv
            slot_cons[sl] = ((CS.h, na), (CV.h, nv))
            em.g(W(CV.h, nv))
            nst = SQWp.reserve(16)
            em.g(lambda g, m=m, ti=ti, qsl=qsl, SQWp=SQWp: g.dma_start(
                out=tile_dst(m, ti), in_=qt[:, qsl, :]).then_inc(SQWp.h, 16))
            stq_cons[qsl] = (SQWp.h, nst)
    n_sqw0, n_sqw1 = SQW0.n, SQW1.n
    P.run_block("l0f_quant", em)
    ctx.close()

    f1 = P.fence()
    ctx = ExitStack()
    sb = lambda n, shp, dt=F32: ctx.enter_context(nc.sbuf_tensor("l0fu_" + n, list(shp), dt))
    TU = sb("TU", [128, 16, 2, 512], BF16)
    TG = sb("TG", [128, 16, 2, 512], BF16)
    hrow = sb("hrow", [128, 2, S], BF16)
    sil = sb("sil", [128, 2, 512])

    em = Em(P)
    wf = P.wait_fence_fns(f1)
    for k in "gstva":
        em.L[k].append(wf)
    BXF = P.sem("bBXF")
    for c in range(16):
        BXF.reserve(16)
        em.s(lambda s, c=c: s.dma_start(out=XBF[:, c, :],
                                        in_=X1[0:S, 128 * c:128 * (c + 1)],
                                        transpose=True).then_inc(BXF.h, 16))
    n_xbf = BXF.n
    em.t(W(BXF.h, n_xbf))

    SU = [P.sem("bSU%d" % i) for i in range(2)]
    HW0 = P.sem("bHW0"); HW1 = P.sem("bHW1")
    half_cons = {}
    hrow_cons = {}
    sil_cons = {}
    upg_cons = {0: None, 1: None, 2: None, 3: None}
    for su in range(8):
        for half in range(2):
            if (su, half) != (0, 0):
                pass
            if half in half_cons:
                em.s(W(CT.h, half_cons[half]))
            for dc in range(16):
                SU[half].reserve(16)
                em.s(lambda s, su=su, half=half, dc=dc: s.dma_start(
                    out=TU[:, dc, half, :],
                    in_=sc_tu[1024 * su + 512 * half:1024 * su + 512 * (half + 1),
                              128 * dc:128 * (dc + 1)],
                    transpose=True).then_inc(SU[half].h, 16))
                SU[half].reserve(16)
                em.s(lambda s, su=su, half=half, dc=dc: s.dma_start(
                    out=TG[:, dc, half, :],
                    in_=sc_tg[1024 * su + 512 * half:1024 * su + 512 * (half + 1),
                              128 * dc:128 * (dc + 1)],
                    transpose=True).then_inc(SU[half].h, 16))
            n_half = SU[half].n
            em.t(W(SU[half].h, n_half))
            for ei in range(4):
                ech = su * 8 + half * 4 + ei
                eoff = half * 512 + 128 * ei
                for st in range(4):
                    k = (ech * 4 + st)
                    ru = k % 2
                    rg = 2 + k % 2
                    if upg_cons[ru] is not None:
                        em.t(W(*upg_cons[ru]))
                    for dc in range(15):
                        em.t(lambda t, ru=ru, dc=dc, half=half, ei=ei, st=st: t.matmul(
                            psF[:, ru, :], TU[:, dc, half, 128 * ei:128 * (ei + 1)],
                            XBF[:, dc, 512 * st:512 * (st + 1)],
                            start=(dc == 0), stop=False))
                    ntu = em.to(lambda t, ru=ru, half=half, ei=ei, st=st: t.matmul(
                        psF[:, ru, :], TU[:, 15, half, 128 * ei:128 * (ei + 1)],
                        XBF[:, 15, 512 * st:512 * (st + 1)], start=False, stop=True))
                    if upg_cons[rg] is not None:
                        em.t(W(*upg_cons[rg]))
                    for dc in range(15):
                        em.t(lambda t, rg=rg, dc=dc, half=half, ei=ei, st=st: t.matmul(
                            psF[:, rg, :], TG[:, dc, half, 128 * ei:128 * (ei + 1)],
                            XBF[:, dc, 512 * st:512 * (st + 1)],
                            start=(dc == 0), stop=False))
                    ntg = em.to(lambda t, rg=rg, half=half, ei=ei, st=st: t.matmul(
                        psF[:, rg, :], TG[:, 15, half, 128 * ei:128 * (ei + 1)],
                        XBF[:, 15, 512 * st:512 * (st + 1)], start=False, stop=True))
                    ssl = k % 2
                    if ssl in sil_cons:
                        em.a(W(CV.h, sil_cons[ssl]))
                    em.a(W(CT.h, ntu))
                    na = em.ao(lambda a, ru=ru, ssl=ssl: a.activation(
                        out=sil[:, ssl, :], in_=psF[:, ru, :], func=AF.Silu,
                        bias=0.0, scale=tauph[:, 0:1]))
                    upg_cons[ru] = (CS.h, na)
                    em.v(W(CS.h, na))
                    em.v(W(CT.h, ntg))
                    if st == 0 and (ech % 2) in hrow_cons:
                        em.v(W(*hrow_cons[ech % 2]))
                    nv = em.vo(lambda v, rg=rg, ssl=ssl, ech=ech, st=st:
                               v.scalar_tensor_tensor(
                        out=hrow[:, ech % 2, 512 * st:512 * (st + 1)],
                        in0=psF[:, rg, :], scalar=tauph[:, 1:2], in1=sil[:, ssl, :],
                        op0=ALU.mult, op1=ALU.mult))
                    upg_cons[rg] = (CV.h, nv)
                    sil_cons[ssl] = nv
                HWp = HW0 if ech % 2 == 0 else HW1
                em.g(W(CV.h, CV.n))
                nst = HWp.reserve(16)
                em.g(lambda g, ech=ech, HWp=HWp: g.dma_start(
                    out=Hd[128 * ech:128 * (ech + 1), :], in_=hrow[:, ech % 2, :])
                    .then_inc(HWp.h, 16))
                hrow_cons[ech % 2] = (HWp.h, nst)
            half_cons[half] = CT.n
    n_hw0, n_hw1 = HW0.n, HW1.n
    P.run_block("l0f_upgate", em)
    ctx.close()

    f2 = P.fence()
    ctx = ExitStack()
    sb = lambda n, shp, dt=F32: ctx.enter_context(nc.sbuf_tensor("l0fd_" + n, list(shp), dt))
    TD = sb("TD", [128, 64, 512], BF16)
    hsl = sb("hsl", [128, 2, 8, 512], BF16)

    em = Em(P)
    wf = P.wait_fence_fns(f2)
    for k in "gstva":
        em.L[k].append(wf)
    TDC = [P.sem(n) for n in ["aCLC", "aCLX", "aCLW", "aSL0",
                              "aSL1", "bCLC", "bSU0", "bSU1"]]
    HS0 = P.sem("bHS0"); HS1 = P.sem("bHS1")
    hsl_cons = {}
    st3_cnt = {}
    for dgrp in range(4):
        tdc_n = {}
        for g in range(8):
            if dgrp > 0:
                em.a(W(CT.h, st3_cnt[g]))
            for ei in range(8):
                ec = 8 * g + ei
                tdc_n[g] = TDC[g].reserve(16)
                em.a(lambda a, dgrp=dgrp, ec=ec, g=g: a.dma_start(
                    out=TD[:, ec, :],
                    in_=sc_td[512 * dgrp:512 * (dgrp + 1), 128 * ec:128 * (ec + 1)],
                    transpose=True).then_inc(TDC[g].h, 16))
        for st in range(4):
            ev_guard = CV.n
            em.t(W(CV.h, ev_guard))
            for ecg in range(8):
                hslot = ecg % 2
                HSp = HS0 if hslot == 0 else HS1
                if hslot in hsl_cons:
                    em.s(W(CT.h, hsl_cons[hslot]))
                nl = HSp.reserve(16)
                em.s(lambda s, st=st, ecg=ecg, hslot=hslot, HSp=HSp: s.dma_start(
                    out=hsl[:, hslot, :, :],
                    in_=Hd.rearrange("(g p) s -> p g s", p=128)
                    [:, 8 * ecg:8 * (ecg + 1), 512 * st:512 * (st + 1)])
                    .then_inc(HSp.h, 16))
                em.t(W(HSp.h, nl))
                if st == 0:
                    em.t(W(TDC[ecg].h, tdc_n[ecg]))
                for ei in range(8):
                    ec = ecg * 8 + ei
                    for dc4 in range(4):
                        last = (ec == 63)
                        if (dc4 == 3 and ei == 7) or last:
                            nlast = em.to(lambda t, dc4=dc4, ec=ec, hslot=hslot, ei=ei,
                                          last=last: t.matmul(
                                psF[:, dc4, :], TD[:, ec, 128 * dc4:128 * (dc4 + 1)],
                                hsl[:, hslot, ei, :], start=(ec == 0), stop=last))
                            if dc4 == 3:
                                hsl_cons[hslot] = nlast
                        else:
                            em.t(lambda t, dc4=dc4, ec=ec, hslot=hslot, ei=ei,
                                 last=last: t.matmul(
                                psF[:, dc4, :], TD[:, ec, 128 * dc4:128 * (dc4 + 1)],
                                hsl[:, hslot, ei, :], start=(ec == 0), stop=last))
                if st == 3:
                    st3_cnt[ecg] = CT.n
            nt_done = CT.n
            em.v(W(CT.h, nt_done))
            for dc4 in range(4):
                dc = dgrp * 4 + dc4
                em.vo(lambda v, dc4=dc4, dc=dc, st=st: v.scalar_tensor_tensor(
                    out=XBF[:, dc, 512 * st:512 * (st + 1)], in0=psF[:, dc4, :],
                    scalar=tauph[:, 2:3], in1=XBF[:, dc, 512 * st:512 * (st + 1)],
                    op0=ALU.mult, op1=ALU.add))
    P.run_block("l0f_down", em)
    ctx.close()

    f3 = P.fence()
    ctx = ExitStack()
    sb = lambda n, shp, dt=F32: ctx.enter_context(nc.sbuf_tensor("l0fl_" + n, list(shp), dt))
    zsq = sb("zsq", [128, 2, 512], BF16)
    srow = sb("srow", [1, S])
    sqrow = sb("sqrow", [1, S])
    c1b = sb("c1b", [128, S])
    c0b = sb("c0b", [128, S])
    tmpf = sb("tmpf", [128, S])
    x2c = sb("x2c", [128, D], BF16)

    em = Em(P)
    wf = P.wait_fence_fns(f3)
    for k in "gstva":
        em.L[k].append(wf)
    zsq_cons = {}
    for st in range(4):
        if st > 0:
            em.t(W(CV.h, CV.n))
        for dc in range(16):
            ssl = dc % 2
            if ssl in zsq_cons:
                em.a(W(CT.h, zsq_cons[ssl]))
            na = em.ao(lambda a, dc=dc, st=st, ssl=ssl: a.activation(
                out=zsq[:, ssl, :], in_=XBF[:, dc, 512 * st:512 * (st + 1)],
                func=AF.Square, bias=0.0, scale=1.0))
            em.t(W(CS.h, na))
            em.t(lambda t, dc=dc, st=st: t.matmul(
                psF[0:1, 4, :], conescb2[:, 0:1], XBF[:, dc, 512 * st:512 * (st + 1)],
                start=(dc == 0), stop=(dc == 15)))
            ntq = em.to(lambda t, dc=dc, ssl=ssl: t.matmul(
                psF[0:1, 5, :], conescb2[:, 0:1], zsq[:, ssl, :],
                start=(dc == 0), stop=(dc == 15)))
            zsq_cons[ssl] = ntq
        em.v(W(CT.h, CT.n))
        em.vo(lambda v, st=st: v.tensor_copy(out=srow[0:1, 512 * st:512 * (st + 1)],
                                             in_=psF[0:1, 4, :]))
        em.vo(lambda v, st=st: v.tensor_copy(out=sqrow[0:1, 512 * st:512 * (st + 1)],
                                             in_=psF[0:1, 5, :]))
    em.vo(lambda v: v.tensor_scalar(out=srow[0:1, :], in0=srow[0:1, :],
                                    scalar1=1.0 / D, scalar2=None, op0=ALU.mult))
    em.vo(lambda v: v.tensor_scalar(out=sqrow[0:1, :], in0=sqrow[0:1, :],
                                    scalar1=1.0 / D, scalar2=None, op0=ALU.mult))
    em.vo(lambda v: v.tensor_copy(out=tmpf[0:1, 0:S], in_=srow[0:1, :]))
    em.vo(lambda v: v.tensor_tensor(out=tmpf[0:1, 0:S], in0=tmpf[0:1, 0:S],
                                    in1=srow[0:1, :], op=ALU.mult))
    nvv = em.vo(lambda v: v.tensor_tensor(out=sqrow[0:1, :], in0=sqrow[0:1, :],
                                          in1=tmpf[0:1, 0:S], op=ALU.subtract))
    em.a(W(CV.h, nvv))
    na_s = em.ao(lambda a: a.activation(out=sqrow[0:1, :], in_=sqrow[0:1, :],
                                        func=AF.Sqrt, bias=cscal2[0:1, 0:1], scale=1.0))
    em.v(W(CS.h, na_s))
    em.vo(lambda v: v.reciprocal(out=sqrow[0:1, :], in_=sqrow[0:1, :]))
    nvr = em.vo(lambda v: v.scalar_tensor_tensor(out=srow[0:1, :], in0=srow[0:1, :],
                                                 scalar=-1.0, in1=sqrow[0:1, :],
                                                 op0=ALU.mult, op1=ALU.mult))
    em.t(W(CV.h, nvr))
    for st in range(4):
        nt1 = em.to(lambda t, st=st: t.matmul(psF[:, 4, :], conesr[0:1, :],
                                              sqrow[0:1, 512 * st:512 * (st + 1)],
                                              start=True, stop=True))
        nt2 = em.to(lambda t, st=st: t.matmul(psF[:, 5, :], conesr[0:1, :],
                                              srow[0:1, 512 * st:512 * (st + 1)],
                                              start=True, stop=True))
        em.v(W(CT.h, nt2))
        em.vo(lambda v, st=st: v.tensor_copy(out=c1b[:, 512 * st:512 * (st + 1)],
                                             in_=psF[:, 4, :]))
        nvc = em.vo(lambda v, st=st: v.tensor_copy(out=c0b[:, 512 * st:512 * (st + 1)],
                                                   in_=psF[:, 5, :]))
        em.t(W(CV.h, nvc))
    st_base = ST.n
    for dc in range(16):
        em.vo(lambda v, dc=dc: v.tensor_tensor(out=tmpf[:, :], in0=XBF[:, dc, :],
                                               in1=c1b[:, :], op=ALU.mult))
        if dc >= 1:
            em.v(W(ST.h, st_base + 16 * dc))
        nv = em.vo(lambda v, dc=dc: v.tensor_tensor(out=x2c[:, :], in0=tmpf[:, :],
                                                    in1=c0b[:, :], op=ALU.add))
        em.s(W(CV.h, nv))
        ST.reserve(16)
        em.s(lambda s, dc=dc: s.dma_start(out=X2T[128 * dc:128 * (dc + 1), :],
                                          in_=x2c[:, :]).then_inc(ST.h, 16))
    em.s(W(ST.h, ST.n))
    P.run_block("l0f_ln2", em)
    ctx.close()
    phctx.close()
    return None


def emit_l1a(P, last=False):
    nc = P.nc
    X2T = P.dram.get("X2T_fm")
    if X2T is None:
        X2T = P.din("X2T_fm", [D, S], BF16)
    wq = P.din("L1_attn_Wq", [D, D], F32)
    wk = P.din("L1_attn_Wk", [D, 512], F32)
    wv = P.din("L1_attn_Wv", [D, 512], F32)
    wo = P.din("L1_attn_Wo", [D, D], F32)
    X3 = (P.dout if last else P.dint)("X3_tok", [S, D], BF16)
    QR = P.dint("l1a_QR", [S, D], BF16)
    KR = P.dint("l1a_KR", [S, 512], BF16)

    CT, CV, CS, CP, ST = (P.sem(k) for k in ["CT", "CV", "CS", "CP", "ST"])

    phctx = ExitStack()
    sbp = lambda n, shp, dt=F32: phctx.enter_context(nc.sbuf_tensor("l1a_" + n, list(shp), dt))
    psF = phctx.enter_context(nc.psum_tensor("l1a_psF", [128, 8, 512], F32))
    conesr = sbp("conesr", [1, 128])
    conesrb = sbp("conesrb", [1, 128], BF16)
    conescb = sbp("conescb", [128, 1], BF16)
    cscal = sbp("cscal", [128, 4])

    abctx = ExitStack()
    VB = abctx.enter_context(nc.sbuf_tensor("l1a_VB", [128, 16, 512], BF16))

    f0 = P.fence()
    ctxA = ExitStack()
    sbA = lambda n, shp, dt=F32: ctxA.enter_context(nc.sbuf_tensor("l1aA_" + n, list(shp), dt))
    XB = sbA("XB", [128, 16, S], BF16)
    CEq = sbA("CEq", [128, 16, 128])
    SEq = sbA("SEq", [128, 16, 128])
    CEk = CEq
    SEk = SEq
    qt1 = sbA("qt1", [128, S])
    qsw = sbA("qsw", [128, S])
    qrc = sbA("qrc", [128, 2, S], BF16)

    em = Em(P)
    wf = P.wait_fence_fns(f0)
    for k in "gstva":
        em.L[k].append(wf)
    CLC = P.sem("cCLC")
    n_const = load_const_tiles(P, em, [("c_ones_row_f", conesr), ("c_ones_row_b", conesrb),
                                       ("c_ones_col_b", conescb), ("c_scal", cscal)], "cCLC")
    CXB = P.sem("cCXB")
    for c in range(16):
        CXB.reserve(16)
        em.s(lambda s, c=c: s.dma_start(out=XB[:, c, :],
                                        in_=X2T[128 * c:128 * (c + 1), :])
             .then_inc(CXB.h, 16))
    n_xb = CXB.n
    CTB = P.sem("cCTB")
    for nm, tile, cn in [("ceq", CEq, "c_ceq"), ("seq", SEq, "c_seq")]:
        ap = P.const(cn)
        CTB.reserve(16)
        em.s(lambda s, ap=ap, tile=tile: s.dma_start(
            out=tile[:], in_=ap.rearrange("(c p) f -> p c f", p=128)).then_inc(CTB.h, 16))
    n_tb = CTB.n

    ctxA2 = ExitStack()
    WqB = ctxA2.enter_context(nc.sbuf_tensor("l1aA_WqB", [128, 16, S], BF16))
    CWQ = P.sem("cCWQ")
    n_wq = CWQ.reserve(16)
    em.g(lambda g: g.dma_start(out=WqB[:], in_=wq.rearrange("(c p) f -> p c f", p=128))
         .then_inc(CWQ.h, 16))
    em.t(W(CWQ.h, n_wq))
    em.t(W(CXB.h, n_xb))
    em.t(W(CLC.h, n_const))
    QW0 = P.sem("cQW0"); QW1 = P.sem("cQW1")
    qst_cons = {}
    rope_done = {}
    for ic in range(16):
        par = ic % 2
        bb = par * 4
        if par in rope_done:
            em.t(W(CV.h, rope_done[par]))
        for ft in range(4):
            for dc in range(15):
                em.t(lambda t, ic=ic, ft=ft, dc=dc, bb=bb: t.matmul(
                    psF[:, bb + ft, :], XB[:, dc, 128 * ic:128 * (ic + 1)],
                    WqB[:, dc, 512 * ft:512 * (ft + 1)], start=(dc == 0), stop=False))
            nt = em.to(lambda t, ic=ic, ft=ft, bb=bb: t.matmul(
                psF[:, bb + ft, :], XB[:, 15, 128 * ic:128 * (ic + 1)],
                WqB[:, 15, 512 * ft:512 * (ft + 1)], start=False, stop=True))
        em.v(W(CT.h, nt))
        em.v(W(CTB.h, n_tb))
        pview = psF[:, bb:bb + 4, :].rearrange("p a b -> p (a b)").rearrange(
            "p (h f) -> p h f", h=16)
        pv4 = psF[:, bb:bb + 4, :].rearrange("p a b -> p (a b)").rearrange(
            "p (h k w) -> p h k w", k=64, w=2)
        sw4 = qsw[:].rearrange("p (h k w) -> p h k w", k=64, w=2)
        em.vo(lambda v, pv4=pv4, ic=ic: v.tensor_copy(out=sw4[:, :, :, 0],
                                                      in_=pv4[:, :, :, 1]))
        em.vo(lambda v, pv4=pv4, ic=ic: v.tensor_copy(out=sw4[:, :, :, 1],
                                                      in_=pv4[:, :, :, 0]))
        nv_qt1 = em.vo(lambda v, pview=pview, ic=ic: v.tensor_tensor(
            out=qt1[:].rearrange("p (h f) -> p h f", h=16), in0=pview,
            in1=CEq[:, ic:ic + 1, :].to_broadcast([128, 16, 128]), op=ALU.mult))
        rope_done[par] = nv_qt1
        em.vo(lambda v, ic=ic: v.tensor_tensor(
            out=qsw[:].rearrange("p (h f) -> p h f", h=16),
            in0=qsw[:].rearrange("p (h f) -> p h f", h=16),
            in1=SEq[:, ic:ic + 1, :].to_broadcast([128, 16, 128]), op=ALU.mult))
        qsl = ic % 2
        if qsl in qst_cons:
            em.v(W(*qst_cons[qsl]))
        nv = em.vo(lambda v, qsl=qsl: v.tensor_tensor(
            out=qrc[:, qsl, :], in0=qt1[:], in1=qsw[:], op=ALU.add))
        em.s(W(CV.h, nv))
        QWp = QW0 if qsl == 0 else QW1
        nst = QWp.reserve(16)
        em.s(lambda s, ic=ic, qsl=qsl, QWp=QWp: s.dma_start(
            out=QR[128 * ic:128 * (ic + 1), :], in_=qrc[:, qsl, :]).then_inc(QWp.h, 16))
        qst_cons[qsl] = (QWp.h, nst)
    P.run_block("l1a_q", em)
    ctxA2.close()

    f1 = P.fence()
    ctxA3 = ExitStack()
    WkB = ctxA3.enter_context(nc.sbuf_tensor("l1aA_WkB", [128, 16, 512], BF16))
    WvB = ctxA3.enter_context(nc.sbuf_tensor("l1aA_WvB", [128, 16, 512], BF16))
    em = Em(P)
    wf = P.wait_fence_fns(f1)
    for k in "gstva":
        em.L[k].append(wf)
    CWK = P.sem("cCWK")
    CWV = P.sem("cCWV")
    n_wk = CWK.reserve(16)
    em.g(lambda g: g.dma_start(out=WkB[:], in_=wk.rearrange("(c p) f -> p c f", p=128))
         .then_inc(CWK.h, 16))
    n_wv = CWV.reserve(16)
    em.g(lambda g: g.dma_start(out=WvB[:], in_=wv.rearrange("(c p) f -> p c f", p=128))
         .then_inc(CWV.h, 16))
    for nm, tile, cn in [("cek", CEk, "c_cek"), ("sek", SEk, "c_sek")]:
        ap = P.const(cn)
        CTB.reserve(16)
        em.s(lambda s, ap=ap, tile=tile: s.dma_start(
            out=tile[:], in_=ap.rearrange("(c p) f -> p c f", p=128)).then_inc(CTB.h, 16))
    n_tb2 = CTB.n
    em.v(W(CTB.h, n_tb2))
    em.t(W(CWK.h, n_wk))
    em.t(W(CWV.h, n_wv))
    kst_cons = {}
    kv_done = {}
    for ic in range(16):
        par = ic % 2
        bk = 4 + par * 2
        bv = 5 + par * 2
        if par in kv_done:
            em.t(W(CV.h, kv_done[par]))
        for dc in range(15):
            em.t(lambda t, ic=ic, dc=dc, bk=bk: t.matmul(
                psF[:, bk, :], XB[:, dc, 128 * ic:128 * (ic + 1)], WkB[:, dc, :],
                start=(dc == 0), stop=False))
        ntk = em.to(lambda t, ic=ic, bk=bk: t.matmul(
            psF[:, bk, :], XB[:, 15, 128 * ic:128 * (ic + 1)], WkB[:, 15, :],
            start=False, stop=True))
        for dc in range(15):
            em.t(lambda t, ic=ic, dc=dc, bv=bv: t.matmul(
                psF[:, bv, :], XB[:, dc, 128 * ic:128 * (ic + 1)], WvB[:, dc, :],
                start=(dc == 0), stop=False))
        ntv = em.to(lambda t, ic=ic, bv=bv: t.matmul(
            psF[:, bv, :], XB[:, 15, 128 * ic:128 * (ic + 1)], WvB[:, 15, :],
            start=False, stop=True))
        em.v(W(CT.h, ntk))
        pk = psF[:, bk, :].rearrange("p (h f) -> p h f", h=4)
        pk4 = psF[:, bk, :].rearrange("p (h k w) -> p h k w", k=64, w=2)
        skw = qsw[:, 0:512].rearrange("p (h k w) -> p h k w", k=64, w=2)
        em.vo(lambda v, pk4=pk4: v.tensor_copy(out=skw[:, :, :, 0], in_=pk4[:, :, :, 1]))
        em.vo(lambda v, pk4=pk4: v.tensor_copy(out=skw[:, :, :, 1], in_=pk4[:, :, :, 0]))
        nv_kt1 = em.vo(lambda v, pk=pk, ic=ic: v.tensor_tensor(
            out=qt1[:, 0:512].rearrange("p (h f) -> p h f", h=4), in0=pk,
            in1=CEk[:, ic:ic + 1, :].to_broadcast([128, 4, 128]), op=ALU.mult))
        em.vo(lambda v, ic=ic: v.tensor_tensor(
            out=qsw[:, 0:512].rearrange("p (h f) -> p h f", h=4),
            in0=qsw[:, 0:512].rearrange("p (h f) -> p h f", h=4),
            in1=SEk[:, ic:ic + 1, :].to_broadcast([128, 4, 128]), op=ALU.mult))
        ksl = ic % 2
        if ksl in kst_cons:
            em.v(W(*kst_cons[ksl]))
        nvk = em.vo(lambda v, ksl=ksl: v.tensor_tensor(
            out=qrc[:, ksl, 0:512], in0=qt1[:, 0:512], in1=qsw[:, 0:512], op=ALU.add))
        em.s(W(CV.h, nvk))
        KWp = QW0 if ksl == 0 else QW1
        nstk = KWp.reserve(16)
        em.s(lambda s, ic=ic, ksl=ksl, KWp=KWp: s.dma_start(
            out=KR[128 * ic:128 * (ic + 1), :], in_=qrc[:, ksl, 0:512])
            .then_inc(KWp.h, 16))
        kst_cons[ksl] = (KWp.h, nstk)
        em.v(W(CT.h, ntv))
        kv_done[par] = em.vo(lambda v, ic=ic, bv=bv: v.tensor_copy(
            out=VB[:, ic, :], in_=psF[:, bv, :]))
    P.run_block("l1a_kv", em)
    ctxA3.close()
    ctxA.close()

    f2 = P.fence()
    ctxB1 = ExitStack()
    sbB = lambda n, shp, dt=F32: ctxB1.enter_context(nc.sbuf_tensor("l1aB_" + n, list(shp), dt))
    KT = sbB("KT", [128, 4, S], BF16)
    ATT = sbB("ATT", [128, 16, S], BF16)
    ctxB2 = ExitStack()
    sbB2 = lambda n, shp, dt=F32: ctxB2.enter_context(nc.sbuf_tensor("l1aB2_" + n, list(shp), dt))
    qh = sbB2("qh", [128, 2, S], BF16)
    PTb = sbB2("PTb", [128, 16, 512], BF16)
    rowz = sbB2("rowz", [1, 512])
    rzb = sbB2("rzb", [128, 512])

    em = Em(P)
    wf = P.wait_fence_fns(f2)
    for k in "gstva":
        em.L[k].append(wf)
    CKT = P.sem("cCKT")
    for kh in range(4):
        CKT.reserve(16)
        em.s(lambda s, kh=kh: s.dma_start(out=KT[:, kh, :],
                                          in_=KR[0:S, 128 * kh:128 * (kh + 1)],
                                          transpose=True).then_inc(CKT.h, 16))
    n_kt = CKT.n
    em.t(W(CKT.h, n_kt))

    CQH0 = P.sem("cCQH0"); CQH1 = P.sem("cCQH1")
    qh_cons = {}
    evac_n = {}
    recip_prev = None
    for h in range(16):
        kh = h // 4
        hsl = h % 2
        QHp = CQH0 if hsl == 0 else CQH1
        if hsl in qh_cons:
            em.s(W(CT.h, qh_cons[hsl]))
        nqh = QHp.reserve(16)
        em.s(lambda s, h=h, hsl=hsl, QHp=QHp: s.dma_start(
            out=qh[:, hsl, :], in_=QR[0:S, 128 * h:128 * (h + 1)], transpose=True)
            .then_inc(QHp.h, 16))
        em.t(W(QHp.h, nqh))
        for it in range(4):
            exp_n = {}
            for jc in range(16):
                r = jc % 2
                if jc >= 2:
                    em.t(W(CS.h, exp_n[jc - 2]))
                nt = em.to(lambda t, r=r, jc=jc, it=it, hsl=hsl, kh=kh: t.matmul(
                    psF[:, r, :], KT[:, kh, 128 * jc:128 * (jc + 1)],
                    qh[:, hsl, 512 * it:512 * (it + 1)], start=True, stop=True))
                em.a(W(CT.h, nt))
                exp_n[jc] = em.ao(lambda a, r=r, jc=jc: a.activation(
                    out=PTb[:, jc, :], in_=psF[:, r, :], func=AF.Exp,
                    bias=0.0, scale=1.0))
            if recip_prev is not None:
                em.t(W(CV.h, recip_prev))
            for jc in range(15):
                em.t(W(CS.h, exp_n[jc]))
                em.t(lambda t, jc=jc: t.matmul(psF[0:1, 4, :], conescb[:, 0:1],
                                               PTb[:, jc, :], start=(jc == 0),
                                               stop=False))
            em.t(W(CS.h, exp_n[15]))
            ntz = em.to(lambda t: t.matmul(psF[0:1, 4, :], conescb[:, 0:1],
                                           PTb[:, 15, :], start=False, stop=True))
            em.v(W(CT.h, ntz))
            nv_recip = em.vo(lambda v: v.reciprocal(out=rowz[0:1, :],
                                                    in_=psF[0:1, 4, :]))
            recip_prev = nv_recip
            par = it % 2
            if par in evac_n:
                em.t(W(CV.h, evac_n[par]))
            for jc in range(15):
                em.t(lambda t, jc=jc, kh=kh, par=par: t.matmul(
                    psF[:, 2 + par, :], VB[:, jc, 128 * kh:128 * (kh + 1)],
                    PTb[:, jc, :], start=(jc == 0), stop=False))
            nta = em.to(lambda t, kh=kh, par=par: t.matmul(
                psF[:, 2 + par, :], VB[:, 15, 128 * kh:128 * (kh + 1)],
                PTb[:, 15, :], start=False, stop=True))
            em.t(W(CV.h, nv_recip))
            ntb = em.to(lambda t: t.matmul(psF[:, 5, :], conesr[0:1, :],
                                           rowz[0:1, :], start=True, stop=True))
            em.v(W(CT.h, ntb))
            em.vo(lambda v: v.tensor_copy(out=rzb[:, :], in_=psF[:, 5, :]))
            em.v(W(CT.h, nta))
            evac_n[par] = em.vo(lambda v, h=h, it=it, par=par: v.tensor_tensor(
                out=ATT[:, h, 512 * it:512 * (it + 1)], in0=psF[:, 2 + par, :],
                in1=rzb[:, :], op=ALU.mult))
        qh_cons[hsl] = CT.n
    P.run_block("l1a_attn", em)
    ctxB2.close()

    f3 = P.fence()
    ctxC = ExitStack()
    sbC = lambda n, shp, dt=F32: ctxC.enter_context(nc.sbuf_tensor("l1aC_" + n, list(shp), dt))
    WoB = sbC("WoB", [128, 16, S], BF16)
    xsl = sbC("xsl", [128, 2, 512], BF16)
    zsq = sbC("zsq", [128, 2, 512], BF16)
    sumB = sbC("sumB", [128, 16])
    sqB = sbC("sqB", [128, 16])
    red2 = sbC("red2", [128, 2])
    c1 = sbC("c1", [128, 16])
    c0 = sbC("c0", [128, 16])
    scr16 = sbC("scr16", [128, 16])
    x3c = sbC("x3c", [128, D], BF16)
    ARZ = sbC("ARZ", [128, 4, D], BF16)

    em = Em(P)
    wf = P.wait_fence_fns(f3)
    for k in "gstva":
        em.L[k].append(wf)
    CWO = P.sem("cCWO")
    n_wo = CWO.reserve(16)
    em.g(lambda g: g.dma_start(out=WoB[:], in_=wo.rearrange("(c p) f -> p c f", p=128))
         .then_inc(CWO.h, 16))
    em.t(W(CWO.h, n_wo))
    SL0 = P.sem("cSL0"); SL1 = P.sem("cSL1")
    st_base = ST.n
    o_tt = {}
    for icg in range(4):
        for k in range(16):
            ic_l = k // 4
            ft = k % 4
            ic = icg * 4 + ic_l
            r = k % 2
            SLp = SL0 if k % 2 == 0 else SL1
            if k >= 2:
                em.s(W(CV.h, o_tt[k - 2]))
            elif icg > 0:
                em.s(W(CV.h, CV.n))
            nl = SLp.reserve(16)
            em.s(lambda s, ic=ic, ft=ft, SLp=SLp, k=k: s.dma_start(
                out=xsl[:, k % 2, :],
                in_=X2T[512 * ft:512 * (ft + 1), 128 * ic:128 * (ic + 1)],
                transpose=True).then_inc(SLp.h, 16))
            if k >= 2:
                em.t(W(CV.h, o_tt[k - 2]))
            elif icg > 0:
                em.t(W(CV.h, CV.n))
            for hh in range(15):
                em.t(lambda t, r=r, ic=ic, ft=ft, hh=hh: t.matmul(
                    psF[:, 2 + r, :], ATT[:, hh, 128 * ic:128 * (ic + 1)],
                    WoB[:, hh, 512 * ft:512 * (ft + 1)], start=(hh == 0), stop=False))
            nto = em.to(lambda t, r=r, ic=ic, ft=ft: t.matmul(
                psF[:, 2 + r, :], ATT[:, 15, 128 * ic:128 * (ic + 1)],
                WoB[:, 15, 512 * ft:512 * (ft + 1)], start=False, stop=True))
            em.v(W(CT.h, nto))
            em.v(W(SLp.h, nl))
            o_tt[k] = em.vo(lambda v, r=r, ic_l=ic_l, ft=ft, k=k: v.tensor_tensor(
                out=ARZ[:, ic_l, 512 * ft:512 * (ft + 1)], in0=psF[:, 2 + r, :],
                in1=xsl[:, k % 2, :], op=ALU.add))
            em.a(W(CV.h, o_tt[k]))
            na = em.ao(lambda a, k=k, ic_l=ic_l, ft=ft: a.activation(
                out=zsq[:, k % 2, :], in_=ARZ[:, ic_l, 512 * ft:512 * (ft + 1)],
                func=AF.Square, bias=0.0, scale=1.0))
            em.vo(lambda v, ic_l=ic_l, ft=ft: v.tensor_reduce(
                out=red2[:, 0:1], in_=ARZ[:, ic_l, 512 * ft:512 * (ft + 1)],
                axis=AX.X, op=ALU.add))
            em.v(W(CS.h, na))
            em.vo(lambda v, k=k: v.tensor_reduce(
                out=red2[:, 1:2], in_=zsq[:, k % 2, :], axis=AX.X, op=ALU.add))
            if ft == 0:
                em.vo(lambda v, ic=ic: v.tensor_copy(out=sumB[:, ic:ic + 1],
                                                     in_=red2[:, 0:1]))
                em.vo(lambda v, ic=ic: v.tensor_copy(out=sqB[:, ic:ic + 1],
                                                     in_=red2[:, 1:2]))
            else:
                em.vo(lambda v, ic=ic: v.tensor_tensor(
                    out=sumB[:, ic:ic + 1], in0=sumB[:, ic:ic + 1], in1=red2[:, 0:1],
                    op=ALU.add))
                em.vo(lambda v, ic=ic: v.tensor_tensor(
                    out=sqB[:, ic:ic + 1], in0=sqB[:, ic:ic + 1], in1=red2[:, 1:2],
                    op=ALU.add))
        sl = slice(icg * 4, icg * 4 + 4)
        em.vo(lambda v, sl=sl: v.tensor_scalar(out=sumB[:, sl], in0=sumB[:, sl],
                                               scalar1=1.0 / D, scalar2=None,
                                               op0=ALU.mult))
        em.vo(lambda v, sl=sl: v.tensor_scalar(out=sqB[:, sl], in0=sqB[:, sl],
                                               scalar1=1.0 / D, scalar2=None,
                                               op0=ALU.mult))
        em.vo(lambda v, sl=sl: v.tensor_tensor(out=scr16[:, sl], in0=sumB[:, sl],
                                               in1=sumB[:, sl], op=ALU.mult))
        nvs = em.vo(lambda v, sl=sl: v.tensor_tensor(out=sqB[:, sl], in0=sqB[:, sl],
                                                     in1=scr16[:, sl],
                                                     op=ALU.subtract))
        em.a(W(CV.h, nvs))
        na_std = em.ao(lambda a, sl=sl: a.activation(out=sqB[:, sl], in_=sqB[:, sl],
                                                     func=AF.Sqrt, bias=cscal[:, 0:1],
                                                     scale=1.0))
        em.v(W(CS.h, na_std))
        em.vo(lambda v, sl=sl: v.reciprocal(out=c1[:, sl], in_=sqB[:, sl]))
        em.vo(lambda v, sl=sl: v.scalar_tensor_tensor(out=c0[:, sl], in0=sumB[:, sl],
                                                      scalar=-1.0, in1=c1[:, sl],
                                                      op0=ALU.mult, op1=ALU.mult))
        for ic_l in range(4):
            ic = icg * 4 + ic_l
            if ic >= 1:
                em.v(W(ST.h, st_base + 16 * ic))
            nv = em.vo(lambda v, ic=ic, ic_l=ic_l: v.tensor_scalar(
                out=x3c[:, :], in0=ARZ[:, ic_l, :], scalar1=c1[:, ic:ic + 1],
                scalar2=c0[:, ic:ic + 1], op0=ALU.mult, op1=ALU.add))
            em.s(W(CV.h, nv))
            ST.reserve(16)
            em.s(lambda s, ic=ic: s.dma_start(out=X3[128 * ic:128 * (ic + 1), :],
                                              in_=x3c[:, :]).then_inc(ST.h, 16))
    em.s(W(ST.h, ST.n))
    P.run_block("l1a_wo", em)
    ctxC.close()
    ctxB1.close()
    abctx.close()
    phctx.close()
    return None


L1F_PARTS = [6]


def emit_l1f(P, last=True):
    nc = P.nc
    X3 = P.dram.get("X3_tok")
    if X3 is None:
        X3 = P.din("X3_tok", [S, D], BF16)
    mask = P.din("L1_ffn_mask", [1024, 1024], F32)
    freqs = P.din("L1_ffn_freqs_cis", [1024, 64, 2], F32)
    wrouter = P.din("L1_ffn_w_router", [D], F32)
    g1 = P.din("L1_ffn_g1", [D], F32)
    g2 = P.din("L1_ffn_g2", [D], F32)
    wqI = P.din("L1_ffn_wq", [D, D], F32)
    wkI = P.din("L1_ffn_wk", [D, D], F32)
    wvI = P.din("L1_ffn_wv", [D, D], F32)
    woI = P.din("L1_ffn_wo", [D, D], F32)
    w1I = P.din("L1_ffn_w1", [D, 8192], F32)
    w3I = P.din("L1_ffn_w3", [D, 8192], F32)
    w2I = P.din("L1_ffn_w2", [8192, D], F32)
    OUT = P.dout("OUT", [S, D], F32)

    M = 1024
    scmask = P.dint("l1f_maskb", [1024, 1024], BF16)
    XG_d = P.dint("l1f_XG", [M, D], BF16)
    H1_d = P.dint("l1f_H1", [M, D], BF16)
    QI_d = P.dint("l1f_QI", [M, D], BF16)
    KI_d = P.dint("l1f_KI", [M, D], BF16)
    Z2_d = P.dint("l1f_Z2", [M, D], BF16)
    H2_d = P.dint("l1f_H2", [M, D], BF16)
    HZ_d = P.dint("l1f_HZ", [8192, M], BF16)
    Z3T_d = P.dint("l1f_Z3T", [D, M], BF16)
    ATT_d = P.dint("l1f_ATT", [D, M], BF16)

    def ATTd_rows(h):
        return ATT_d[128 * h:128 * (h + 1), :]
    row_d = P.dint("l1f_rows", [4, 16, 128], F32)

    CT, CV, CS, CP, ST = (P.sem(k) for k in ["CT", "CV", "CS", "CP", "ST"])

    phctx = ExitStack()
    sbp = lambda n, shp, dt=F32: phctx.enter_context(nc.sbuf_tensor("l1f_" + n, list(shp), dt))
    psF = phctx.enter_context(nc.psum_tensor("l1f_psF", [128, 8, 512], F32))
    conesr = sbp("conesr", [1, 128])
    conesrb = sbp("conesrb", [1, 128], BF16)
    conescb = sbp("conescb", [128, 1], BF16)
    conescf = sbp("conescf", [128, 1])
    cidb = sbp("cidb", [128, 128], BF16)
    cidf = sbp("cidf", [128, 128], F32)
    cscal = sbp("cscal", [128, 4])
    cmcols = sbp("cmcols", [128, 8])
    T128 = sbp("T128", [128, 128], BF16)
    ones128 = sbp("ones128", [128, 128], BF16)
    miota = sbp("miota", [128, 1024])
    selmat = sbp("selmat", [128, 16])
    cummat = sbp("cummat", [128, 16])
    Lmat = sbp("Lmat", [128, 16])
    Lmatb = sbp("Lmatb", [128, 16], BF16)
    rsmat = sbp("rsmat", [128, 8])
    rw = sbp("rw", [128, 8])
    t16 = sbp("t16", [16, 128])
    ssc = sbp("ssc", [128, 16])

    f0 = P.fence()
    ctx1 = ExitStack()
    sb1 = lambda n, shp, dt=F32: ctx1.enter_context(nc.sbuf_tensor("l1f1_" + n, list(shp), dt))
    XTOK = sb1("XTOK", [128, 16, D], BF16)
    bcb = sb1("bcb", [128, D])
    scr = sb1("scr", [128, D])
    wrow = sb1("wrow", [1, D])
    lrow = sb1("lrow", [1, D])
    PT = sb1("PT", [128, 16, 1024], BF16)
    xgv = sb1("xgv", [128, 2, 512], BF16)

    em = Em(P)
    wf = P.wait_fence_fns(f0)
    for k in "gstva":
        em.L[k].append(wf)
    DCLC = P.sem("dCLC")
    n_const = load_const_tiles(P, em, [
        ("c_ones_row_f", conesr), ("c_ones_row_b", conesrb), ("c_ones_col_b", conescb),
        ("c_ones_col_f", conescf), ("c_identb", cidb), ("c_identf", cidf),
        ("c_scal", cscal), ("c_mcols8", cmcols), ("c_T128b", T128),
        ("c_ones128b", ones128), ("c_miota", miota)], "dCLC")
    DXT = P.sem("dDXT")
    for c in range(16):
        DXT.reserve(16)
        em.s(lambda s, c=c: s.dma_start(out=XTOK[:, c, :],
                                        in_=X3[128 * c:128 * (c + 1), :])
             .then_inc(DXT.h, 16))
    n_xt = DXT.n
    DWR = P.sem("dDWR")
    n_wr = DWR.reserve(16)
    em.s(lambda s: s.dma_start(out=wrow[:], in_=wrouter.rearrange("(a d) -> a d", a=1))
         .then_inc(DWR.h, 16))
    DMSK = P.sem("dDMSK")
    n_mk = DMSK.reserve(16)
    em.g(lambda g: g.dma_start(out=scmask, in_=mask).then_inc(DMSK.h, 16))

    em.t(W(DWR.h, n_wr))
    em.t(W(DCLC.h, n_const))
    for ft in range(4):
        if ft > 0:
            em.t(W(CV.h, CV.n))
        nt = em.to(lambda t, ft=ft: t.matmul(psF[:, 4, :], conesr[0:1, :],
                                             wrow[0:1, 512 * ft:512 * (ft + 1)],
                                             start=True, stop=True))
        em.v(W(CT.h, nt))
        em.vo(lambda v, ft=ft: v.tensor_copy(out=bcb[:, 512 * ft:512 * (ft + 1)],
                                             in_=psF[:, 4, :]))
    em.v(W(DXT.h, n_xt))
    for ic in range(16):
        em.vo(lambda v, ic=ic: v.tensor_tensor(out=scr[:, :], in0=XTOK[:, ic, :],
                                               in1=bcb[:, :], op=ALU.mult))
        em.vo(lambda v, ic=ic: v.tensor_reduce(out=Lmat[:, ic:ic + 1], in_=scr[:, :],
                                               axis=AX.X, op=ALU.add))
    nvL = em.vo(lambda v: v.tensor_copy(out=Lmatb[:, :], in_=Lmat[:, :]))
    em.t(W(CV.h, nvL))
    ntt = em.to(lambda t: t.matmul(psF[0:16, 4, 0:128], Lmat[:, :], cidf[:, :],
                                   start=True, stop=True))
    em.v(W(CT.h, ntt))
    nvt = em.vo(lambda v: v.tensor_copy(out=t16[:, :], in_=psF[0:16, 4, 0:128]))
    em.s(W(CV.h, nvt))
    DRW = P.sem("dDRW")
    n_r1 = DRW.reserve(16)
    em.s(lambda s: s.dma_start(out=row_d[0], in_=t16[:, :]).then_inc(DRW.h, 16))
    em.s(W(DRW.h, n_r1))
    n_r2 = DRW.reserve(16)
    em.s(lambda s: s.dma_start(out=lrow[:], in_=row_d.rearrange("a b c -> a (b c)")[0:1, :])
         .then_inc(DRW.h, 16))
    em.t(W(DRW.h, n_r2))
    for ft in range(4):
        em.t(W(CV.h, CV.n))
        nt = em.to(lambda t, ft=ft: t.matmul(psF[:, 4, :], conesr[0:1, :],
                                             lrow[0:1, 512 * ft:512 * (ft + 1)],
                                             start=True, stop=True))
        em.v(W(CT.h, nt))
        em.vo(lambda v, ft=ft: v.tensor_copy(out=bcb[:, 512 * ft:512 * (ft + 1)],
                                             in_=psF[:, 4, :]))
    for ic in range(16):
        em.vo(lambda v, ic=ic: v.tensor_scalar(out=scr[:, :], in0=bcb[:, :],
                                               scalar1=Lmat[:, ic:ic + 1], scalar2=None,
                                               op0=ALU.is_gt))
        em.vo(lambda v, ic=ic: v.tensor_reduce(out=ssc[:, ic:ic + 1], in_=scr[:, :],
                                               axis=AX.X, op=ALU.add))
    nv_sel = em.vo(lambda v: v.tensor_scalar(out=selmat[:, :], in0=ssc[:, :],
                                             scalar1=float(M), scalar2=None,
                                             op0=ALU.is_lt))
    nv_selb = em.vo(lambda v: v.tensor_copy(out=ssc[:, :], in_=selmat[:, :]))
    nv_selb2 = em.vo(lambda v: v.tensor_copy(out=PT[:, 0, 0:16], in_=selmat[:, :]))
    em.t(W(CV.h, nv_selb2))
    for I in range(16):
        for J in range(I):
            em.t(lambda t, I=I, J=J: t.matmul(psF[:, 5, 0:1], ones128[:, :],
                                              PT[:, 0, J:J + 1], start=(J == 0),
                                              stop=False))
        ntc = em.to(lambda t, I=I: t.matmul(psF[:, 5, 0:1], T128[:, :],
                                            PT[:, 0, I:I + 1], start=(I == 0 and True)
                                            if I == 0 else False, stop=True))
        em.v(W(CT.h, ntc))
        nvc = em.vo(lambda v, I=I: v.tensor_copy(out=cummat[:, I:I + 1],
                                                 in_=psF[:, 5, 0:1]))
        em.t(W(CV.h, nvc))
    for ic in range(16):
        em.vo(lambda v, ic=ic: v.tensor_scalar(
            out=PT[:, ic, :], in0=miota[:, :], scalar1=cummat[:, ic:ic + 1],
            scalar2=selmat[:, ic:ic + 1], op0=ALU.is_equal, op1=ALU.mult))
    nv_pt = CV.n
    em.t(W(CV.h, nv_pt))
    xg_st = {}
    GW0 = P.sem("dGW0"); GW1 = P.sem("dGW1")
    for mc in range(8):
        for dt_ in range(4):
            kk = mc * 4 + dt_
            r = kk % 2
            if kk >= 2:
                em.t(W(CV.h, xg_st[kk - 2][2]))
            for ic in range(15):
                em.t(lambda t, mc=mc, dt_=dt_, ic=ic, r=r: t.matmul(
                    psF[:, r, :], PT[:, ic, 128 * mc:128 * (mc + 1)],
                    XTOK[:, ic, 512 * dt_:512 * (dt_ + 1)], start=(ic == 0), stop=False))
            nt = em.to(lambda t, mc=mc, dt_=dt_, r=r: t.matmul(
                psF[:, r, :], PT[:, 15, 128 * mc:128 * (mc + 1)],
                XTOK[:, 15, 512 * dt_:512 * (dt_ + 1)], start=False, stop=True))
            em.v(W(CT.h, nt))
            if kk >= 2:
                em.v(W(*xg_st[kk - 2][0:2]))
            nv = em.vo(lambda v, r=r, kk=kk: v.tensor_copy(out=xgv[:, kk % 2, :],
                                                           in_=psF[:, r, :]))
            em.s(W(CV.h, nv))
            GWp = GW0 if kk % 2 == 0 else GW1
            nst = GWp.reserve(16)
            em.s(lambda s, mc=mc, dt_=dt_, kk=kk, GWp=GWp: s.dma_start(
                out=XG_d[128 * mc:128 * (mc + 1), 512 * dt_:512 * (dt_ + 1)],
                in_=xgv[:, kk % 2, :]).then_inc(GWp.h, 16))
            xg_st[kk] = (GWp.h, nst, nv)
    for mc in range(8):
        for ic in range(15):
            em.t(lambda t, mc=mc, ic=ic: t.matmul(
                psF[:, 4, 0:1], PT[:, ic, 128 * mc:128 * (mc + 1)], Lmatb[:, ic:ic + 1],
                start=(ic == 0), stop=False))
        ntr = em.to(lambda t, mc=mc: t.matmul(
            psF[:, 4, 0:1], PT[:, 15, 128 * mc:128 * (mc + 1)], Lmatb[:, 15:16],
            start=False, stop=True))
        em.v(W(CT.h, ntr))
        nvr = em.vo(lambda v, mc=mc: v.tensor_copy(out=rsmat[:, mc:mc + 1],
                                                   in_=psF[:, 4, 0:1]))
        em.t(W(CV.h, nvr))
    em.vo(lambda v: v.tensor_reduce(out=ssc[:, 0:1], in_=rsmat[:, :], axis=AX.X,
                                    op=ALU.max))
    nvg = em.vo(lambda v: v.tensor_copy(out=ssc[:, 1:2], in_=ssc[:, 0:1]))
    em.t(W(CV.h, nvg))
    ntg = em.to(lambda t: t.matmul(psF[0:1, 4, 0:128], ssc[:, 1:2], cidf[:, :],
                                   start=True, stop=True))
    em.v(W(CT.h, ntg))
    em.vo(lambda v: v.tensor_reduce(out=ssc[0:1, 2:3], in_=psF[0:1, 4, 0:128],
                                    axis=AX.X, op=ALU.max))
    nvn = em.vo(lambda v: v.tensor_scalar(out=ssc[0:1, 3:4], in0=ssc[0:1, 2:3],
                                          scalar1=-1.0, scalar2=None, op0=ALU.mult))
    em.t(W(CV.h, nvn))
    ntgb = em.to(lambda t: t.matmul(psF[:, 4, 0:1], conesr[0:1, :], ssc[0:1, 3:4],
                                    start=True, stop=True))
    em.v(W(CT.h, ntgb))
    nvb = em.vo(lambda v: v.tensor_copy(out=ssc[:, 4:5], in_=psF[:, 4, 0:1]))
    em.a(W(CV.h, nvb))
    nae = em.ao(lambda a: a.activation(out=rsmat[:, :], in_=rsmat[:, :], func=AF.Exp,
                                       bias=ssc[:, 4:5], scale=1.0,
                                       accum_out=ssc[:, 5:6]))
    em.t(W(CS.h, nae))
    ntz = em.to(lambda t: t.matmul(psF[0:1, 4, 0:1], ssc[:, 5:6], conescf[:, 0:1],
                                   start=True, stop=True))
    em.v(W(CT.h, ntz))
    em.vo(lambda v: v.reciprocal(out=ssc[0:1, 6:7], in_=psF[0:1, 4, 0:1]))
    nvz = em.vo(lambda v: v.tensor_copy(out=ssc[0:1, 7:8], in_=ssc[0:1, 6:7]))
    em.t(W(CV.h, nvz))
    ntzb = em.to(lambda t: t.matmul(psF[:, 4, 0:1], conesr[0:1, :], ssc[0:1, 7:8],
                                    start=True, stop=True))
    em.v(W(CT.h, ntzb))
    em.vo(lambda v: v.tensor_copy(out=ssc[:, 8:9] if False else rw[:, 0:1],
                                  in_=psF[:, 4, 0:1]))
    nv_rz = em.vo(lambda v: v.tensor_scalar(out=rw[:, :], in0=rsmat[:, :],
                                            scalar1=rw[:, 0:1], scalar2=None,
                                            op0=ALU.mult))
    em.vo(lambda v: v.tensor_scalar(out=ssc[:, :], in0=cummat[:, :], scalar1=1.0,
                                    scalar2=None, op0=ALU.add))
    em.vo(lambda v: v.tensor_tensor(out=ssc[:, :], in0=ssc[:, :], in1=selmat[:, :],
                                    op=ALU.mult))
    nv_cp = em.vo(lambda v: v.tensor_scalar(out=ssc[:, :], in0=ssc[:, :], scalar1=-1.0,
                                            scalar2=None, op0=ALU.add))
    em.t(W(CV.h, nv_cp))
    ntt2 = em.to(lambda t: t.matmul(psF[0:16, 4, 0:128], ssc[:, :], cidf[:, :],
                                    start=True, stop=True))
    em.v(W(CT.h, ntt2))
    nvt2 = em.vo(lambda v: v.tensor_copy(out=t16[:, :], in_=psF[0:16, 4, 0:128]))
    em.s(W(CV.h, nvt2))
    n_r3 = DRW.reserve(16)
    em.s(lambda s: s.dma_start(out=row_d[1], in_=t16[:, :]).then_inc(DRW.h, 16))
    P.run_block("l1f_sel", em)
    ctx1.close()

    if L1F_PARTS[0] < 2:
        return None
    ctxVI = ExitStack()
    VI = ctxVI.enter_context(nc.sbuf_tensor("l1f_VI", [128, 8, D], BF16))

    f1 = P.fence()
    ctx2 = ExitStack()
    sb2 = lambda n, shp, dt=F32: ctx2.enter_context(nc.sbuf_tensor("l1f2_" + n, list(shp), dt))
    WB = sb2("WB", [128, 16, D], BF16)
    H1T = sb2("H1T", [128, 16, M], BF16)
    CI = sb2("CI", [128, 8, 128])
    SI = sb2("SI", [128, 8, 128])
    FR = sb2("FR", [128, 8, 128])
    g1b = sb2("g1b", [128, D], BF16)
    xgs = sb2("xgs", [128, 2, D], BF16)
    sqs = sb2("sqs", [128, D], BF16)
    inv1 = sb2("inv1", [128, 2])
    h1c = sb2("h1c", [128, 2, D], BF16)
    qt1 = sb2("qt1", [128, D])
    qsw = sb2("qsw", [128, D], BF16)
    qrc = sb2("qrc", [128, 2, D], BF16)
    grow = sb2("grow", [1, D], BF16)

    em = Em(P)
    wf = P.wait_fence_fns(f1)
    for k in "gstva":
        em.L[k].append(wf)
    DG1 = P.sem("dDG1")
    n_g1 = DG1.reserve(16)
    em.g(lambda g_: g_.dma_start(out=grow[:], in_=g1.rearrange("(a d) -> a d", a=1))
         .then_inc(DG1.h, 16))
    DFR = P.sem("dDFR")
    n_fr = DFR.reserve(16)
    em.s(lambda s: s.dma_start(out=FR[:], in_=freqs.rearrange("(c p) k w -> p c (k w)",
                                                             p=128)).then_inc(DFR.h, 16))
    em.v(W(DFR.h, n_fr))
    FRv = FR[:].rearrange("p c (k w) -> p c k w", w=2)
    em.vo(lambda v: v.tensor_copy(out=CI[:, :, 0:64], in_=FRv[:, :, :, 0]))
    em.vo(lambda v: v.tensor_copy(out=CI[:, :, 64:128], in_=FRv[:, :, :, 0]))
    em.vo(lambda v: v.tensor_scalar(out=SI[:, :, 0:64], in0=FRv[:, :, :, 1],
                                    scalar1=-1.0, scalar2=None, op0=ALU.mult))
    em.vo(lambda v: v.tensor_copy(out=SI[:, :, 64:128], in_=FRv[:, :, :, 1]))
    rsc = 1.0 / float(np.sqrt(128.0))
    em.t(W(DG1.h, n_g1))
    for ft in range(4):
        em.t(W(CV.h, CV.n))
        nt = em.to(lambda t, ft=ft: t.matmul(psF[:, 4, :], conesrb[0:1, :],
                                             grow[0:1, 512 * ft:512 * (ft + 1)],
                                             start=True, stop=True))
        em.v(W(CT.h, nt))
        em.vo(lambda v, ft=ft: v.tensor_copy(out=g1b[:, 512 * ft:512 * (ft + 1)],
                                             in_=psF[:, 4, :]))
    DXG = [P.sem("dDXG%d" % i) for i in range(2)]
    DH1 = [P.sem("dDH1%d" % i) for i in range(2)]
    xg_cons = {}
    h1_cons = {}
    for mc in range(8):
        sl = mc % 2
        if sl in xg_cons:
            em.s(W(CV.h, xg_cons[sl]))
        nl = DXG[sl].reserve(16)
        em.s(lambda s, mc=mc, sl=sl: s.dma_start(
            out=xgs[:, sl, :], in_=XG_d[128 * mc:128 * (mc + 1), :])
            .then_inc(DXG[sl].h, 16))
        em.a(W(DXG[sl].h, nl))
        em.a(W(CV.h, CV.n))
        nsq = em.ao(lambda a, sl=sl: a.activation(out=sqs[:, :], in_=xgs[:, sl, :],
                                                  func=AF.Square, bias=0.0, scale=1.0))
        em.v(W(DXG[sl].h, nl))
        em.v(W(CS.h, nsq))
        em.vo(lambda v, sl=sl: v.tensor_reduce(out=inv1[:, sl:sl + 1], in_=sqs[:, :],
                                               axis=AX.X, op=ALU.add))
        nvv = em.vo(lambda v, sl=sl: v.tensor_scalar(out=inv1[:, sl:sl + 1],
                                                     in0=inv1[:, sl:sl + 1],
                                                     scalar1=1.0 / D, scalar2=None,
                                                     op0=ALU.mult))
        em.a(W(CV.h, nvv))
        na = em.ao(lambda a, sl=sl: a.activation(out=inv1[:, sl:sl + 1],
                                                 in_=inv1[:, sl:sl + 1], func=AF.Sqrt,
                                                 bias=cscal[:, 1:2], scale=1.0))
        em.v(W(CS.h, na))
        em.vo(lambda v, sl=sl: v.reciprocal(out=inv1[:, sl:sl + 1],
                                            in_=inv1[:, sl:sl + 1]))
        if sl in h1_cons:
            em.v(W(*h1_cons[sl]))
        nvh = em.vo(lambda v, sl=sl: v.scalar_tensor_tensor(
            out=h1c[:, sl, :], in0=xgs[:, sl, :], scalar=inv1[:, sl:sl + 1],
            in1=g1b[:, :], op0=ALU.mult, op1=ALU.mult))
        xg_cons[sl] = nvh
        em.s(W(CV.h, nvh))
        nst = DH1[sl].reserve(16)
        em.s(lambda s, mc=mc, sl=sl: s.dma_start(
            out=H1_d[128 * mc:128 * (mc + 1), :], in_=h1c[:, sl, :])
            .then_inc(DH1[sl].h, 16))
        h1_cons[sl] = (DH1[sl].h, nst)
    DH1T = P.sem("dDH1T")
    em.s(W(DH1[0].h, DH1[0].n))
    em.s(W(DH1[1].h, DH1[1].n))
    for dc in range(16):
        DH1T.reserve(16)
        em.s(lambda s, dc=dc: s.dma_start(out=H1T[:, dc, :],
                                          in_=H1_d[0:M, 128 * dc:128 * (dc + 1)],
                                          transpose=True).then_inc(DH1T.h, 16))
    n_h1t = DH1T.n

    DWB = P.sem("dDWB")
    QIW0 = P.sem("dQIW0"); QIW1 = P.sem("dQIW1")
    qst = {}
    qkv_done = {}
    em.t(W(DH1T.h, n_h1t))
    for wi, (wsrc, kind) in enumerate([(wqI, "q"), (wkI, "k"), (wvI, "v")]):
        if wi > 0:
            em.g(W(CT.h, CT.n))
        n_w = DWB.reserve(16)
        em.g(lambda g, wsrc=wsrc: g.dma_start(
            out=WB[:], in_=wsrc.rearrange("(c p) f -> p c f", p=128)).then_inc(DWB.h, 16))
        em.t(W(DWB.h, n_w))
        for mc in range(8):
            par = (wi * 8 + mc) % 2
            bb = par * 4
            if par in qkv_done:
                em.t(W(CV.h, qkv_done[par]))
            for ft in range(4):
                for dc in range(15):
                    em.t(lambda t, mc=mc, ft=ft, dc=dc, bb=bb: t.matmul(
                        psF[:, bb + ft, :], H1T[:, dc, 128 * mc:128 * (mc + 1)],
                        WB[:, dc, 512 * ft:512 * (ft + 1)], start=(dc == 0), stop=False))
                nt = em.to(lambda t, mc=mc, ft=ft, bb=bb: t.matmul(
                    psF[:, bb + ft, :], H1T[:, 15, 128 * mc:128 * (mc + 1)],
                    WB[:, 15, 512 * ft:512 * (ft + 1)], start=False, stop=True))
            em.v(W(CT.h, nt))
            if kind == "v":
                qkv_done[par] = em.vo(lambda v, mc=mc, bb=bb: v.tensor_copy(
                    out=VI[:, mc, :],
                    in_=psF[:, bb:bb + 4, :].rearrange("p a b -> p (a b)")))
            else:
                qs_ = rsc if kind == "q" else 1.0
                pview = psF[:, bb:bb + 4, :].rearrange("p a b -> p (a b)").rearrange(
                    "p (h f) -> p h f", h=16)
                pv2 = psF[:, bb:bb + 4, :].rearrange("p a b -> p (a b)").rearrange(
                    "p (h a f) -> p h a f", a=2, f=64)
                sw2 = qsw[:].rearrange("p (h a f) -> p h a f", a=2, f=64)
                em.vo(lambda v, pv2=pv2, qs_=qs_: v.tensor_scalar(
                    out=sw2[:, :, 0, :], in0=pv2[:, :, 1, :], scalar1=qs_,
                    scalar2=None, op0=ALU.mult))
                em.vo(lambda v, pv2=pv2, qs_=qs_: v.tensor_scalar(
                    out=sw2[:, :, 1, :], in0=pv2[:, :, 0, :], scalar1=qs_,
                    scalar2=None, op0=ALU.mult))
                qkv_done[par] = em.vo(lambda v, mc=mc, pview=pview, qs_=qs_:
                    v.scalar_tensor_tensor(
                        out=qt1[:].rearrange("p (h f) -> p h f", h=16), in0=pview,
                        scalar=qs_,
                        in1=CI[:, mc:mc + 1, :].to_broadcast([128, 16, 128]),
                        op0=ALU.mult, op1=ALU.mult))
                em.vo(lambda v, mc=mc: v.tensor_tensor(
                    out=qsw[:].rearrange("p (h f) -> p h f", h=16),
                    in0=qsw[:].rearrange("p (h f) -> p h f", h=16),
                    in1=SI[:, mc:mc + 1, :].to_broadcast([128, 16, 128]), op=ALU.mult))
                qsl = mc % 2
                if qsl in qst:
                    em.v(W(*qst[qsl]))
                nv = em.vo(lambda v, qsl=qsl: v.tensor_tensor(
                    out=qrc[:, qsl, :], in0=qt1[:], in1=qsw[:], op=ALU.add))
                em.s(W(CV.h, nv))
                QWp = QIW0 if qsl == 0 else QIW1
                nst = QWp.reserve(16)
                dst = QI_d if kind == "q" else KI_d
                em.s(lambda s, mc=mc, qsl=qsl, QWp=QWp, dst=dst: s.dma_start(
                    out=dst[128 * mc:128 * (mc + 1), :], in_=qrc[:, qsl, :])
                    .then_inc(QWp.h, 16))
                qst[qsl] = (QWp.h, nst)
    P.run_block("l1f_qkv", em)
    ctx2.close()

    if L1F_PARTS[0] < 3:
        return None
    f2 = P.fence()
    ctx3 = ExitStack()
    sb3 = lambda n, shp, dt=F32: ctx3.enter_context(nc.sbuf_tensor("l1f3_" + n, list(shp), dt))
    MT = sb3("MT", [128, 8, M], BF16)
    MB = sb3("MB", [128, 2, M], BF16)
    PTI = sb3("PTI", [128, 8, 512], BF16)
    qh = sb3("qh", [128, 2, M], BF16)
    kh = sb3("kh", [128, 2, M], BF16)
    mm8 = sb3("mm8", [128, 8])
    t16b = sb3("t16b", [16, 128])
    mrow = sb3("mrow", [1, M])
    mrowb = sb3("mrowb", [128, M], BF16)
    rowz = sb3("rowz", [1, 512])
    rzb = sb3("rzb", [128, 512])
    attc = sb3("attc", [128, 2, M], BF16)

    em = Em(P)
    wf = P.wait_fence_fns(f2)
    for k in "gstva":
        em.L[k].append(wf)
    DMT = P.sem("dDMT")
    DMB = [P.sem("dDMB%d" % i) for i in range(2)]
    for jc in range(8):
        DMT.reserve(16)
        em.s(lambda s, jc=jc: s.dma_start(out=MT[:, jc, :],
                                          in_=scmask[0:M, 128 * jc:128 * (jc + 1)],
                                          transpose=True).then_inc(DMT.h, 16))
    n_mt = DMT.n
    mb_cons = {}
    for ic in range(8):
        sl = ic % 2
        if sl in mb_cons:
            em.s(W(CV.h, mb_cons[sl]))
        nl = DMB[sl].reserve(16)
        em.s(lambda s, ic=ic, sl=sl: s.dma_start(
            out=MB[:, sl, :], in_=scmask[128 * ic:128 * (ic + 1), :])
            .then_inc(DMB[sl].h, 16))
        em.v(W(DMB[sl].h, nl))
        mb_cons[sl] = em.vo(lambda v, ic=ic, sl=sl: v.tensor_reduce(
            out=mm8[:, ic:ic + 1], in_=MB[:, sl, :], axis=AX.X, op=ALU.max))
    nvm = CV.n
    em.t(W(CV.h, nvm))
    ntm = em.to(lambda t: t.matmul(psF[0:8, 4, 0:128], mm8[:, :], cidf[:, :],
                                   start=True, stop=True))
    em.v(W(CT.h, ntm))
    nvt = em.vo(lambda v: v.tensor_copy(out=t16b[0:8, :], in_=psF[0:8, 4, 0:128]))
    em.s(W(CV.h, nvt))
    DRW2 = P.sem("dDRW2")
    n_m1 = DRW2.reserve(16)
    em.s(lambda s: s.dma_start(out=row_d[2, 0:8, :], in_=t16b[0:8, :]).then_inc(DRW2.h, 16))
    em.s(W(DRW2.h, n_m1))
    n_m2 = DRW2.reserve(16)
    em.s(lambda s: s.dma_start(out=mrow[:],
                               in_=row_d.rearrange("a b c -> a (b c)")[2:3, 0:M])
         .then_inc(DRW2.h, 16))
    em.v(W(DRW2.h, n_m2))
    em.t(W(DRW2.h, n_m2))
    for half in range(2):
        ntb2 = em.to(lambda t, half=half: t.matmul(
            psF[:, 4 + half, :], conesr[0:1, :], mrow[0:1, 512 * half:512 * (half + 1)],
            start=True, stop=True))
        em.v(W(CT.h, ntb2))
        em.vo(lambda v, half=half: v.tensor_copy(
            out=mrowb[:, 512 * half:512 * (half + 1)], in_=psF[:, 4 + half, :]))
    em.v(W(DMT.h, n_mt))
    for jc in range(8):
        em.vo(lambda v, jc=jc: v.tensor_tensor(
            out=MT[:, jc, :], in0=MT[:, jc, :], in1=mrowb[:, :], op=ALU.subtract))
    nv_mtp = CV.n

    DQH = [P.sem("dDQH%d" % i) for i in range(2)]
    DKH = [P.sem("dDKH%d" % i) for i in range(2)]
    AW = [P.sem("dAW%d" % i) for i in range(2)]
    qh_cons = {}
    att_cons = {}
    evac_n = {}
    recip_prev = nv_mtp
    em.t(W(DMT.h, n_mt))
    em.t(W(CV.h, nv_mtp))
    for h in range(16):
        hsl = h % 2
        if hsl in qh_cons:
            em.s(W(CT.h, qh_cons[hsl]))
        nqh = DQH[hsl].reserve(16)
        em.s(lambda s, h=h, hsl=hsl: s.dma_start(
            out=qh[:, hsl, :], in_=QI_d[0:M, 128 * h:128 * (h + 1)], transpose=True)
            .then_inc(DQH[hsl].h, 16))
        nkh = DKH[hsl].reserve(16)
        em.s(lambda s, h=h, hsl=hsl: s.dma_start(
            out=kh[:, hsl, :], in_=KI_d[0:M, 128 * h:128 * (h + 1)], transpose=True)
            .then_inc(DKH[hsl].h, 16))
        em.t(W(DQH[hsl].h, nqh))
        em.t(W(DKH[hsl].h, nkh))
        for it in range(2):
            exp_n = {}
            for jc in range(8):
                r = jc % 2
                if jc >= 2:
                    em.t(W(CS.h, exp_n[jc - 2]))
                em.t(lambda t, r=r, jc=jc, it=it, hsl=hsl: t.matmul(
                    psF[:, r, :], kh[:, hsl, 128 * jc:128 * (jc + 1)],
                    qh[:, hsl, 512 * it:512 * (it + 1)], start=True, stop=False))
                nt = em.to(lambda t, r=r, jc=jc, it=it: t.matmul(
                    psF[:, r, :], cidb[:, :], MT[:, jc, 512 * it:512 * (it + 1)],
                    start=False, stop=True))
                em.a(W(CT.h, nt))
                exp_n[jc] = em.ao(lambda a, r=r, jc=jc: a.activation(
                    out=PTI[:, jc, :], in_=psF[:, r, :], func=AF.Exp,
                    bias=0.0, scale=1.0))
            em.t(W(CV.h, recip_prev))
            for jc in range(7):
                em.t(W(CS.h, exp_n[jc]))
                em.t(lambda t, jc=jc: t.matmul(psF[0:1, 4, :], conescb[:, 0:1],
                                               PTI[:, jc, :], start=(jc == 0),
                                               stop=False))
            em.t(W(CS.h, exp_n[7]))
            ntz = em.to(lambda t: t.matmul(psF[0:1, 4, :], conescb[:, 0:1],
                                           PTI[:, 7, :], start=False, stop=True))
            em.v(W(CT.h, ntz))
            nv_recip = em.vo(lambda v: v.reciprocal(out=rowz[0:1, :],
                                                    in_=psF[0:1, 4, :]))
            recip_prev = nv_recip
            if it in evac_n:
                em.t(W(CV.h, evac_n[it]))
            for jc in range(7):
                em.t(lambda t, jc=jc, it=it, h=h: t.matmul(
                    psF[:, 2 + it, :], VI[:, jc, 128 * h:128 * (h + 1)],
                    PTI[:, jc, :], start=(jc == 0), stop=False))
            nta = em.to(lambda t, it=it, h=h: t.matmul(
                psF[:, 2 + it, :], VI[:, 7, 128 * h:128 * (h + 1)],
                PTI[:, 7, :], start=False, stop=True))
            em.t(W(CV.h, nv_recip))
            ntb = em.to(lambda t: t.matmul(psF[:, 5, :], conesr[0:1, :],
                                           rowz[0:1, :], start=True, stop=True))
            em.v(W(CT.h, ntb))
            em.vo(lambda v: v.tensor_copy(out=rzb[:, :], in_=psF[:, 5, :]))
            em.v(W(CT.h, nta))
            if it == 0 and hsl in att_cons:
                em.v(W(*att_cons[hsl]))
            evac_n[it] = em.vo(lambda v, it=it, hsl=hsl: v.tensor_tensor(
                out=attc[:, hsl, 512 * it:512 * (it + 1)], in0=psF[:, 2 + it, :],
                in1=rzb[:, :], op=ALU.mult))
        nv_att = CV.n
        em.s(W(CV.h, nv_att))
        AWp = AW[hsl]
        nsta = AWp.reserve(16)
        em.s(lambda s, h=h, hsl=hsl, AWp=AWp: s.dma_start(
            out=ATTd_rows(h), in_=attc[:, hsl, :])
            .then_inc(AWp.h, 16))
        att_cons[hsl] = (AWp.h, nsta)
        qh_cons[hsl] = CT.n
    P.run_block("l1f_attn", em)
    ctx3.close()
    ctxVI.close()

    if L1F_PARTS[0] < 4:
        return None
    f3 = P.fence()
    ctx4 = ExitStack()
    sb4 = lambda n, shp, dt=F32: ctx4.enter_context(nc.sbuf_tensor("l1f4_" + n, list(shp), dt))
    WB2 = sb4("WB2", [128, 16, D], BF16)
    ATs = sb4("ATs", [128, 2, M], BF16)
    xgs2 = sb4("xgs2", [128, 2, D], BF16)
    z2c = sb4("z2c", [128, 2, D], BF16)
    g2b = sb4("g2b", [128, D], BF16)
    grow2 = sb4("grow2", [1, D], BF16)
    sqs2 = sb4("sqs2", [128, D], BF16)
    inv2 = sb4("inv2", [128, 2])
    h2c = sb4("h2c", [128, 2, D], BF16)

    em = Em(P)
    wf = P.wait_fence_fns(f3)
    for k in "gstva":
        em.L[k].append(wf)
    DWB2 = P.sem("dDWB2")
    n_w = DWB2.reserve(16)
    em.g(lambda g_: g_.dma_start(out=WB2[:], in_=woI.rearrange("(c p) f -> p c f", p=128))
         .then_inc(DWB2.h, 16))
    DG2 = P.sem("dDG2")
    n_g2 = DG2.reserve(16)
    em.g(lambda g_: g_.dma_start(out=grow2[:], in_=g2.rearrange("(a d) -> a d", a=1))
         .then_inc(DG2.h, 16))
    em.t(W(DG2.h, n_g2))
    for ft in range(4):
        em.t(W(CV.h, CV.n))
        nt = em.to(lambda t, ft=ft: t.matmul(psF[:, 4, :], conesrb[0:1, :],
                                             grow2[0:1, 512 * ft:512 * (ft + 1)],
                                             start=True, stop=True))
        em.v(W(CT.h, nt))
        em.vo(lambda v, ft=ft: v.tensor_copy(out=g2b[:, 512 * ft:512 * (ft + 1)],
                                             in_=psF[:, 4, :]))
    ATF = sb4("ATF", [128, 16, M], BF16)
    DAT = P.sem("dDAT")
    for h in range(16):
        DAT.reserve(16)
        em.s(lambda s, h=h: s.dma_start(out=ATF[:, h, :],
                                        in_=ATT_d[128 * h:128 * (h + 1), :])
             .then_inc(DAT.h, 16))
    n_at = DAT.n
    em.t(W(DAT.h, n_at))
    em.t(W(DWB2.h, n_w))
    DXG2 = [P.sem("dDXG2%d" % i) for i in range(2)]
    DZ2 = [P.sem("dDZ2%d" % i) for i in range(2)]
    DH2 = [P.sem("dDH2%d" % i) for i in range(2)]
    xg2_cons = {}
    z2_cons = {}
    h2_cons = {}
    for mc in range(8):
        sl = mc % 2
        if sl in xg2_cons:
            em.s(W(CV.h, xg2_cons[sl]))
        nl = DXG2[sl].reserve(16)
        em.s(lambda s, mc=mc, sl=sl: s.dma_start(
            out=xgs2[:, sl, :], in_=XG_d[128 * mc:128 * (mc + 1), :])
            .then_inc(DXG2[sl].h, 16))
        if mc > 0:
            em.t(W(CV.h, CV.n))
        for ft in range(4):
            for hh in range(15):
                em.t(lambda t, mc=mc, ft=ft, hh=hh: t.matmul(
                    psF[:, ft, :], ATF[:, hh, 128 * mc:128 * (mc + 1)],
                    WB2[:, hh, 512 * ft:512 * (ft + 1)], start=(hh == 0), stop=False))
            nt = em.to(lambda t, mc=mc, ft=ft: t.matmul(
                psF[:, ft, :], ATF[:, 15, 128 * mc:128 * (mc + 1)],
                WB2[:, 15, 512 * ft:512 * (ft + 1)], start=False, stop=True))
        em.v(W(CT.h, nt))
        em.v(W(DXG2[sl].h, nl))
        if sl in z2_cons:
            em.v(W(*z2_cons[sl]))
        nvz = em.vo(lambda v, sl=sl: v.tensor_tensor(
            out=z2c[:, sl, :], in0=psF[:, 0:4, :].rearrange("p a b -> p (a b)"),
            in1=xgs2[:, sl, :], op=ALU.add))
        xg2_cons[sl] = nvz
        em.s(W(CV.h, nvz))
        nstz = DZ2[sl].reserve(16)
        em.s(lambda s, mc=mc, sl=sl: s.dma_start(
            out=Z2_d[128 * mc:128 * (mc + 1), :], in_=z2c[:, sl, :])
            .then_inc(DZ2[sl].h, 16))
        z2_cons[sl] = (DZ2[sl].h, nstz)
        em.a(W(CV.h, nvz))
        nsq = em.ao(lambda a, sl=sl: a.activation(out=sqs2[:, :], in_=z2c[:, sl, :],
                                                  func=AF.Square, bias=0.0, scale=1.0))
        em.v(W(CS.h, nsq))
        em.vo(lambda v, sl=sl: v.tensor_reduce(out=inv2[:, sl:sl + 1], in_=sqs2[:, :],
                                               axis=AX.X, op=ALU.add))
        nvv = em.vo(lambda v, sl=sl: v.tensor_scalar(out=inv2[:, sl:sl + 1],
                                                     in0=inv2[:, sl:sl + 1],
                                                     scalar1=1.0 / D, scalar2=None,
                                                     op0=ALU.mult))
        em.a(W(CV.h, nvv))
        na = em.ao(lambda a, sl=sl: a.activation(out=inv2[:, sl:sl + 1],
                                                 in_=inv2[:, sl:sl + 1], func=AF.Sqrt,
                                                 bias=cscal[:, 1:2], scale=1.0))
        em.v(W(CS.h, na))
        em.vo(lambda v, sl=sl: v.reciprocal(out=inv2[:, sl:sl + 1],
                                            in_=inv2[:, sl:sl + 1]))
        if sl in h2_cons:
            em.v(W(*h2_cons[sl]))
        nvh = em.vo(lambda v, sl=sl: v.scalar_tensor_tensor(
            out=h2c[:, sl, :], in0=z2c[:, sl, :], scalar=inv2[:, sl:sl + 1],
            in1=g2b[:, :], op0=ALU.mult, op1=ALU.mult))
        em.s(W(CV.h, nvh))
        nsth = DH2[sl].reserve(16)
        em.s(lambda s, mc=mc, sl=sl: s.dma_start(
            out=H2_d[128 * mc:128 * (mc + 1), :], in_=h2c[:, sl, :])
            .then_inc(DH2[sl].h, 16))
        h2_cons[sl] = (DH2[sl].h, nsth)
    P.run_block("l1f_wo", em)
    ctx4.close()

    if L1F_PARTS[0] < 5:
        return None
    f4 = P.fence()
    ctx5 = ExitStack()
    sb5 = lambda n, shp, dt=F32: ctx5.enter_context(nc.sbuf_tensor("l1f5_" + n, list(shp), dt))
    H2T = sb5("H2T", [128, 16, M], BF16)
    W1S = sb5("W1S", [128, 2, 16, 1024], BF16)
    W3S = sb5("W3S", [128, 2, 16, 1024], BF16)
    silb = sb5("silb", [128, 2, 512])
    hzc = sb5("hzc", [128, 2, M], BF16)

    em = Em(P)
    wf = P.wait_fence_fns(f4)
    for k in "gstva":
        em.L[k].append(wf)
    DH2T = P.sem("dDH2T")
    for dc in range(16):
        DH2T.reserve(16)
        em.s(lambda s, dc=dc: s.dma_start(out=H2T[:, dc, :],
                                          in_=H2_d[0:M, 128 * dc:128 * (dc + 1)],
                                          transpose=True).then_inc(DH2T.h, 16))
    n_h2t = DH2T.n
    em.t(W(DH2T.h, n_h2t))
    DW13 = [P.sem("dDW13%d" % i) for i in range(2)]
    DHZ0 = P.sem("dDHZ0"); DHZ1 = P.sem("dDHZ1")
    sil_cons = {}
    hz_cons = {}
    w13_n = {}
    w13_cons = {}

    def load_w13(su):
        sl = su % 2
        for ws, wsrc in ((W1S, w1I), (W3S, w3I)):
            w13_n[sl] = DW13[sl].reserve(16)
            em.g(lambda g_, su=su, sl=sl, ws=ws, wsrc=wsrc: g_.dma_start(
                out=ws[:, sl, :, :], in_=wsrc.rearrange("(c p) e -> p c e", p=128)
                [:, :, 1024 * su:1024 * (su + 1)]).then_inc(DW13[sl].h, 16))

    load_w13(0)
    for su in range(8):
        sl = su % 2
        if su + 1 < 8:
            nsl = (su + 1) % 2
            if nsl in w13_cons:
                em.g(W(CT.h, w13_cons[nsl]))
            load_w13(su + 1)
        em.t(W(DW13[sl].h, w13_n[sl]))
        for ei in range(8):
            ec = su * 8 + ei
            for mt in range(2):
                kk = ei * 2 + mt
                ru = kk % 2
                rg = 2 + kk % 2
                if (su, ei, mt) != (0, 0, 0):
                    em.t(W(CS.h, CS.n))
                for dc in range(15):
                    em.t(lambda t, ru=ru, dc=dc, ei=ei, mt=mt, sl=sl: t.matmul(
                        psF[:, ru, :], W1S[:, sl, dc, 128 * ei:128 * (ei + 1)],
                        H2T[:, dc, 512 * mt:512 * (mt + 1)], start=(dc == 0), stop=False))
                ntu = em.to(lambda t, ru=ru, ei=ei, mt=mt, sl=sl: t.matmul(
                    psF[:, ru, :], W1S[:, sl, 15, 128 * ei:128 * (ei + 1)],
                    H2T[:, 15, 512 * mt:512 * (mt + 1)], start=False, stop=True))
                if (su, ei, mt) != (0, 0, 0):
                    em.t(W(CV.h, CV.n))
                for dc in range(15):
                    em.t(lambda t, rg=rg, dc=dc, ei=ei, mt=mt, sl=sl: t.matmul(
                        psF[:, rg, :], W3S[:, sl, dc, 128 * ei:128 * (ei + 1)],
                        H2T[:, dc, 512 * mt:512 * (mt + 1)], start=(dc == 0), stop=False))
                ntg = em.to(lambda t, rg=rg, ei=ei, mt=mt, sl=sl: t.matmul(
                    psF[:, rg, :], W3S[:, sl, 15, 128 * ei:128 * (ei + 1)],
                    H2T[:, 15, 512 * mt:512 * (mt + 1)], start=False, stop=True))
                ssl = kk % 2
                if ssl in sil_cons:
                    em.a(W(CV.h, sil_cons[ssl]))
                em.a(W(CT.h, ntu))
                na = em.ao(lambda a, ru=ru, ssl=ssl: a.activation(
                    out=silb[:, ssl, :], in_=psF[:, ru, :], func=AF.Silu,
                    bias=0.0, scale=1.0))
                em.v(W(CS.h, na))
                em.v(W(CT.h, ntg))
                if mt == 0 and (ec % 2) in hz_cons:
                    em.v(W(*hz_cons[ec % 2]))
                nv = em.vo(lambda v, rg=rg, ssl=ssl, ec=ec, mt=mt: v.tensor_tensor(
                    out=hzc[:, ec % 2, 512 * mt:512 * (mt + 1)],
                    in0=psF[:, rg, :], in1=silb[:, ssl, :], op=ALU.mult))
                sil_cons[ssl] = nv
            DHZp = DHZ0 if ec % 2 == 0 else DHZ1
            em.s(W(CV.h, CV.n))
            nst = DHZp.reserve(16)
            em.s(lambda s, ec=ec, DHZp=DHZp: s.dma_start(
                out=HZ_d[128 * ec:128 * (ec + 1), :], in_=hzc[:, ec % 2, :])
                .then_inc(DHZp.h, 16))
            hz_cons[ec % 2] = (DHZp.h, nst)
        w13_cons[sl] = CT.n
    P.run_block("l1f_ffn1", em)
    ctx5.close()

    f5 = P.fence()
    ctx6 = ExitStack()
    sb6 = lambda n, shp, dt=F32: ctx6.enter_context(nc.sbuf_tensor("l1f6_" + n, list(shp), dt))
    TD2 = sb6("TD2", [128, 2, 64, 512], BF16)
    hzsl = sb6("hzsl", [128, 2, 8, 512], BF16)
    z2ts = sb6("z2ts", [128, 2, 512], BF16)
    z3c = sb6("z3c", [128, 2, 512], BF16)

    em = Em(P)
    wf = P.wait_fence_fns(f5)
    for k in "gstva":
        em.L[k].append(wf)
    DTD = [P.sem("dDTD%d" % i) for i in range(2)]
    DHS = [P.sem("dDHS%d" % i) for i in range(2)]
    DZT = [P.sem("dDZT%d" % i) for i in range(2)]
    DZ3 = [P.sem("dDZ3%d" % i) for i in range(2)]
    hs_cons = {}
    zt_cons = {}
    z3_cons = {}
    td_n = {}
    td_cons = {}

    def load_td2(dgrp):
        sl = dgrp % 2
        td_n[sl] = DTD[sl].reserve(16)
        em.g(lambda g_, dgrp=dgrp, sl=sl: g_.dma_start(
            out=TD2[:, sl, :, :], in_=w2I.rearrange("(c p) d -> p c d", p=128)
            [:, :, 512 * dgrp:512 * (dgrp + 1)]).then_inc(DTD[sl].h, 16))

    load_td2(0)
    for dgrp in range(4):
        sl = dgrp % 2
        if dgrp + 1 < 4:
            nsl = (dgrp + 1) % 2
            if nsl in td_cons:
                em.g(W(CT.h, td_cons[nsl]))
            load_td2(dgrp + 1)
        em.t(W(DTD[sl].h, td_n[sl]))
        for mt in range(2):
            em.t(W(CV.h, CV.n))
            for ecg in range(8):
                hslot = ecg % 2
                if hslot in hs_cons:
                    em.s(W(CT.h, hs_cons[hslot]))
                nl = DHS[hslot].reserve(16)
                em.s(lambda s, ecg=ecg, mt=mt, hslot=hslot: s.dma_start(
                    out=hzsl[:, hslot, :, :],
                    in_=HZ_d.rearrange("(g p) s -> p g s", p=128)
                    [:, 8 * ecg:8 * (ecg + 1), 512 * mt:512 * (mt + 1)])
                    .then_inc(DHS[hslot].h, 16))
                em.t(W(DHS[hslot].h, nl))
                for ei in range(8):
                    ec = ecg * 8 + ei
                    for dc4 in range(4):
                        last = (ec == 63)
                        if (dc4 == 3 and ei == 7) or last:
                            nlast = em.to(lambda t, dc4=dc4, ec=ec, hslot=hslot, ei=ei,
                                          last=last, sl=sl: t.matmul(
                                psF[:, dc4, :], TD2[:, sl, ec, 128 * dc4:128 * (dc4 + 1)],
                                hzsl[:, hslot, ei, :], start=(ec == 0), stop=last))
                            if dc4 == 3:
                                hs_cons[hslot] = nlast
                        else:
                            em.t(lambda t, dc4=dc4, ec=ec, hslot=hslot, ei=ei,
                                 last=last, sl=sl: t.matmul(
                                psF[:, dc4, :], TD2[:, sl, ec, 128 * dc4:128 * (dc4 + 1)],
                                hzsl[:, hslot, ei, :], start=(ec == 0), stop=last))
            nt_done = CT.n
            for dc4 in range(4):
                dc = dgrp * 4 + dc4
                zsl = dc4 % 2
                if zsl in zt_cons:
                    em.s(W(CV.h, zt_cons[zsl]))
                nlz = DZT[zsl].reserve(16)
                em.s(lambda s, dc=dc, mt=mt, zsl=zsl: s.dma_start(
                    out=z2ts[:, zsl, :],
                    in_=Z2_d[512 * mt:512 * (mt + 1), 128 * dc:128 * (dc + 1)],
                    transpose=True).then_inc(DZT[zsl].h, 16))
                em.v(W(CT.h, nt_done))
                em.v(W(DZT[zsl].h, nlz))
                if zsl in z3_cons:
                    em.v(W(*z3_cons[zsl]))
                nvz = em.vo(lambda v, dc4=dc4, zsl=zsl: v.tensor_tensor(
                    out=z3c[:, zsl, :], in0=psF[:, dc4, :], in1=z2ts[:, zsl, :],
                    op=ALU.add))
                zt_cons[zsl] = nvz
                em.s(W(CV.h, nvz))
                nstz = DZ3[zsl].reserve(16)
                em.s(lambda s, dc=dc, mt=mt, zsl=zsl: s.dma_start(
                    out=Z3T_d[128 * dc:128 * (dc + 1), 512 * mt:512 * (mt + 1)],
                    in_=z3c[:, zsl, :]).then_inc(DZ3[zsl].h, 16))
                z3_cons[zsl] = (DZ3[zsl].h, nstz)
        td_cons[sl] = CT.n
    P.run_block("l1f_ffn2", em)
    ctx6.close()

    if L1F_PARTS[0] < 6:
        return None
    f6 = P.fence()
    ctx7 = ExitStack()
    sb7 = lambda n, shp, dt=F32: ctx7.enter_context(nc.sbuf_tensor("l1f7_" + n, list(shp), dt))
    Pbuf = sb7("Pbuf", [128, 8, S], BF16)
    Z3S = sb7("Z3S", [128, 8, D], BF16)
    XT2 = sb7("XT2", [128, 16, D], BF16)
    bcb2 = sb7("bcb2", [128, S])
    lrow2 = sb7("lrow2", [1, S])
    zit = sb7("zit", [128, 4, D], BF16)
    zsq4 = sb7("zsq4", [128, 2, 512], BF16)
    sumB4 = sb7("sumB4", [128, 16])
    sqB4 = sb7("sqB4", [128, 16])
    red24 = sb7("red24", [128, 2])
    c14 = sb7("c14", [128, 16])
    c04 = sb7("c04", [128, 16])
    scr164 = sb7("scr164", [128, 16])
    outc = sb7("outc", [128, D])

    em = Em(P)
    wf = P.wait_fence_fns(f6)
    for k in "gstva":
        em.L[k].append(wf)
    DRW3 = P.sem("dDRW3")
    n_r4 = DRW3.reserve(16)
    em.s(lambda s: s.dma_start(out=lrow2[:],
                               in_=row_d.rearrange("a b c -> a (b c)")[1:2, :])
         .then_inc(DRW3.h, 16))
    em.t(W(DRW3.h, n_r4))
    for ft in range(4):
        em.t(W(CV.h, CV.n))
        nt = em.to(lambda t, ft=ft: t.matmul(psF[:, 4, :], conesr[0:1, :],
                                             lrow2[0:1, 512 * ft:512 * (ft + 1)],
                                             start=True, stop=True))
        em.v(W(CT.h, nt))
        em.vo(lambda v, ft=ft: v.tensor_copy(out=bcb2[:, 512 * ft:512 * (ft + 1)],
                                             in_=psF[:, 4, :]))
    for mc in range(8):
        em.vo(lambda v, mc=mc: v.tensor_scalar(
            out=Pbuf[:, mc, :], in0=bcb2[:, :], scalar1=cmcols[:, mc:mc + 1],
            scalar2=rw[:, mc:mc + 1], op0=ALU.is_equal, op1=ALU.mult))
    nv_p = CV.n
    if L1F_PARTS[0] == 60:
        P.run_block("l1f_scatter", em)
        ctx7.close()
        return None
    DZ3S = P.sem("dDZ3S")
    for mc in range(8):
        DZ3S.reserve(16)
        em.s(lambda s, mc=mc: s.dma_start(out=Z3S[:, mc, :],
                                          in_=Z3T_d[0:D, 128 * mc:128 * (mc + 1)],
                                          transpose=True).then_inc(DZ3S.h, 16))
    n_z3s = DZ3S.n
    DXT2 = P.sem("dDXT2")
    for c in range(16):
        DXT2.reserve(16)
        em.s(lambda s, c=c: s.dma_start(out=XT2[:, c, :],
                                        in_=X3[128 * c:128 * (c + 1), :])
             .then_inc(DXT2.h, 16))
    n_xt2 = DXT2.n
    em.t(W(DZ3S.h, n_z3s))
    em.t(W(CV.h, nv_p))
    em.v(W(DXT2.h, n_xt2))
    if L1F_PARTS[0] == 61:
        P.run_block("l1f_scatter", em)
        ctx7.close()
        return None
    st_base = ST.n
    o_tt = {}
    for icg in range(([1] if L1F_PARTS[0] == 62 else [4])[0]):
        for k in range(16):
            ic_l = k // 4
            ft = k % 4
            ic = icg * 4 + ic_l
            r = k % 2
            if k >= 2:
                em.t(W(CV.h, o_tt[k - 2]))
            elif icg > 0:
                em.t(W(CV.h, CV.n))
            for mc in range(7):
                em.t(lambda t, r=r, ic=ic, ft=ft, mc=mc: t.matmul(
                    psF[:, 2 + r, :], Pbuf[:, mc, 128 * ic:128 * (ic + 1)],
                    Z3S[:, mc, 512 * ft:512 * (ft + 1)], start=(mc == 0), stop=False))
            nto = em.to(lambda t, r=r, ic=ic, ft=ft: t.matmul(
                psF[:, 2 + r, :], Pbuf[:, 7, 128 * ic:128 * (ic + 1)],
                Z3S[:, 7, 512 * ft:512 * (ft + 1)], start=False, stop=True))
            em.v(W(CT.h, nto))
            o_tt[k] = em.vo(lambda v, r=r, ic=ic, ic_l=ic_l, ft=ft: v.tensor_tensor(
                out=zit[:, ic_l, 512 * ft:512 * (ft + 1)], in0=psF[:, 2 + r, :],
                in1=XT2[:, ic, 512 * ft:512 * (ft + 1)], op=ALU.add))
            em.a(W(CV.h, o_tt[k]))
            na = em.ao(lambda a, k=k, ic_l=ic_l, ft=ft: a.activation(
                out=zsq4[:, k % 2, :], in_=zit[:, ic_l, 512 * ft:512 * (ft + 1)],
                func=AF.Square, bias=0.0, scale=1.0))
            em.vo(lambda v, ic_l=ic_l, ft=ft: v.tensor_reduce(
                out=red24[:, 0:1], in_=zit[:, ic_l, 512 * ft:512 * (ft + 1)],
                axis=AX.X, op=ALU.add))
            em.v(W(CS.h, na))
            em.vo(lambda v, k=k: v.tensor_reduce(
                out=red24[:, 1:2], in_=zsq4[:, k % 2, :], axis=AX.X, op=ALU.add))
            if ft == 0:
                em.vo(lambda v, ic=ic: v.tensor_copy(out=sumB4[:, ic:ic + 1],
                                                     in_=red24[:, 0:1]))
                em.vo(lambda v, ic=ic: v.tensor_copy(out=sqB4[:, ic:ic + 1],
                                                     in_=red24[:, 1:2]))
            else:
                em.vo(lambda v, ic=ic: v.tensor_tensor(
                    out=sumB4[:, ic:ic + 1], in0=sumB4[:, ic:ic + 1],
                    in1=red24[:, 0:1], op=ALU.add))
                em.vo(lambda v, ic=ic: v.tensor_tensor(
                    out=sqB4[:, ic:ic + 1], in0=sqB4[:, ic:ic + 1],
                    in1=red24[:, 1:2], op=ALU.add))
        sl = slice(icg * 4, icg * 4 + 4)
        em.vo(lambda v, sl=sl: v.tensor_scalar(out=sumB4[:, sl], in0=sumB4[:, sl],
                                               scalar1=1.0 / D, scalar2=None,
                                               op0=ALU.mult))
        em.vo(lambda v, sl=sl: v.tensor_scalar(out=sqB4[:, sl], in0=sqB4[:, sl],
                                               scalar1=1.0 / D, scalar2=None,
                                               op0=ALU.mult))
        em.vo(lambda v, sl=sl: v.tensor_tensor(out=scr164[:, sl], in0=sumB4[:, sl],
                                               in1=sumB4[:, sl], op=ALU.mult))
        nvs = em.vo(lambda v, sl=sl: v.tensor_tensor(out=sqB4[:, sl], in0=sqB4[:, sl],
                                                     in1=scr164[:, sl],
                                                     op=ALU.subtract))
        em.a(W(CV.h, nvs))
        na_std = em.ao(lambda a, sl=sl: a.activation(out=sqB4[:, sl], in_=sqB4[:, sl],
                                                     func=AF.Sqrt, bias=cscal[:, 0:1],
                                                     scale=1.0))
        em.v(W(CS.h, na_std))
        em.vo(lambda v, sl=sl: v.reciprocal(out=c14[:, sl], in_=sqB4[:, sl]))
        em.vo(lambda v, sl=sl: v.scalar_tensor_tensor(out=c04[:, sl], in0=sumB4[:, sl],
                                                      scalar=-1.0, in1=c14[:, sl],
                                                      op0=ALU.mult, op1=ALU.mult))
        for ic_l in range(4):
            ic = icg * 4 + ic_l
            if ic >= 1:
                em.v(W(ST.h, st_base + 16 * ic))
            nv = em.vo(lambda v, ic=ic, ic_l=ic_l: v.tensor_scalar(
                out=outc[:, :], in0=zit[:, ic_l, :], scalar1=c14[:, ic:ic + 1],
                scalar2=c04[:, ic:ic + 1], op0=ALU.mult, op1=ALU.add))
            em.s(W(CV.h, nv))
            ST.reserve(16)
            em.s(lambda s, ic=ic: s.dma_start(out=OUT[128 * ic:128 * (ic + 1), :],
                                              in_=outc[:, :]).then_inc(ST.h, 16))
    em.s(W(ST.h, ST.n))
    P.run_block("l1f_scatter", em)
    ctx7.close()
    return None



def _build_program(phase_list, last_flags):
    P = Prog("+".join(n for n, _ in phase_list))
    for (name, emitfn), last in zip(phase_list, last_flags):
        em = emitfn(P, last=last)
        if em is not None:
            P.run_block(name, em)
    fin = P.fence()
    em = Em(P)
    em.s(P.wait_fence_fns(fin))
    P.run_block("drain", em)
    return P


def _collect_inputs(P, inputs, b, carry_b):
    m = {}
    for alloc in P.nc.m.functions[0].allocations:
        if not isinstance(alloc, mybir.MemoryLocationSet) or alloc.kind != "ExternalInput":
            continue
        name = alloc.memorylocations[0].name
        if name == "partition_id":
            continue
        if name in CONSTS:
            m[name] = CONSTS[name]
        elif name == "x":
            m[name] = np.ascontiguousarray(
                np.asarray(inputs["L0_attn_hidden_states"], dtype=np.float32)[b])
        elif name in carry_b:
            m[name] = carry_b[name]
        elif name in inputs:
            m[name] = np.ascontiguousarray(np.asarray(inputs[name], dtype=np.float32))
        else:
            raise KeyError(f"no source for input {name}")
    return m


def _run_programs(programs, inputs):
    carry = [dict() for _ in range(8)]
    for P in programs:
        in_maps = [_collect_inputs(P, inputs, b, carry[b]) for b in range(8)]
        res = run_bass_kernel_spmd(P.nc, in_maps, core_ids=list(range(8)))
        for b in range(8):
            carry[b].update(res.results[b])
    return carry


FUSED = True


def kernel(**inputs):
    if FUSED:
        progs = [_build_program(
            [("l0a", emit_l0a), ("l0f", emit_l0f), ("l1a", emit_l1a),
             ("l1f", emit_l1f)], [False, False, False, True])]
    else:
        progs = [
            _build_program([("l0a", emit_l0a)], [True]),
            _build_program([("l0f", emit_l0f)], [True]),
            _build_program([("l1a", emit_l1a)], [True]),
            _build_program([("l1f", emit_l1f)], [True]),
        ]
    carry = _run_programs(progs, inputs)
    return np.stack([carry[b]["OUT"] for b in range(8)], axis=0)


def dev_run_l0a(inputs):
    P = _build_program([("l0a", emit_l0a)], [True])
    carry = _run_programs([P], inputs)
    return [c["X1_tok"] for c in carry]
